# version 10
# baseline (speedup 1.0000x reference)
import contextlib
import numpy as np
import ml_dtypes

import concourse.bass as bass
import concourse.mybir as mybir
from concourse.bass_utils import run_bass_kernel_spmd

F32 = mybir.dt.float32
BF16 = mybir.dt.bfloat16
U8 = mybir.dt.uint8
U32 = mybir.dt.uint32
AF = mybir.ActivationFunctionType
ALU = mybir.AluOpType
AX = mybir.AxisListType

NCORES = 8
D = 1024
TPC = 2048
NH = 16
DH = 64
NEG = -1.0e30
EPS_LN = 1e-5

D_INNER = 2048
D_STATE = 128
NHEADS = 32
CONV_DIM = 2304
D_IN_PROJ = 4384
POOL_SIZE = 50
SCORER_HID = 256
SUMM = 64


class Ctx:

    def __init__(self, nc, block, st):
        self.nc = nc
        self.block = block
        self.st = st
        self.sem = {}
        self.cnt = {}
        self._cc = {}
        self.cpool = None
        self._waited = {e: {} for e in
                        ("tensor", "vector", "scalar", "gpsimd", "sync")}
        self._cur_eng = None

    def barrier(self):
        names = [n for n in ("t", "v", "s", "g", "dma", "gd", "cc", "pre")
                 if n in self.sem]
        for eng in ("tensor", "vector", "scalar", "gpsimd", "sync"):
            def body(e, names=names, eng=eng):
                self._cur_eng = eng
                self.wait(e, *names)
            getattr(self.block, eng)(body)

    def constcell(self, value):
        key = float(value)
        if key not in self._cc:
            col = len(self._cc)
            self._cc[key] = col
            cell = self.cpool[0:1, col:col + 1]
            def _set(v, cell=cell, value=value):
                return v.memset(cell, float(value))
            self.step("vector", (), _set, "v")
        return self.cpool[0:1, self._cc[key]:self._cc[key] + 1]

    def mksem(self, *names):
        for n in names:
            self.sem[n] = self.st.enter_context(self.nc.semaphore(n))
            self.cnt[n] = 0

    def inc(self, inst, name, k=1):
        inst.then_inc(self.sem[name], k)
        self.cnt[name] += k
        return inst

    def wait(self, eng, *names):
        seen = []
        for n in names:
            seen.append(n)
            if n == "dma" and "gd" in self.sem and "gd" not in names:
                seen.append("gd")
        wl = self._waited.get(self._cur_eng, {})
        for n in seen:
            v = self.cnt[n]
            if v > 0 and wl.get(n, -1) < v:
                eng.wait_ge(self.sem[n], v)
                wl[n] = v

    _OWN = {"tensor": "t", "vector": "v", "scalar": "s", "gpsimd": "g"}

    def snap(self, name):
        return self.cnt[name]

    def step(self, engname, deps, fn, inc=None, inck=1, waits=()):
        own = self._OWN.get(engname)
        if own is not None and own not in deps and engname != "tensor":
            deps = tuple(deps) + (own,)
        def body(eng):
            self._cur_eng = engname
            wl = self._waited.get(engname, {})
            for n, val in waits:
                if val > 0 and wl.get(n, -1) < val:
                    eng.wait_ge(self.sem[n], val)
                    wl[n] = val
            self.wait(eng, *deps)
            last = fn(eng)
            if inc is not None:
                assert last is not None, f"step for {inc} returned no inst"
                self.inc(last, inc, inck)
        getattr(self.block, engname)(body)

    def tstep(self, deps, fn, waits=()):
        self.step("tensor", deps, fn, "t", waits=waits)

    def vstep(self, deps, fn, waits=()):
        self.step("vector", deps, fn, "v", waits=waits)

    def sstep(self, deps, fn, waits=()):
        self.step("scalar", deps, fn, "s", waits=waits)

    def gstep(self, deps, fn, inc="g", inck=1):
        self.step("gpsimd", deps, fn, inc, inck)

    def _dma_step(self, engname, deps, fn):
        own = self._OWN.get(engname)
        if own is not None and own not in deps:
            deps = tuple(deps) + (own,)
        sem = "gd" if engname == "gpsimd" else "dma"
        def body(eng):
            self._cur_eng = engname
            self.wait(eng, *deps)
            def dma(**kw):
                inst = eng.dma_start(**kw)
                inst.then_inc(self.sem[sem], 16)
                self.cnt[sem] += 16
                return inst
            fn(eng, dma)
        getattr(self.block, engname)(body)

    def dstep(self, deps, fn, n=None):
        self._dma_step("sync", deps, fn)

    def gdma(self, deps, fn, n=None):
        self._dma_step("gpsimd", deps, fn)


ALLC = ("t", "v", "s", "g", "dma", "cc")


def emit_layernorm(ctx, P, Xc, out_c, t0, t1, ln_stats=None):
    nc = ctx.nc
    T = ctx.T
    sq = T["sq"]
    strow = T["strow"]
    onecol = T["onecol"]
    onesf = T["onesf"]
    for tt0 in range(t0, t1, 512):
        w = min(512, t1 - tt0)
        def _sumx(t, tt0=tt0, w=w):
            last = None
            for dc in range(8):
                last = t.matmul(P["row"][0:1, 0:w], onecol[:],
                                Xc[dc][:, tt0:tt0 + w],
                                start=(dc == 0), stop=(dc == 7))
            return last
        ctx.tstep(("v", "s", "dma"), _sumx)
        for dc in range(8):
            def _sq(s, dc=dc, tt0=tt0, w=w):
                return s.activation(sq[:, 0:w], Xc[dc][:, tt0:tt0 + w], AF.Square)
            ctx.sstep(("t", "v", "dma"), _sq)
            def _sumsq(t, dc=dc, w=w):
                return t.matmul(P["row"][0:1, 512:512 + w], onecol[:], sq[:, 0:w],
                                start=(dc == 0), stop=(dc == 7))
            ctx.tstep(("s",), _sumsq)
        def _st1a(v, w=w):
            v.tensor_scalar(strow[:, 0:w], P["row"][0:1, 0:w], 1.0 / D, None, ALU.mult)
            return v.tensor_scalar(strow[:, 512:512 + w], P["row"][0:1, 512:512 + w],
                                   1.0 / D, None, ALU.mult)
        ctx.vstep(("t",), _st1a)
        def _st1b(v, w=w):
            return v.tensor_mul(strow[:, 1024:1024 + w], strow[:, 0:w],
                                strow[:, 0:w])
        ctx.vstep((), _st1b)
        def _st1c(v, w=w):
            return v.tensor_sub(strow[:, 1024:1024 + w], strow[:, 512:512 + w],
                                strow[:, 1024:1024 + w])
        ctx.vstep((), _st1c)
        epsc = ctx.constcell(EPS_LN)
        def _st2(s, w=w, epsc=epsc):
            return s.activation(strow[:, 1024:1024 + w], strow[:, 1024:1024 + w],
                                AF.Sqrt, bias=epsc)
        ctx.sstep(("v",), _st2)
        def _st3a(v, w=w):
            return v.reciprocal(strow[:, 1536:1536 + w], strow[:, 1024:1024 + w])
        ctx.vstep(("s",), _st3a)
        def _st3b(v, w=w):
            return v.tensor_mul(strow[:, 512:512 + w], strow[:, 0:w],
                                strow[:, 1536:1536 + w])
        ctx.vstep((), _st3b)
        def _bc(t, w=w):
            t.matmul(P["b0"][:, 0:w], onesf[0:1, 0:128], strow[0:1, 1536:1536 + w],
                     start=True, stop=True)
            return t.matmul(P["b1"][:, 0:w], onesf[0:1, 0:128],
                            strow[0:1, 512:512 + w], start=True, stop=True)
        ctx.tstep(("v",), _bc)
        for dc in range(8):
            def _ap1(v, dc=dc, tt0=tt0, w=w):
                return v.scalar_tensor_tensor(sq[:, 0:w], Xc[dc][:, tt0:tt0 + w],
                                              1.0, P["b0"][:, 0:w],
                                              ALU.mult, ALU.mult)
            ctx.vstep(("t",), _ap1)
            def _ap2(v, dc=dc, tt0=tt0, w=w):
                return v.tensor_sub(out_c[dc][:, tt0 - t0:tt0 - t0 + w],
                                    sq[:, 0:w], P["b1"][:, 0:w])
            ctx.vstep((), _ap2)
        if ln_stats is not None:
            def _keep(v, tt0=tt0, w=w):
                v.tensor_copy(ln_stats[:, tt0:tt0 + w], strow[:, 1536:1536 + w])
                return v.tensor_copy(ln_stats[:, TPC + tt0:TPC + tt0 + w],
                                     strow[:, 512:512 + w])
            ctx.vstep((), _keep)


def emit_store_dbg(ctx, chunks, out_fm):
    Xc = ctx.T["X"]
    def _cvt(v):
        last = None
        for c in range(8):
            last = v.tensor_copy(Xc[c][:], chunks[c][:])
        return last
    ctx.vstep(ALLC, _cvt)
    def _st(sync, dma):
        for c in range(8):
            dma(out=out_fm[c * 128:(c + 1) * 128, :], in_=Xc[c][:])
    ctx.dstep(("v",), _st)
    ctx.step("sync", ("dma",), lambda sync: None)


def emit_store_out(ctx, Xc, out_fm):
    def _st(sync, dma):
        for c in range(8):
            dma(out=out_fm[c * 128:(c + 1) * 128, :], in_=Xc[c][:])
    ctx.dstep(ALLC, _st)
    ctx.step("sync", ("dma",), lambda sync: None)



def emit_ffn(ctx, nc, X, out_fm, wdram, scl_cells, P, bufs):
    w0h_d, w1_d, w2_d = wdram
    lnf, a1, a2 = bufs["lnf"], bufs["a1"], bufs["a2"]
    wff, w1b = bufs["wff"], bufs["w1b"]
    scl = scl_cells
    for tt in range(4):
        t0 = tt * 512
        emit_layernorm(ctx, P, X, lnf, t0, t0 + 512)
        snaps = {}
        def load0(nb, pp):
            def _lw(sync, dma, nb=nb, pp=pp):
                for dc in range(8):
                    dma(out=wff[pp][dc][:],
                        in_=w0h_d[dc * 128:(dc + 1) * 128,
                                  nb * 512:(nb + 1) * 512])
            ctx.dstep(("pre", "t"), _lw)
            snaps[("l0", nb)] = ctx.snap("dma")
        load0(0, 0)
        for nb in range(8):
            pp = nb % 2
            if nb + 1 < 8:
                load0(nb + 1, (nb + 1) % 2)
            for ncq in range(4):
                nch = nb * 4 + ncq
                ps = P["m0"] if ncq % 2 == 0 else P["m1"]
                def _mm(t, ncq=ncq, ps=ps, pp=pp):
                    last = None
                    for dc in range(8):
                        last = t.matmul(ps[:, 0:512],
                                        wff[pp][dc][:, ncq * 128:(ncq + 1) * 128],
                                        lnf[dc][:, 0:512],
                                        start=(dc == 0), stop=(dc == 7))
                    return last
                ctx.tstep(("v",), _mm, waits=[("dma", snaps[("l0", nb)])])
                def _ev(v, nch=nch, ps=ps):
                    return v.tensor_scalar(a1[nch][:, 0:512], ps[:, 0:512],
                                           scl[:, 0:1], 0.0, ALU.mult, ALU.max)
                ctx.vstep(("t",), _ev)
        def load1(nb, pp):
            def _lw1(sync, dma, nb=nb, pp=pp):
                for kc in range(32):
                    dma(out=w1b[pp][kc][:],
                        in_=w1_d[kc * 128:(kc + 1) * 128,
                                 nb * 256:(nb + 1) * 256])
            ctx.dstep(("pre", "t"), _lw1)
            snaps[("l1", nb)] = ctx.snap("dma")
        load1(0, 0)
        for nb in range(16):
            pp = nb % 2
            if nb + 1 < 16:
                load1(nb + 1, (nb + 1) % 2)
            for ncq in range(2):
                nch = nb * 2 + ncq
                ps = P["m0"] if ncq % 2 == 0 else P["m1"]
                def _mm1(t, ncq=ncq, ps=ps, pp=pp):
                    last = None
                    for kc in range(32):
                        last = t.matmul(ps[:, 0:512],
                                        w1b[pp][kc][:, ncq * 128:(ncq + 1) * 128],
                                        a1[kc][:, 0:512],
                                        start=(kc == 0), stop=(kc == 31))
                    return last
                ctx.tstep(("v",), _mm1, waits=[("dma", snaps[("l1", nb)])])
                def _ev1(v, nch=nch, ps=ps):
                    return v.tensor_scalar(a2[nch][:, 0:512], ps[:, 0:512],
                                           scl[:, 1:2], 0.0, ALU.mult, ALU.max)
                ctx.vstep(("t",), _ev1)
        def load2(nb, pp):
            def _lw2(sync, dma, nb=nb, pp=pp):
                for kc in range(32):
                    dma(out=w1b[pp][kc][:],
                        in_=w2_d[kc * 128:(kc + 1) * 128,
                                 nb * 256:(nb + 1) * 256])
            ctx.dstep(("pre", "t"), _lw2)
            snaps[("l2", nb)] = ctx.snap("dma")
        load2(0, 0)
        for nb in range(4):
            pp = nb % 2
            if nb + 1 < 4:
                load2(nb + 1, (nb + 1) % 2)
            for ncq in range(2):
                oc = nb * 2 + ncq
                ps = P["m0"] if ncq % 2 == 0 else P["m1"]
                def _mm2(t, ncq=ncq, ps=ps, pp=pp):
                    last = None
                    for kc in range(32):
                        last = t.matmul(ps[:, 0:512],
                                        w1b[pp][kc][:, ncq * 128:(ncq + 1) * 128],
                                        a2[kc][:, 0:512],
                                        start=(kc == 0), stop=(kc == 31))
                    return last
                ctx.tstep(("v",), _mm2, waits=[("dma", snaps[("l2", nb)])])
                def _ev2(v, oc=oc, ps=ps, t0=t0):
                    return v.scalar_tensor_tensor(
                        X[oc][:, t0:t0 + 512], ps[:, 0:512], scl[:, 2:3],
                        X[oc][:, t0:t0 + 512], ALU.mult, ALU.add)
                ctx.vstep(("t",), _ev2)


def build_program(stop_after=None):
    _EIN_NAMES.clear()
    nc = bass.Bass("TRN2", target_bir_lowering=False)

    def ein(name, shape, dtype=F32):
        _EIN_NAMES.append(name)
        return nc.dram_tensor(name, shape, dtype, kind="ExternalInput").ap()

    def idram(name, shape, dtype=F32):
        return nc.dram_tensor(name, shape, dtype, kind="Internal").ap()

    import os as _osd0
    _kd0 = set(_osd0.environ.get("KDECL", "all").split(","))
    def _want0(g):
        return "all" in _kd0 or g in _kd0
    x_fm = ein("x_fm", [D, TPC])
    if _want0("aw"):
        wq = ein("wq", [D, D])
        wk = ein("wk", [D, D])
        wv = ein("wv", [D, D])
        wo = ein("wo", [D, D])
    else:
        wq = wk = wv = wo = None
    if _want0("msk"):
        pos2 = ein("pos2", [128, TPC])
        dead = ein("dead", [128, 256])
        diagmasks = ein("diagmasks", [128, 4 * 512])
        identf = ein("identf", [128, 128])
        identb = ein("identb", [128, 128], BF16)
    else:
        pos2 = dead = diagmasks = identf = identb = None
    import os as _osd
    _kd = set(_osd.environ.get("KDECL", "all").split(","))
    def _want(g):
        return "all" in _kd or g in _kd
    if _want("u8"):
        k0u8 = ein("k0u8", [2 * D, 4096], U8)
        k1u8 = ein("k1u8", [4096, 4096], U8)
        k2u8 = ein("k2u8", [4096, D], U8)
    else:
        k0u8 = k1u8 = k2u8 = None
    if _want("sc"):
        s0in = ein("s0in", [1, 1])
        s1in = ein("s1in", [1, 1])
        s2in = ein("s2in", [1, 1])
    else:
        s0in = s1in = s2in = None

    if _want("ab"):
        ab_norm_w = ein("ab_norm_w", [128, 8])
        ab_inproj = ein("ab_inproj", [D, D_IN_PROJ])
        ab_conv_w = ein("ab_conv_w", [128, 18 * 4])
        ab_conv_b = ein("ab_conv_b", [128, 18])
        ab_dt_bias = ein("ab_dt_bias", [NHEADS, 1])
        ab_A_log = ein("ab_A_log", [NHEADS, 1])
        ab_D = ein("ab_D", [NHEADS, 1])
        ab_ssm_norm_w = ein("ab_ssm_norm_w", [128, 16])
        ab_outproj = ein("ab_outproj", [D_INNER, D])
        ab_scorer_w1 = ein("ab_scorer_w1", [D, SCORER_HID])
        ab_scorer_w2 = ein("ab_scorer_w2", [128, 2])
        ab_summ_w = ein("ab_summ_w", [D, SUMM])
        ab_q_w = ein("ab_q_w", [D, SUMM])
        ab_k_w = ein("ab_k_w", [SUMM, SUMM])
        ab_v_w = ein("ab_v_w", [SUMM, D])
        ab_gate_w = ein("ab_gate_w", [2 * D, D])
    else:
        ab_norm_w = ab_inproj = ab_conv_w = ab_conv_b = ab_dt_bias = None
        ab_A_log = ab_D = ab_ssm_norm_w = ab_outproj = ab_scorer_w1 = None
        ab_scorer_w2 = ab_summ_w = ab_q_w = ab_k_w = ab_v_w = ab_gate_w = None

    if _want("sched"):
        esel = ein("esel", [NHEADS, NHEADS * 128])
        psel = ein("psel", [NHEADS, 8 * 128])
        trineg = ein("trineg", [128, 2 * 256])
        halo_offs = ein("halo_offs", [128, 16], U32)
        halo_mask = ein("halo_mask", [128, 2])
        cc_mats = ein("cc_mats", [8, 16])
        cc_bias = ein("cc_bias", [8, 2])
        ref_iota = ein("ref_iota", [128, 128])
        g_iota = ein("g_iota", [128, 128])
        slot_iota = ein("slot_iota", [1, 64])
    else:
        esel = psel = trineg = halo_offs = halo_mask = cc_mats = cc_bias = None
        ref_iota = g_iota = slot_iota = None

    out_fm = nc.dram_tensor("out_fm", [D, TPC], F32, kind="ExternalOutput").ap()

    if _want0("wd"):
        w0h_d = idram("w0h_d", [D, 4096], BF16)
        w1_d = idram("w1_d", [4096, 4096], BF16)
        w2_d = idram("w2_d", [4096, D], BF16)
    else:
        w0h_d = w1_d = w2_d = None
    cc_k_in = idram("cc_k_in", [TPC, D], BF16)
    cc_k_out = idram("cc_k_out", [NCORES * TPC, D], BF16)
    ccv_i = idram("ccv_i", [TPC, NH * 65], BF16)
    ccv_o = idram("ccv_o", [NCORES * TPC, NH * 65], BF16)
    cc_n_in = idram("cc_n_in", [NH, 1])
    cc_n_out = idram("cc_n_out", [NCORES * NH, 1])
    cc_t_in = None if not _want0("ccx") else idram("cc_t_in", [2 * D, 3])
    cc_t_out = None if not _want0("ccx") else idram("cc_t_out", [NCORES * 2 * D, 3])
    cc_s_in = None if not _want0("ccx") else idram("cc_s_in", [128, 4096], BF16)
    lz_d = idram("lz_d", [128, 6 * 2048], BF16)
    y_d = idram("y_d", [D, TPC], BF16)
    q_d = idram("q_d", [D, TPC], BF16)
    cc_s_out = None if not _want0("ccx") else idram("cc_s_out", [NCORES * 128, 4096], BF16)
    cc_d_in = None if not _want0("ccx") else idram("cc_d_in", [NHEADS, 2])
    cc_d_out = None if not _want0("ccx") else idram("cc_d_out", [NCORES * NHEADS, 2])
    cc_p_in = None if not _want0("ccx") else idram("cc_p_in", [65, TPC])
    cc_p_out = None if not _want0("ccx") else idram("cc_p_out", [NCORES * 65, TPC])
    cc_q_in = None if not _want0("ccx") else idram("cc_q_in", [TPC, SUMM])
    cc_q_out = None if not _want0("ccx") else idram("cc_q_out", [NCORES * TPC, SUMM])

    RG = [list(range(NCORES))]

    with contextlib.ExitStack() as st:
        block = st.enter_context(nc.Block())
        ctx = Ctx(nc, block, st)
        ctx.mksem("t", "v", "s", "g", "dma", "gd", "cc", "pre")
        ctx.cpool = st.enter_context(nc.sbuf_tensor("cpool", [1, 64], F32))

        X = [st.enter_context(nc.sbuf_tensor(f"x{c}", [128, TPC], F32))
             for c in range(8)]
        idf = st.enter_context(nc.sbuf_tensor("idf", [128, 128], F32))
        idb = st.enter_context(nc.sbuf_tensor("idb", [128, 128], BF16))
        onesf = st.enter_context(nc.sbuf_tensor("onesf", [1, 128], F32))
        onecol = st.enter_context(nc.sbuf_tensor("onecol", [128, 1], F32))
        sq = st.enter_context(nc.sbuf_tensor("sq", [128, 512], F32))
        strow = st.enter_context(nc.sbuf_tensor("strow", [1, 4 * 512], F32))
        ctx.T = {"sq": sq, "strow": strow, "onecol": onecol, "onesf": onesf, "X": X}

        import os as _osz
        def _ld(sync, dma):
            for c in range(8):
                dma(out=X[c][:], in_=x_fm[c * 128:(c + 1) * 128, :])
            if identf is None:
                return
            dma(out=idf[:], in_=identf)
            dma(out=idb[:], in_=identb)
        if _osz.environ.get("KSKIP_LD", "") != "1":
            ctx.dstep((), _ld)
        def _ones(v):
            v.memset(onesf[:], 1.0)
            return v.memset(onecol[:], 1.0)
        ctx.vstep((), _ones)

        import os as _os0
        if _os0.environ.get("KSTAGE", "") == "mini":
            def _ccm(g):
                ctx.inc(g.collective_compute(
                    "AllGather", ALU.bypass, replica_groups=RG,
                    ins=[cc_k_in.opt()], outs=[cc_k_out.opt()]), "cc")
                ctx.inc(g.collective_compute(
                    "AllGather", ALU.bypass, replica_groups=RG,
                    ins=[ccv_i.opt()], outs=[ccv_o.opt()]), "cc")
                ctx.inc(g.collective_compute(
                    "AllGather", ALU.bypass, replica_groups=RG,
                    ins=[cc_n_in.opt()], outs=[cc_n_out.opt()]), "cc")
                return None
            ctx.step("gpsimd", ("dma",), _ccm)
            def _ccw(g):
                ctx.wait(g, "cc")
                return None
            ctx.step("gpsimd", ("cc",), _ccw)
            emit_store_out(ctx, X, out_fm)
            return nc
        with contextlib.ExitStack() as p1:
            p1.enter_context(nc.named_scope("ph1_attn"))
            dead_sb = p1.enter_context(nc.sbuf_tensor("dead_sb", [128, 256], F32))
            dmask_sb = p1.enter_context(
                nc.sbuf_tensor("dmask_sb", [128, 4 * 512], BF16))
            def _ldg(g, dma):
                dma(out=dmask_sb[:], in_=diagmasks)
            ctx.gdma((), _ldg)
            def _ldd(sync, dma):
                dma(out=dead_sb[:], in_=dead)
            ctx.dstep((), _ldd)
            wsb = [p1.enter_context(nc.sbuf_tensor(f"wsb{c}", [128, 1024], BF16))
                   for c in range(8)]
            qrow = p1.enter_context(nc.sbuf_tensor("qrow", [1, 16], F32))
            krow = p1.enter_context(nc.sbuf_tensor("krow", [1, 16], F32))
            knorm = p1.enter_context(nc.sbuf_tensor("knorm", [NH, 1], F32))
            ktmp = p1.enter_context(nc.sbuf_tensor("ktmp", [16, 8], F32))
            btab = p1.enter_context(nc.sbuf_tensor("btab", [128, 256], F32))
            livetab = p1.enter_context(nc.sbuf_tensor("livetab", [128, 16], F32))

            with contextlib.ExitStack() as pp:
                pos_sb = pp.enter_context(
                    nc.sbuf_tensor("pos_sb", [128, TPC], BF16))
                brow = pp.enter_context(nc.sbuf_tensor("brow", [1, 256], F32))
                qsb = [pp.enter_context(nc.sbuf_tensor(f"q{c}", [128, TPC], BF16))
                       for c in range(8)]
                vstage = pp.enter_context(
                    nc.sbuf_tensor("vstage", [128, NH * 65], BF16))
                def _ldpos(g, dma):
                    dma(out=pos_sb[:], in_=pos2)
                ctx.gdma((), _ldpos)
                ln = [pp.enter_context(nc.sbuf_tensor(f"ln{c}", [128, TPC], BF16))
                      for c in range(8)]
                kch1 = pp.enter_context(nc.sbuf_tensor("kch1", [128, TPC], BF16))
                pcv_u = pp.enter_context(nc.sbuf_tensor("pcv_u", [128, 2 * 4096], U8))
                pcv_b = pp.enter_context(nc.sbuf_tensor("pcv_b", [128, 4096], BF16))
                import os as _os
                _nopre = _os.environ.get("KNOPRE", "") == "1"
                def _preconv(g):
                    for c in range(8):
                        if ctx.cnt["pre"]:
                            g.wait_ge(ctx.sem["pre"], ctx.cnt["pre"])
                        i0 = g.dma_start(out=pcv_u[:, 0:4096],
                                         in_=k0u8[c * 128:(c + 1) * 128, :])
                        i1 = g.dma_start(
                            out=pcv_u[:, 4096:8192],
                            in_=k0u8[1024 + c * 128:1024 + (c + 1) * 128, :])
                        i0.then_inc(ctx.sem["pre"], 16)
                        i1.then_inc(ctx.sem["pre"], 16)
                        ctx.cnt["pre"] += 32
                        g.wait_ge(ctx.sem["pre"], ctx.cnt["pre"])
                        cv = g.tensor_tensor(pcv_b[:], pcv_u[:, 0:4096],
                                             pcv_u[:, 4096:8192], ALU.subtract)
                        cv.then_inc(ctx.sem["g"], 1)
                        ctx.cnt["g"] += 1
                        g.wait_ge(ctx.sem["g"], ctx.cnt["g"])
                        st = g.dma_start(out=w0h_d[c * 128:(c + 1) * 128, :],
                                         in_=pcv_b[:])
                        st.then_inc(ctx.sem["pre"], 16)
                        ctx.cnt["pre"] += 16
                    for (src, dst, rows, ncols) in ((k1u8, w1_d, 4096, 4096),
                                                    (k2u8, w2_d, 4096, 1024)):
                        for c in range(rows // 128):
                            g.wait_ge(ctx.sem["pre"], ctx.cnt["pre"])
                            i0 = g.dma_start(out=pcv_u[:, 0:ncols],
                                             in_=src[c * 128:(c + 1) * 128, :])
                            i0.then_inc(ctx.sem["pre"], 16)
                            ctx.cnt["pre"] += 16
                            g.wait_ge(ctx.sem["pre"], ctx.cnt["pre"])
                            cv = g.tensor_scalar(pcv_b[:, 0:ncols],
                                                 pcv_u[:, 0:ncols],
                                                 2.0, -1.0, ALU.mult, ALU.add)
                            cv.then_inc(ctx.sem["g"], 1)
                            ctx.cnt["g"] += 1
                            g.wait_ge(ctx.sem["g"], ctx.cnt["g"])
                            st = g.dma_start(out=dst[c * 128:(c + 1) * 128, :],
                                             in_=pcv_b[:, 0:ncols])
                            st.then_inc(ctx.sem["pre"], 16)
                            ctx.cnt["pre"] += 16
                    return None
                if not _nopre:
                    ctx.step("gpsimd", ("dma",), _preconv)
                P = {
                    "row": pp.enter_context(nc.psum_tensor("p_row", [1, 1024], F32)),
                    "b0": pp.enter_context(nc.psum_tensor("p_b0", [128, 512], F32)),
                    "b1": pp.enter_context(nc.psum_tensor("p_b1", [128, 512], F32)),
                    "m0": pp.enter_context(nc.psum_tensor("p_m0", [128, 512], F32)),
                    "m1": pp.enter_context(nc.psum_tensor("p_m1", [128, 512], F32)),
                    "n": pp.enter_context(nc.psum_tensor("p_n", [1, 512], F32)),
                }

                import os as _os
                _kstage = _os.environ.get("KSTAGE", "")
                emit_layernorm(ctx, P, X, ln, 0, TPC)
                if _kstage == "ln":
                    if _os.environ.get("KDBG", "") == "ln":
                        emit_store_dbg(ctx, ln, out_fm)
                    else:
                        emit_store_out(ctx, X, out_fm)
                    return nc

                def _ldw(which):
                    def f(g, dma):
                        for c in range(8):
                            dma(out=wsb[c][:], in_=which[c * 128:(c + 1) * 128, :])
                    return f
                ctx.gdma(("t", "v"), _ldw(wq))

                def emit_proj_fm(out_chunks, add_pos):
                    for oc in range(8):
                        for tt in range(4):
                            ps = P["m0"] if (oc + tt) % 2 == 0 else P["m1"]
                            def _mm(t, oc=oc, tt=tt, ps=ps):
                                last = None
                                for dc in range(8):
                                    last = t.matmul(
                                        ps[:, 0:512],
                                        wsb[dc][:, oc * 128:(oc + 1) * 128],
                                        ln[dc][:, tt * 512:(tt + 1) * 512],
                                        start=(dc == 0), stop=(dc == 7))
                                return last
                            ctx.tstep(("v", "dma"), _mm)
                            if add_pos:
                                def _ev(v, oc=oc, tt=tt, ps=ps):
                                    return v.scalar_tensor_tensor(
                                        out_chunks[oc][:, tt * 512:(tt + 1) * 512],
                                        ps[:, 0:512], 1.0,
                                        pos_sb[:, tt * 512:(tt + 1) * 512],
                                        ALU.mult, ALU.add)
                            else:
                                def _ev(v, oc=oc, tt=tt, ps=ps):
                                    return v.tensor_copy(
                                        out_chunks[oc][:, tt * 512:(tt + 1) * 512],
                                        ps[:, 0:512])
                            ctx.vstep(("t",), _ev)

                emit_proj_fm(qsb, True)

                def emit_chunk_maxsq(src, dc, out_row):
                    for half in range(4):
                        def _sq(s_, half=half):
                            return s_.activation(sq[:, 0:512],
                                                 src[:, half * 512:(half + 1) * 512],
                                                 AF.Square)
                        ctx.sstep(("t", "v"), _sq)
                        def _nm(t, half=half):
                            t.matmul(P["n"][0:1, 0:512],
                                     onecol[0:64, 0:1], sq[0:64, 0:512],
                                     start=True, stop=True)
                            return t.matmul(P["row"][0:1, 0:512],
                                            onecol[64:128, 0:1], sq[64:128, 0:512],
                                            start=True, stop=True)
                        ctx.tstep(("s",), _nm)
                        def _mxr(v, half=half):
                            v.tensor_reduce(strow[0:1, 0:1],
                                            P["n"][0:1, 0:512],
                                            AX.X, ALU.max)
                            return v.tensor_reduce(strow[0:1, 1:2],
                                                   P["row"][0:1, 0:512],
                                                   AX.X, ALU.max)
                        ctx.vstep(("t",), _mxr)
                        def _mxc(v, dc=dc, half=half):
                            if half == 0:
                                v.tensor_copy(out_row[0:1, 2 * dc:2 * dc + 1],
                                              strow[0:1, 0:1])
                                return v.tensor_copy(
                                    out_row[0:1, 2 * dc + 1:2 * dc + 2],
                                    strow[0:1, 1:2])
                            v.tensor_max(out_row[0:1, 2 * dc:2 * dc + 1],
                                         out_row[0:1, 2 * dc:2 * dc + 1],
                                         strow[0:1, 0:1])
                            return v.tensor_max(
                                out_row[0:1, 2 * dc + 1:2 * dc + 2],
                                out_row[0:1, 2 * dc + 1:2 * dc + 2],
                                strow[0:1, 1:2])
                        ctx.vstep((), _mxc)

                for dc in range(8):
                    emit_chunk_maxsq(qsb[dc], dc, qrow)

                ctx.gdma(("t", "v"), _ldw(wk))
                for oc in range(8):
                    for tt in range(4):
                        ps = P["m0"] if tt % 2 == 0 else P["m1"]
                        def _mm(t, oc=oc, tt=tt, ps=ps):
                            last = None
                            for dc in range(8):
                                last = t.matmul(
                                    ps[:, 0:512],
                                    wsb[dc][:, oc * 128:(oc + 1) * 128],
                                    ln[dc][:, tt * 512:(tt + 1) * 512],
                                    start=(dc == 0), stop=(dc == 7))
                            return last
                        ctx.tstep(("v", "dma"), _mm)
                        def _ev(v, oc=oc, tt=tt, ps=ps):
                            return v.scalar_tensor_tensor(
                                kch1[:, tt * 512:(tt + 1) * 512],
                                ps[:, 0:512], 1.0,
                                pos_sb[:, tt * 512:(tt + 1) * 512],
                                ALU.mult, ALU.add)
                        ctx.vstep(("t",), _ev)
                    emit_chunk_maxsq(kch1, oc, krow)
                    def _stk(sync, dma, oc=oc):
                        dma(out=cc_k_in[oc * 128:(oc + 1) * 128, :],
                            in_=kch1[:, 0:1024])
                        dma(out=cc_k_in[1024 + oc * 128:1024 + (oc + 1) * 128, :],
                            in_=kch1[:, 1024:2048])
                    ctx.dstep(("v", "s"), _stk)
                def _trn(t):
                    return t.transpose(P["m0"][0:16, 0:1], krow[:, 0:16],
                                       idf[0:1, 0:1])
                ctx.tstep(("v",), _trn)
                def _evn(v):
                    return v.tensor_copy(knorm[:], P["m0"][0:16, 0:1])
                ctx.vstep(("t",), _evn)
                def _stn(sync, dma):
                    dma(out=cc_n_in, in_=knorm[:])
                ctx.dstep(("v",), _stn)

                ctx.gdma(("t", "v"), _ldw(wv))
                def _vones(v):
                    return v.memset(
                        vstage[:].rearrange("p (h c) -> p h c", c=65)[:, :, 64:65],
                        1.0)
                ctx.vstep((), _vones)
                for tc in range(16):
                    for half in range(2):
                        ps = P["m0"] if half == 0 else P["m1"]
                        def _mmv(t, tc=tc, half=half, ps=ps):
                            last = None
                            for dc in range(8):
                                last = t.matmul(
                                    ps[:, 0:512],
                                    ln[dc][:, tc * 128:(tc + 1) * 128],
                                    wsb[dc][:, half * 512:(half + 1) * 512],
                                    start=(dc == 0), stop=(dc == 7))
                            return last
                        ctx.tstep(("dma", "v"), _mmv)
                        def _evv(v, half=half, ps=ps):
                            dst = vstage[:].rearrange("p (h c) -> p h c", c=65)[
                                :, half * 8:(half + 1) * 8, 0:64]
                            return v.tensor_copy(
                                dst, ps[:, 0:512].rearrange("p (h c) -> p h c", c=64))
                        ctx.vstep(("t",), _evv)
                    def _stv(sync, dma, tc=tc):
                        dma(out=ccv_i[tc * 128:(tc + 1) * 128, :], in_=vstage[:])
                    ctx.dstep(("v",), _stv)

                def _stq(sync, dma):
                    for hp in range(8):
                        dma(out=q_d[hp * 128:(hp + 1) * 128, :],
                            in_=qsb[hp][:])
                ctx.dstep(("v", "s"), _stq)

                if _kstage == "qkv":
                    if _os.environ.get("KDBG", "") == "q":
                        emit_store_dbg(ctx, qsb, out_fm)
                    else:
                        emit_store_out(ctx, X, out_fm)
                    return nc
                def _cc1(g):
                    ctx.inc(g.collective_compute(
                        "AllGather", ALU.bypass, replica_groups=RG,
                        ins=[cc_k_in.opt()], outs=[cc_k_out.opt()]), "cc")
                    ctx.inc(g.collective_compute(
                        "AllGather", ALU.bypass, replica_groups=RG,
                        ins=[ccv_i.opt()], outs=[ccv_o.opt()]), "cc")
                    ctx.inc(g.collective_compute(
                        "AllGather", ALU.bypass, replica_groups=RG,
                        ins=[cc_n_in.opt()], outs=[cc_n_out.opt()]), "cc")
                    return None
                ctx.step("gpsimd", ("dma",), _cc1)

                if _kstage == "cc":
                    def _ccw(g):
                        ctx.wait(g, "cc")
                        return None
                    ctx.step("gpsimd", ("cc",), _ccw)
                    emit_store_out(ctx, X, out_fm)
                    return nc
                def _ldkn(sync, dma):
                    dma(out=ktmp[:],
                        in_=cc_n_out.rearrange("(r h) c -> h (r c)", r=8),
                        allow_slow_non_contiguous=True)
                ctx.dstep(("cc",), _ldkn)
                if _kstage == "ldkn":
                    emit_store_out(ctx, X, out_fm)
                    return nc
                def _b1(v):
                    return v.tensor_reduce(ktmp[:, 0:1], ktmp[:], AX.X, ALU.max)
                ctx.vstep(("dma",), _b1)
                def _b2(t):
                    return t.transpose(P["m0"][0:1, 0:16], ktmp[0:16, 0:1],
                                       idf[0:16, 0:16])
                ctx.tstep(("v",), _b2)
                def _b3a(v):
                    return v.tensor_copy(krow[:, 0:16], P["m0"][0:1, 0:16])
                ctx.vstep(("t",), _b3a)
                def _b3b(v):
                    return v.tensor_mul(krow[:, 0:16], krow[:, 0:16], qrow[:, 0:16])
                ctx.vstep((), _b3b)
                def _b4(s):
                    return s.activation(krow[:, 0:16], krow[:, 0:16], AF.Sqrt)
                ctx.sstep(("v",), _b4)
                def _b5(v):
                    last = None
                    for i in range(16):
                        last = v.tensor_scalar(brow[:, i * 16:(i + 1) * 16],
                                               krow[:, 0:16], -1.0 / 32.0, None,
                                               ALU.mult)
                    return last
                ctx.vstep(("s",), _b5)
                def _b6(t):
                    return t.matmul(P["b0"][:, 0:256], onesf[0:1, 0:128], brow[:],
                                    start=True, stop=True)
                ctx.tstep(("v",), _b6)
                def _b7(v):
                    v.tensor_copy(livetab[:], P["b0"][:, 0:16])
                    return v.tensor_add(btab[:], P["b0"][:, 0:256], dead_sb[:])
                ctx.vstep(("t",), _b7)

            if _kstage == "bias":
                emit_store_out(ctx, X, out_fm)
                return nc
            ctx.barrier()

            with contextlib.ExitStack() as pp:
                SA = pp.enter_context(nc.psum_tensor("p_sa", [128, 1024], F32))
                SB = pp.enter_context(nc.psum_tensor("p_sb", [128, 1024], F32))
                OA = pp.enter_context(nc.psum_tensor("p_oa", [128, 512], F32))
                OB = pp.enter_context(nc.psum_tensor("p_ob", [128, 512], F32))
                U6 = pp.enter_context(nc.psum_tensor("p_u6", [128, 512], F32))
                U7 = pp.enter_context(nc.psum_tensor("p_u7", [128, 512], F32))
                kkb = [pp.enter_context(
                    nc.sbuf_tensor(f"kkb{p}", [128, 8 * 1024], BF16))
                    for p in range(2)]
                vkb = [pp.enter_context(
                    nc.sbuf_tensor(f"vkb{p}", [128, 8 * 1040], BF16))
                    for p in range(2)]
                ptile = pp.enter_context(nc.sbuf_tensor("ptile", [128, 2048], BF16))
                qg = [pp.enter_context(nc.sbuf_tensor(f"qg{c}", [128, 512], BF16))
                      for c in range(8)]
                acc = pp.enter_context(nc.sbuf_tensor("acc", [65, NH * 512], F32))
                nrmst = pp.enter_context(nc.sbuf_tensor("nrmst", [128, 512], BF16))

                def kload(slot_src, base, pb):
                    def f(sync, dma):
                        for c in range(8):
                            dma(out=kkb[pb][:, c * 1024:(c + 1) * 1024],
                                in_=cc_k_out[base + c * 128:base + (c + 1) * 128, :]
                                if slot_src == "g" else
                                cc_k_in[base + c * 128:base + (c + 1) * 128, :])
                            dma(out=vkb[pb][:, c * 1040:(c + 1) * 1040],
                                in_=ccv_o[base + c * 128:base + (c + 1) * 128, :]
                                if slot_src == "g" else
                                ccv_i[base + c * 128:base + (c + 1) * 128, :])
                    return f

                def kload_args(qb, pkind, kp):
                    if pkind == "diag":
                        return ("o", qb * 1024, ("t", "s"))
                    slot = 2 * kp if qb == 0 else 2 * (7 - kp) + 1
                    return ("g", slot * 1024, ("t", "s", "cc"))

                ctx.gdma(("t",), _ldw(wo))

                import os
                _qbr = int(os.environ.get("KQB", "2"))
                _tqr = int(os.environ.get("KTQ", "2"))
                plist0 = [("diag", None)] + [("bulk", kp) for kp in range(7)]
                sched = [(qb, tqh, pk, kp) for qb in range(_qbr)
                         for tqh in range(_tqr) for pk, kp in plist0]
                kv_ready = {}
                src0, base0, ldeps0 = kload_args(sched[0][0], sched[0][2],
                                                 sched[0][3])
                ctx.dstep(ldeps0, kload(src0, base0, 0))
                kv_ready[0] = ctx.snap("dma")
                pidx = 0
                sprev = {0: 0, 1: 0}
                for qb in range(_qbr):
                    for tqh in range(_tqr):
                        tq0 = (qb * 2 + tqh) * 512
                        def _ldqg(sync, dma, tq0=tq0):
                            for hp in range(8):
                                dma(out=qg[hp][:],
                                    in_=q_d[hp * 128:(hp + 1) * 128,
                                            tq0:tq0 + 512])
                        ctx.dstep(("t",), _ldqg)
                        qgsnap = ctx.snap("dma")
                        accsnap = {}
                        def _zacc(v):
                            return v.memset(acc[:], 0.0)
                        ctx.vstep(("t",), _zacc)
                        for pkind, kp in plist0:
                            pb = pidx % 2
                            if pidx + 1 < len(sched):
                                nqb, _ntq, npk, nkp = sched[pidx + 1]
                                srcn, basen, ldepsn = kload_args(nqb, npk, nkp)
                                ctx.dstep(ldepsn,
                                          kload(srcn, basen, (pidx + 1) % 2))
                                kv_ready[pidx + 1] = ctx.snap("dma")
                            if pkind == "diag":
                                tks = list(range(0, 4)) if tqh == 0 else list(range(8))
                            else:
                                tks = list(range(8))
                            kvsnap = max(kv_ready[pidx], qgsnap)
                            pidx += 1
                            for hp in range(8):
                                PO_A = OA if hp % 2 == 0 else U6
                                PO_B = OB if hp % 2 == 0 else U7
                                accw = accsnap.get(hp % 2, 0)
                                for ti, tk in enumerate(tks):
                                    par = tk % 2
                                    first = ti == 0
                                    is_last = ti == len(tks) - 1
                                    def _qk(t, hp=hp, tk=tk, par=par, pb=pb):
                                        last = None
                                        for sub in range(2):
                                            ps_s = SA if sub == 0 else SB
                                            last = t.matmul(
                                                ps_s[:, par * 512:(par + 1) * 512],
                                                kkb[pb][sub * 64:sub * 64 + 64,
                                                    hp * 1024 + tk * 128:
                                                    hp * 1024 + (tk + 1) * 128],
                                                qg[hp][sub * 64:sub * 64 + 64,
                                                       0:512],
                                                start=True, stop=True,
                                                tile_position=(sub * 64, 0))
                                        return last
                                    qdeps = (("v",) if (pkind == "diag"
                                                        and ti == 0) else ())
                                    ctx.tstep(qdeps, _qk,
                                              waits=[("dma", kvsnap),
                                                     ("s", sprev[par])])
                                    delta = 128 * tk - 512 * tqh
                                    has_partial = (pkind == "diag"
                                                   and 0 <= delta <= 384)
                                    if has_partial:
                                        def _msk(v, par=par, delta=delta):
                                            mi = delta // 128
                                            last = None
                                            for sub in range(2):
                                                ps_s = SA if sub == 0 else SB
                                                last = v.tensor_add(
                                                    ps_s[:, par * 512:(par + 1) * 512],
                                                    ps_s[:, par * 512:(par + 1) * 512],
                                                    dmask_sb[:, mi * 512:(mi + 1) * 512])
                                            return last
                                        ctx.vstep(("t",), _msk)
                                    def _exp(s, hp=hp, par=par,
                                             pkind=pkind, kp=kp, qb=qb):
                                        last = None
                                        for sub in range(2):
                                            h = 2 * hp + sub
                                            ps_s = SA if sub == 0 else SB
                                            if pkind == "diag":
                                                bt, bc = livetab, h
                                            else:
                                                bt, bc = btab, (qb * 8 + kp) * 16 + h
                                            last = s.activation(
                                                ptile[:, sub * 1024 + par * 512:
                                                      sub * 1024 + par * 512 + 512],
                                                ps_s[:, par * 512:(par + 1) * 512],
                                                AF.Exp, bias=bt[:, bc:bc + 1],
                                                scale=1.0 / 32.0)
                                        return last
                                    ctx.sstep(("t", "v") if has_partial
                                              else ("t",), _exp)
                                    sprev[par] = ctx.snap("s")
                                    def _av(t, hp=hp, tk=tk, par=par,
                                            first=first, is_last=is_last,
                                            PO_A=PO_A, PO_B=PO_B, pb=pb):
                                        last = None
                                        for sub in range(2):
                                            ps_o = PO_A if sub == 0 else PO_B
                                            h = 2 * hp + sub
                                            last = t.matmul(
                                                ps_o[0:65, 0:512],
                                                vkb[pb][:, tk * 1040 + h * 65:
                                                    tk * 1040 + (h + 1) * 65],
                                                ptile[:, sub * 1024 + par * 512:
                                                      sub * 1024 + par * 512 + 512],
                                                start=first, stop=is_last)
                                        return last
                                    ctx.tstep(("s",), _av,
                                              waits=[("v", accw)]
                                              if first else ())
                                def _acc(v, hp=hp, PO_A=PO_A, PO_B=PO_B):
                                    last = None
                                    for sub in range(2):
                                        ps_o = PO_A if sub == 0 else PO_B
                                        h = 2 * hp + sub
                                        last = v.tensor_add(
                                            acc[:, h * 512:(h + 1) * 512],
                                            acc[:, h * 512:(h + 1) * 512],
                                            ps_o[0:65, 0:512])
                                    return last
                                ctx.vstep(("t",), _acc)
                                accsnap[hp % 2] = ctx.snap("v")
                        gslots = [SA[:, 0:512], SA[:, 512:1024],
                                  SB[:, 0:512], SB[:, 512:1024]]
                        for ocg in range(2):
                            for h in range(NH):
                                def _rc(v, h=h):
                                    return v.reciprocal(
                                        strow[0:1, 0:512],
                                        acc[64:65, h * 512:(h + 1) * 512])
                                ctx.vstep(("t",), _rc)
                                def _bc(t, h=h):
                                    return t.matmul(U6[0:64, 0:512],
                                                    onesf[0:1, 0:64],
                                                    strow[0:1, 0:512],
                                                    start=True, stop=True)
                                ctx.tstep(("v", "s"), _bc)
                                par = (h % 2) * 64
                                def _nm(v, h=h, par=par):
                                    return v.tensor_mul(
                                        nrmst[par:par + 64, :],
                                        acc[0:64, h * 512:(h + 1) * 512],
                                        U6[0:64, 0:512])
                                ctx.vstep(("t",), _nm)
                                def _wo(t, h=h, ocg=ocg, par=par):
                                    last = None
                                    for oi in range(4):
                                        oc = ocg * 4 + oi
                                        last = t.matmul(
                                            gslots[oi],
                                            wsb[h // 2][par:par + 64,
                                                        oc * 128:(oc + 1) * 128],
                                            nrmst[par:par + 64, :],
                                            start=(h == 0), stop=(h == NH - 1))
                                    return last
                                ctx.tstep(("v", "gd"), _wo)
                            for oi in range(4):
                                oc = ocg * 4 + oi
                                def _upd(v, oc=oc, oi=oi, tq0=tq0):
                                    return v.tensor_add(X[oc][:, tq0:tq0 + 512],
                                                        X[oc][:, tq0:tq0 + 512],
                                                        gslots[oi])
                                ctx.vstep(("t",), _upd)

        if stop_after == "attn_a":
            emit_store_out(ctx, X, out_fm)
            return nc

        ctx.barrier()
        with contextlib.ExitStack() as p2:
            p2.enter_context(nc.named_scope("ph2_ffn1"))
            lnf = [p2.enter_context(nc.sbuf_tensor(f"lnf{c}", [128, 512], BF16))
                   for c in range(8)]
            a1 = [p2.enter_context(nc.sbuf_tensor(f"a1_{c}", [128, 512], BF16))
                  for c in range(32)]
            a2 = [p2.enter_context(nc.sbuf_tensor(f"a2_{c}", [128, 512], BF16))
                  for c in range(32)]
            wff = [[p2.enter_context(
                nc.sbuf_tensor(f"wff{p}_{c}", [128, 512], BF16))
                for c in range(8)] for p in range(2)]
            w1b = [[p2.enter_context(
                nc.sbuf_tensor(f"w1b{p}_{c}", [128, 256], BF16))
                for c in range(32)] for p in range(2)]
            scl = p2.enter_context(nc.sbuf_tensor("scl", [128, 3], F32))
            srow = p2.enter_context(nc.sbuf_tensor("srow", [1, 4], F32))
            with contextlib.ExitStack() as pq:
                P2 = {
                    "row": pq.enter_context(nc.psum_tensor("q_row", [1, 1024], F32)),
                    "b0": pq.enter_context(nc.psum_tensor("q_b0", [128, 512], F32)),
                    "b1": pq.enter_context(nc.psum_tensor("q_b1", [128, 512], F32)),
                    "m0": pq.enter_context(nc.psum_tensor("q_m0", [128, 512], F32)),
                    "m1": pq.enter_context(nc.psum_tensor("q_m1", [128, 512], F32)),
                    "u": pq.enter_context(nc.psum_tensor("q_u", [128, 512], F32)),
                }
                def _lds(sync, dma):
                    dma(out=srow[0:1, 0:1], in_=s0in)
                    dma(out=srow[0:1, 1:2], in_=s1in)
                    dma(out=srow[0:1, 2:3], in_=s2in)
                ctx.dstep((), _lds)
                def _s2(v):
                    return v.tensor_scalar(srow[0:1, 0:3], srow[0:1, 0:3], 2.0,
                                           None, ALU.mult)
                ctx.vstep(("dma",), _s2)
                def _sbc(t):
                    return t.matmul(P2["u"][:, 0:3], onesf[0:1, 0:128],
                                    srow[0:1, 0:3], start=True, stop=True)
                ctx.tstep(("v",), _sbc)
                def _scp(v):
                    return v.tensor_copy(scl[:], P2["u"][:, 0:3])
                ctx.vstep(("t",), _scp)

                emit_ffn(ctx, nc, X, out_fm, (w0h_d, w1_d, w2_d), scl, P2,
                         {"lnf": lnf, "a1": a1, "a2": a2, "wff": wff, "w1b": w1b})

        if stop_after == "ffn1":
            emit_store_out(ctx, X, out_fm)
            return nc

        ctx.barrier()
        import os as _os3
        KD3 = _os3.environ.get("KDBG3", "")
        with contextlib.ExitStack() as p3:
            p3.enter_context(nc.named_scope("ph3_attnb"))
            y_tl = [p3.enter_context(nc.sbuf_tensor(f"yt{c}", [128, 512], BF16))
                    for c in range(8)]
            qret = p3.enter_context(nc.sbuf_tensor("qret", [64, TPC], BF16))
            nw_sb = p3.enter_context(nc.sbuf_tensor("nw_sb", [128, 8], F32))
            cw_sb = p3.enter_context(nc.sbuf_tensor("cw_sb", [128, 72], F32))
            cb_sb = p3.enter_context(nc.sbuf_tensor("cb_sb", [128, 18], F32))
            nw2_sb = p3.enter_context(nc.sbuf_tensor("nw2_sb", [128, 16], F32))
            dtb_sb = p3.enter_context(nc.sbuf_tensor("dtb_sb", [NHEADS, 1], F32))
            alog_sb = p3.enter_context(nc.sbuf_tensor("alog_sb", [NHEADS, 1], F32))
            aneg_sb = p3.enter_context(nc.sbuf_tensor("aneg_sb", [NHEADS, 1], F32))
            dcap_sb = p3.enter_context(nc.sbuf_tensor("dcap_sb", [NHEADS, 1], F32))
            drep = p3.enter_context(nc.sbuf_tensor("drep", [128, NHEADS], F32))
            wsc1 = [p3.enter_context(nc.sbuf_tensor(f"wsc1_{c}", [128, 256], BF16))
                    for c in range(8)]
            wsc2 = p3.enter_context(nc.sbuf_tensor("wsc2", [128, 2], BF16))
            wsumm = [p3.enter_context(nc.sbuf_tensor(f"wsm{c}", [128, 64], BF16))
                     for c in range(8)]
            wqw = [p3.enter_context(nc.sbuf_tensor(f"wqw{c}", [128, 64], BF16))
                   for c in range(8)]
            onesq = p3.enter_context(nc.sbuf_tensor("onesq", [128, 128], BF16))
            ones32 = p3.enter_context(nc.sbuf_tensor("ones32", [32, 256], F32))
            trin_sb = p3.enter_context(nc.sbuf_tensor("trin_sb", [128, 512], BF16))

            if KD3 == "alloc":
                emit_store_out(ctx, X, out_fm)
                return nc
            def _ldsmall(sync, dma):
                dma(out=nw_sb[:], in_=ab_norm_w)
                dma(out=cw_sb[:], in_=ab_conv_w)
                dma(out=cb_sb[:], in_=ab_conv_b)
                dma(out=nw2_sb[:], in_=ab_ssm_norm_w)
                dma(out=dtb_sb[:], in_=ab_dt_bias)
                dma(out=alog_sb[:], in_=ab_A_log)
                dma(out=dcap_sb[:], in_=ab_D)
            if _os3.environ.get("K3A", "") != "skip":
                ctx.dstep((), _ldsmall)

            def _ldg2(g, dma):
                grp = _os3.environ.get("K3G", "all")
                if grp in ("all", "trin"):
                    dma(out=trin_sb[:], in_=trineg)
                if grp in ("all", "wsc2"):
                    dma(out=wsc2[:], in_=ab_scorer_w2)
                for c in range(8):
                    if grp in ("all", "wsc1"):
                        dma(out=wsc1[c][:], in_=ab_scorer_w1[c * 128:(c + 1) * 128, :])
                    if grp in ("all", "wsm"):
                        dma(out=wsumm[c][:], in_=ab_summ_w[c * 128:(c + 1) * 128, :])
                        dma(out=wqw[c][:], in_=ab_q_w[c * 128:(c + 1) * 128, :])
            if _os3.environ.get("K3B", "") != "skip":
                ctx.gdma((), _ldg2)

            def _mset(v):
                v.memset(onesq[:], 1.0)
                return v.memset(ones32[:], 1.0)
            if _os3.environ.get("K3C", "") != "skip":
                ctx.vstep((), _mset)

            xh = p3.enter_context(nc.sbuf_tensor("xh", [128, 48], F32))
            lnh = [p3.enter_context(nc.sbuf_tensor(f"lnh{c}", [128, 6], BF16))
                   for c in range(8)]
            xnh = [p3.enter_context(nc.sbuf_tensor(f"xnh{c}", [128, 6], BF16))
                   for c in range(8)]
            hmask = p3.enter_context(nc.sbuf_tensor("hmask", [128, 2], F32))
            hoffs = p3.enter_context(nc.sbuf_tensor("hoffs", [128, 16], U32))

            def _tst(sync, dma):
                dma(out=hmask[:], in_=halo_mask)
                dma(out=hoffs[:], in_=halo_offs)
                for c in range(8):
                    dma(out=cc_t_in[c * 128:(c + 1) * 128, :],
                        in_=X[c][:, 1021:1024])
                    dma(out=cc_t_in[1024 + c * 128:1024 + (c + 1) * 128, :],
                        in_=X[c][:, 2045:2048])
            ctx.dstep(("v",), _tst)

            if KD3 == "pre3":
                emit_store_out(ctx, X, out_fm)
                return nc
            def _cc2(g):
                ctx.inc(g.collective_compute(
                    "AllGather", ALU.bypass, replica_groups=RG,
                    ins=[cc_t_in.opt()], outs=[cc_t_out.opt()]), "cc")
                return None
            ctx.step("gpsimd", ("dma",), _cc2)

            def _hgather(g):
                for c in range(8):
                    i0 = g.indirect_dma_start(
                        out=xh[:, c * 6:c * 6 + 3], out_offset=None,
                        in_=cc_t_out,
                        in_offset=bass.IndirectOffsetOnAxis(
                            ap=hoffs[:, c:c + 1], axis=0))
                    i0.then_inc(ctx.sem["gd"], 16)
                    ctx.cnt["gd"] += 16
                    i1 = g.indirect_dma_start(
                        out=xh[:, c * 6 + 3:c * 6 + 6], out_offset=None,
                        in_=cc_t_out,
                        in_offset=bass.IndirectOffsetOnAxis(
                            ap=hoffs[:, 8 + c:9 + c], axis=0))
                    i1.then_inc(ctx.sem["gd"], 16)
                    ctx.cnt["gd"] += 16
                return None
            ctx.step("gpsimd", ("cc",), _hgather)

            with contextlib.ExitStack() as pq:
                P3 = {
                    "row": pq.enter_context(nc.psum_tensor("r_row", [1, 1024], F32)),
                    "b0": pq.enter_context(nc.psum_tensor("r_b0", [128, 512], F32)),
                    "b1": pq.enter_context(nc.psum_tensor("r_b1", [128, 512], F32)),
                    "m0": pq.enter_context(nc.psum_tensor("r_m0", [128, 512], F32)),
                    "m1": pq.enter_context(nc.psum_tensor("r_m1", [128, 512], F32)),
                }
                PSB3 = pq.enter_context(nc.psum_tensor("r_psb", [128, 1024], BF16))

                xhc = [xh[:, c * 6:(c + 1) * 6] for c in range(8)]
                emit_layernorm(ctx, P3, xhc, lnh, 0, 6)
                for dc in range(8):
                    def _hsq(s_, dc=dc):
                        return s_.activation(sq[:, 0:6], lnh[dc][:], AF.Square)
                    ctx.sstep(("t", "v", "dma"), _hsq)
                    def _hmm(t, dc=dc):
                        return t.matmul(P3["row"][0:1, 0:6], onecol[:], sq[:, 0:6],
                                        start=(dc == 0), stop=(dc == 7))
                    ctx.tstep(("s",), _hmm)
                def _hst(v):
                    return v.tensor_scalar(strow[0:1, 0:6], P3["row"][0:1, 0:6],
                                           1.0 / D, None, ALU.mult)
                ctx.vstep(("t",), _hst)
                epsc2 = ctx.constcell(EPS_LN)
                def _hsq2(s_):
                    return s_.activation(strow[0:1, 0:6], strow[0:1, 0:6],
                                         AF.Sqrt, bias=epsc2)
                ctx.sstep(("v",), _hsq2)
                def _hrc(v):
                    return v.reciprocal(strow[0:1, 0:6], strow[0:1, 0:6])
                ctx.vstep(("s",), _hrc)
                def _hbc(t):
                    return t.matmul(P3["b0"][:, 0:6], onesf[0:1, 0:128],
                                    strow[0:1, 0:6], start=True, stop=True)
                ctx.tstep(("v",), _hbc)
                for dc in range(8):
                    def _hap(v, dc=dc):
                        return v.scalar_tensor_tensor(
                            xnh[dc][:], lnh[dc][:], nw_sb[:, dc:dc + 1],
                            P3["b0"][:, 0:6], ALU.mult, ALU.mult)
                    ctx.vstep(("t",), _hap)

                if KD3 == "halo":
                    emit_store_out(ctx, X, out_fm)
                    return nc
                xn = [p3.enter_context(nc.sbuf_tensor(f"xn{c}", [128, 512], BF16))
                      for c in range(8)]
                zs = [p3.enter_context(nc.sbuf_tensor(f"zs{c}", [128, 512], BF16))
                      for c in range(16)]
                p3b = contextlib.ExitStack()
                xc = [p3b.enter_context(nc.sbuf_tensor(f"xc{c}", [128, 520], BF16))
                      for c in range(18)]
                stash = p3b.enter_context(nc.sbuf_tensor("stash", [128, 54], BF16))
                wip = [p3b.enter_context(nc.sbuf_tensor(f"wip{c}", [128, 512], BF16))
                       for c in range(8)]
                accD = p3b.enter_context(nc.sbuf_tensor("accD", [128, 512], F32))
                accP = p3b.enter_context(nc.sbuf_tensor("accP", [128, 512], F32))
                dtraw = p3b.enter_context(nc.sbuf_tensor("dtraw", [32, 512], F32))
                dtw = p3b.enter_context(nc.sbuf_tensor("dtw", [32, 512], F32))
                dt_sb = p3b.enter_context(nc.sbuf_tensor("dt_sb", [32, 512], F32))
                csf = p3b.enter_context(nc.sbuf_tensor("csf", [32, 512], F32))
                decf = p3b.enter_context(nc.sbuf_tensor("decf", [32, 256], F32))
                ddf = p3b.enter_context(nc.sbuf_tensor("ddf", [32, 256], F32))
                dssum = p3b.enter_context(nc.sbuf_tensor("dssum", [32, 8], F32))
                pzx = p3b.enter_context(nc.sbuf_tensor("pzx", [32, 8], F32))
                arow = p3b.enter_context(nc.sbuf_tensor("arow", [1, 32], F32))
                abc = p3b.enter_context(nc.sbuf_tensor("abc", [128, 32], F32))
                cstm = p3b.enter_context(nc.sbuf_tensor("cstm", [128, 64], F32))
                ncstm = p3b.enter_context(nc.sbuf_tensor("ncstm", [128, 64], F32))
                exptm = p3b.enter_context(nc.sbuf_tensor("exptm", [128, 64], F32))
                dttm = p3b.enter_context(nc.sbuf_tensor("dttm", [128, 64], F32))
                ddtm = p3b.enter_context(nc.sbuf_tensor("ddtm", [128, 64], F32))
                bmt = p3b.enter_context(nc.sbuf_tensor("bmt", [128, 256], BF16))
                cbts = p3b.enter_context(nc.sbuf_tensor("cbts", [128, 512], BF16))
                xstm = [p3b.enter_context(
                    nc.sbuf_tensor(f"xstm{p}", [128, 256], BF16))
                    for p in range(2)]
                xdbuf = [p3b.enter_context(
                    nc.sbuf_tensor(f"xdbuf{p}", [128, 128], BF16))
                    for p in range(2)]
                xddbuf = p3b.enter_context(nc.sbuf_tensor("xddbuf", [128, 128], BF16))
                colw = [p3b.enter_context(
                    nc.sbuf_tensor(f"colw{p}", [128, 128], F32))
                    for p in range(2)]
                mseg = [p3b.enter_context(
                    nc.sbuf_tensor(f"mseg{p}", [128, 512], F32))
                    for p in range(2)]
                expsg = [p3b.enter_context(
                    nc.sbuf_tensor(f"expsg{p}", [128, 512], BF16))
                    for p in range(2)]
                msM = [p3b.enter_context(
                    nc.sbuf_tensor(f"msM{p}", [128, 512], BF16))
                    for p in range(2)]
                cms = [p3b.enter_context(
                    nc.sbuf_tensor(f"cms{p}", [128, 256], BF16))
                    for p in range(2)]
                tybuf = [p3b.enter_context(
                    nc.sbuf_tensor(f"tybuf{p}", [128, 256], BF16))
                    for p in range(2)]
                Lrun = p3b.enter_context(nc.sbuf_tensor("Lrun", [128, 2048], BF16))
                Lg = p3b.enter_context(nc.sbuf_tensor("Lg", [128, 2048], BF16))
                Sinb = p3b.enter_context(nc.sbuf_tensor("Sinb", [128, 2048], BF16))
                prevw = p3b.enter_context(nc.sbuf_tensor("prevw", [128, 2048], BF16))
                pzbc = p3b.enter_context(nc.sbuf_tensor("pzbc", [128, 32], F32))
                wop = [p3b.enter_context(nc.sbuf_tensor(f"wop{c}", [128, 128], BF16))
                       for c in range(4)]
                sumst = p3b.enter_context(nc.sbuf_tensor("sumst", [128, 64], F32))
                scrow = p3b.enter_context(nc.sbuf_tensor("scrow", [1, 512], F32))
                s1b = [p3b.enter_context(nc.sbuf_tensor(f"s1b{c}", [128, 512], BF16))
                       for c in range(2)]
                dagT = p3b.enter_context(nc.sbuf_tensor("dagT", [8, 32], F32))
                wqr = p3b.enter_context(nc.sbuf_tensor("wqr", [8, 32], F32))

                def _an1(s_):
                    return s_.activation(aneg_sb[:], alog_sb[:], AF.Exp)
                ctx.sstep(("dma",), _an1)
                def _an2(v):
                    return v.tensor_scalar(aneg_sb[:], aneg_sb[:], -1.0, None,
                                           ALU.mult)
                ctx.vstep(("s",), _an2)
                def _dr1(t):
                    return t.transpose(P3["m1"][0:1, 0:32], dcap_sb[:],
                                       idf[0:32, 0:32])
                ctx.tstep(("dma", "v"), _dr1)
                def _dr2(v):
                    return v.tensor_copy(arow[:], P3["m1"][0:1, 0:32])
                ctx.vstep(("t",), _dr2)
                def _dr3(t):
                    return t.matmul(P3["m1"][:, 0:32], onesf[0:1, 0:128], arow[:],
                                    start=True, stop=True)
                ctx.tstep(("v",), _dr3)
                def _dr4(v):
                    return v.tensor_copy(drep[:], P3["m1"][:, 0:32])
                ctx.vstep(("t",), _dr4)

                def emit_chunk(ti, ci, mode):
                    qb, tloc = ti // 2, ti % 2
                    z = 2 * tloc + ci
                    c0 = ci * 256
                    if mode == 1 and z >= 1:
                        slot = qb * 3 + (z - 1)
                        def _lzs(sync, dma, slot=slot):
                            dma(out=lz_d[:, slot * 2048:(slot + 1) * 2048],
                                in_=Lrun[:])
                        ctx.dstep(("v",), _lzs)
                    if mode == 2 and z >= 1:
                        slot = qb * 3 + (z - 1)
                        def _lzl(sync, dma, slot=slot):
                            dma(out=Lg[:],
                                in_=lz_d[:, slot * 2048:(slot + 1) * 2048])
                        ctx.dstep(("v",), _lzl)
                    for lc in range(2):
                        def _ct(t, lc=lc, c0=c0):
                            return t.transpose(
                                P3["m1"][0:128, lc * 32:(lc + 1) * 32],
                                csf[0:32, c0 + lc * 128:c0 + (lc + 1) * 128],
                                idf[0:32, 0:32])
                        ctx.tstep(("v", "s"), _ct)
                    def _cte1(v):
                        return v.tensor_copy(cstm[:], P3["m1"][0:128, 0:64])
                    ctx.vstep(("t",), _cte1)
                    def _cte2(v):
                        return v.tensor_scalar(ncstm[:], P3["m1"][0:128, 0:64],
                                               -1.0, None, ALU.mult)
                    ctx.vstep((), _cte2)
                    if mode == 2 and _os3.environ.get("K3X", "") != "noetm":
                        def _etm(s_):
                            return s_.activation(exptm[:], cstm[:], AF.Exp)
                        ctx.sstep(("v",), _etm)
                    def _dec(s_, c0=c0):
                        return s_.activation(
                            decf[:], csf[:, c0:c0 + 256], AF.Exp, scale=-1.0,
                            bias=csf[:, c0 + 255:c0 + 256])
                    ctx.sstep(("v",), _dec)
                    def _dd(v, c0=c0):
                        return v.tensor_mul(ddf[:], dt_sb[:, c0:c0 + 256],
                                            decf[:])
                    ctx.vstep(("s",), _dd)
                    for lc in range(2):
                        def _dt1(t, lc=lc):
                            return t.transpose(
                                P3["m1"][0:128, 64 + lc * 32:64 + (lc + 1) * 32],
                                ddf[0:32, lc * 128:(lc + 1) * 128],
                                idf[0:32, 0:32])
                        ctx.tstep(("v", "s"), _dt1)
                    def _dte(v):
                        return v.tensor_copy(ddtm[:], P3["m1"][0:128, 64:128])
                    ctx.vstep(("t",), _dte)
                    if mode == 2:
                        for lc in range(2):
                            def _dt2(t, lc=lc, c0=c0):
                                return t.transpose(
                                    P3["m1"][0:128, lc * 32:(lc + 1) * 32],
                                    dt_sb[0:32, c0 + lc * 128:c0 + (lc + 1) * 128],
                                    idf[0:32, 0:32])
                            ctx.tstep(("v", "s"), _dt2)
                        def _dte2(v):
                            return v.tensor_copy(dttm[:], P3["m1"][0:128, 0:64])
                        ctx.vstep(("t",), _dte2)
                    for lc in range(2):
                        def _bt(t, lc=lc, c0=c0):
                            return t.transpose(
                                PSB3[:, lc * 128:(lc + 1) * 128],
                                xc[16][:, 4 + c0 + lc * 128:4 + c0 + (lc + 1) * 128],
                                idb[:])
                        ctx.tstep(("v", "s"), _bt)
                    def _bte(v):
                        return v.tensor_copy(bmt[:], PSB3[:, 0:256])
                    ctx.vstep(("t",), _bte)
                    if mode == 1:
                        def _aex(s_, c0=c0):
                            return s_.activation(dtw[:, 0:1],
                                                 csf[:, c0 + 255:c0 + 256],
                                                 AF.Exp)
                        ctx.sstep(("v",), _aex)
                        def _atr(t):
                            return t.transpose(P3["m1"][0:1, 0:32],
                                               dtw[0:32, 0:1], idf[0:32, 0:32])
                        ctx.tstep(("s",), _atr)
                        def _are(v):
                            return v.tensor_copy(arow[:], P3["m1"][0:1, 0:32])
                        ctx.vstep(("t",), _are)
                        def _abc(t):
                            return t.matmul(P3["m1"][:, 0:32], onesf[0:1, 0:128],
                                            arow[:], start=True, stop=True)
                        ctx.tstep(("v",), _abc)
                        def _abe(v):
                            return v.tensor_copy(abc[:], P3["m1"][:, 0:32])
                        ctx.vstep(("t",), _abe)
                    if mode == 2:
                        _skip_cb = _os3.environ.get("K3X", "") == "nocb"
                        for sc in range(2 if not _skip_cb else 0):
                            def _cb(t, sc=sc, c0=c0):
                                return t.matmul(
                                    P3["m0"][:, sc * 256:(sc + 1) * 256],
                                    xc[16][:, 4 + c0 + sc * 128:4 + c0 + (sc + 1) * 128],
                                    xc[17][:, 4 + c0:4 + c0 + 256],
                                    start=True, stop=True)
                            ctx.tstep(("v", "s"), _cb)
                        if not _skip_cb:
                            def _cbe(v):
                                return v.tensor_copy(cbts[:], P3["m0"][:, 0:512])
                            ctx.vstep(("t",), _cbe)
                        Sin = Sinb
                        if _os3.environ.get("K3X", "") == "nopv":
                            pass
                        elif z == 0:
                            def _pv0(v, Sin=Sin):
                                return v.tensor_copy(prevw[:], Sin[:])
                            ctx.vstep(("cc",), _pv0)
                        else:
                            slot = qb * 3 + (z - 1)
                            def _pz1(t, qb=qb, z=z):
                                return t.transpose(
                                    P3["m1"][0:1, 0:32],
                                    pzx[0:32, qb * 4 + z:qb * 4 + z + 1],
                                    idf[0:32, 0:32])
                            ctx.tstep(("v", "s"), _pz1)
                            def _pz2(v):
                                return v.tensor_copy(arow[:], P3["m1"][0:1, 0:32])
                            ctx.vstep(("t",), _pz2)
                            def _pz3(t):
                                return t.matmul(P3["m1"][:, 0:32],
                                                onesf[0:1, 0:128], arow[:],
                                                start=True, stop=True)
                            ctx.tstep(("v",), _pz3)
                            def _pz4(v):
                                return v.tensor_copy(pzbc[:], P3["m1"][:, 0:32])
                            ctx.vstep(("t",), _pz4)
                            for h in range(NHEADS):
                                def _pv(v, h=h, Sin=Sin):
                                    return v.scalar_tensor_tensor(
                                        prevw[:, h * 64:(h + 1) * 64],
                                        Sin[:, h * 64:(h + 1) * 64],
                                        pzbc[:, h:h + 1],
                                        Lg[:, h * 64:(h + 1) * 64],
                                        ALU.mult, ALU.add)
                                ctx.vstep(("dma",), _pv)

                    if mode == 1:
                        for hp in range(16):
                            for lc in range(2):
                                def _xt(t, hp=hp, lc=lc, c0=c0):
                                    return t.transpose(
                                        PSB3[:, 256 + lc * 128:256 + (lc + 1) * 128],
                                        xc[hp][:, 4 + c0 + lc * 128:4 + c0 + (lc + 1) * 128],
                                        idb[:])
                                ctx.tstep(("v", "s", "g"), _xt)
                            def _xte(v, hp=hp):
                                return v.tensor_copy(xstm[0][:], PSB3[:, 256:512])
                            ctx.vstep(("t",), _xte)
                            for sub in range(2):
                                h = 2 * hp + sub
                                def _xd(g, sub=sub, h=h):
                                    last = None
                                    for lc in range(2):
                                        last = g.tensor_scalar(
                                            xddbuf[:, lc * 64:(lc + 1) * 64],
                                            xstm[0][:, lc * 128 + sub * 64:
                                                 lc * 128 + sub * 64 + 64],
                                            ddtm[:, lc * 32 + h:lc * 32 + h + 1],
                                            None, ALU.mult)
                                    return last
                                ctx.vstep(("t",), _xd)
                                def _st(t, h=h):
                                    last = None
                                    for lc in range(2):
                                        last = t.matmul(
                                            P3["b1"][0:128, 0:64],
                                            bmt[:, lc * 128:(lc + 1) * 128],
                                            xddbuf[:, lc * 64:(lc + 1) * 64],
                                            start=(lc == 0), stop=(lc == 1))
                                    return last
                                ctx.tstep(("g", "v"), _st)
                                if z == 0:
                                    def _l0(v, h=h):
                                        return v.tensor_copy(
                                            Lrun[:, h * 64:(h + 1) * 64],
                                            P3["b1"][0:128, 0:64])
                                    ctx.vstep(("t", "dma"), _l0)
                                else:
                                    def _lz(v, h=h):
                                        return v.scalar_tensor_tensor(
                                            Lrun[:, h * 64:(h + 1) * 64],
                                            Lrun[:, h * 64:(h + 1) * 64],
                                            abc[:, h:h + 1],
                                            P3["b1"][0:128, 0:64],
                                            ALU.mult, ALU.add)
                                    ctx.vstep(("t", "dma"), _lz)
                    else:
                        def stages_for(h, c0=c0):
                            p = h % 2
                            hp = h // 2
                            sub = h % 2
                            out = []
                            if sub == 0:
                                for lc in range(2):
                                    def _xt(t, hp=hp, lc=lc, c0=c0):
                                        return t.transpose(
                                            PSB3[:, 256 + lc * 128:
                                                 256 + (lc + 1) * 128],
                                            xc[hp][:, 4 + c0 + lc * 128:
                                                   4 + c0 + (lc + 1) * 128],
                                            idb[:])
                                    out.append(("tensor", ("v", "s", "g"), _xt))
                                def _xte(v, hp=hp):
                                    return v.tensor_copy(xstm[hp % 2][:],
                                                         PSB3[:, 256:512])
                                out.append(("vector", ("t",), _xte))
                            PB = P3["b0"] if p == 0 else P3["m0"]
                            PY = P3["b1"] if p == 0 else P3["m1"]
                            def _xd2(g, h=h, hp=hp, sub=sub, p=p):
                                last = None
                                for lc in range(2):
                                    last = g.tensor_scalar(
                                        xdbuf[p][:, lc * 64:(lc + 1) * 64],
                                        xstm[hp % 2][:, lc * 128 + sub * 64:
                                             lc * 128 + sub * 64 + 64],
                                        dttm[:, lc * 32 + h:lc * 32 + h + 1],
                                        None, ALU.mult)
                                return last
                            out.append(("vector", ("t",), _xd2))
                            for lh in range(2):
                                def _cw1(g, lh=lh, h=h, p=p):
                                    return g.tensor_scalar(
                                        colw[p][:], onesq[:],
                                        cstm[:, lh * 32 + h:lh * 32 + h + 1],
                                        None, ALU.mult)
                                out.append(("vector", ("t",), _cw1))
                                def _cwt(t, lh=lh, p=p, PB=PB):
                                    return t.transpose(
                                        PB[:, lh * 128:(lh + 1) * 128],
                                        colw[p][:], idf[:])
                                out.append(("tensor", ("v",), _cwt))
                                def _cw2(g, lh=lh, h=h, p=p):
                                    return g.tensor_scalar(
                                        colw[p][:], onesq[:],
                                        exptm[:, lh * 32 + h:lh * 32 + h + 1],
                                        None, ALU.mult)
                                out.append(("vector", ("t",), _cw2))
                                def _cwt2(t, lh=lh, p=p, PB=PB):
                                    return t.transpose(
                                        PB[:, 256 + lh * 128:
                                           256 + (lh + 1) * 128],
                                        colw[p][:], idf[:])
                                out.append(("tensor", ("v",), _cwt2))
                            for sc in range(2):
                                def _mk(v, sc=sc, p=p, PB=PB):
                                    return v.tensor_add(
                                        mseg[p][:, sc * 256:(sc + 1) * 256],
                                        PB[:, 0:256],
                                        trin_sb[:, sc * 256:(sc + 1) * 256])
                                out.append(("vector", ("t",), _mk))
                                def _ex(s_, sc=sc, h=h, p=p):
                                    return s_.activation(
                                        expsg[p][:, sc * 256:(sc + 1) * 256],
                                        mseg[p][:, sc * 256:(sc + 1) * 256],
                                        AF.Exp,
                                        bias=ncstm[:, sc * 32 + h:
                                                   sc * 32 + h + 1])
                                out.append(("scalar", ("v",), _ex))
                                def _mm(g, sc=sc, p=p):
                                    return g.tensor_mul(
                                        msM[p][:, sc * 256:(sc + 1) * 256],
                                        expsg[p][:, sc * 256:(sc + 1) * 256],
                                        cbts[:, sc * 256:(sc + 1) * 256])
                                out.append(("gpsimd", ("s", "v"), _mm))
                            def _cm(v, c0=c0, p=p, PB=PB):
                                return v.tensor_mul(
                                    cms[p][:], xc[17][:, 4 + c0:4 + c0 + 256],
                                    PB[:, 256:512])
                            out.append(("vector", ("t",), _cm))
                            def _ym(t, h=h, p=p, PY=PY):
                                t.matmul(PY[0:64, 0:256],
                                         xdbuf[p][:, 0:64], msM[p][:, 0:256],
                                         start=True, stop=False)
                                t.matmul(PY[0:64, 0:256],
                                         xdbuf[p][:, 64:128], msM[p][:, 256:512],
                                         start=False, stop=False)
                                return t.matmul(
                                    PY[0:64, 0:256],
                                    prevw[:, h * 64:(h + 1) * 64], cms[p][:],
                                    start=False, stop=True)
                            out.append(("tensor", ("g", "v"), _ym))
                            par = sub * 64
                            def _ye1(v, hp=hp, h=h, c0=c0, par=par, p=p, PY=PY):
                                return v.scalar_tensor_tensor(
                                    tybuf[p][par:par + 64, :],
                                    xc[hp][par:par + 64, 4 + c0:4 + c0 + 256],
                                    drep[par:par + 64, h:h + 1],
                                    PY[0:64, 0:256],
                                    ALU.mult, ALU.add)
                            out.append(("vector", ("t",), _ye1))
                            def _ye2(v, hp=hp, c0=c0, par=par, p=p):
                                return v.tensor_mul(
                                    zs[hp][par:par + 64, c0:c0 + 256],
                                    tybuf[p][par:par + 64, :],
                                    zs[hp][par:par + 64, c0:c0 + 256])
                            out.append(("vector", (), _ye2))
                            return out

                        for h0 in range(0, NHEADS, 2):
                            sa = stages_for(h0)
                            sb = [None] * 3 + stages_for(h0 + 1)
                            n = max(len(sa), len(sb))
                            for k in range(n):
                                if k < len(sa):
                                    eng, deps, fn = sa[k]
                                    ctx.step(eng, deps, fn,
                                             ctx._OWN.get(eng, "v"))
                                if k < len(sb) and sb[k] is not None:
                                    eng, deps, fn = sb[k]
                                    ctx.step(eng, deps, fn,
                                             ctx._OWN.get(eng, "v"))
                    if mode == 1 and z == 3:
                        def _lst(sync, dma, qb=qb):
                            dma(out=cc_s_in[:, qb * 2048:(qb + 1) * 2048],
                                in_=Lrun[:])
                        ctx.dstep(("v",), _lst)

                def emit_tile_post(ti):
                    qb, tloc = ti // 2, ti % 2
                    t0 = ti * 512
                    for fc in range(16):
                        def _ysq(s_, fc=fc):
                            return s_.activation(sq[:, 0:512], zs[fc][:],
                                                 AF.Square)
                        ctx.sstep(("t", "v", "dma"), _ysq)
                        def _ymm(t, fc=fc):
                            return t.matmul(P3["row"][0:1, 0:512], onecol[:],
                                            sq[:, 0:512],
                                            start=(fc == 0), stop=(fc == 15))
                        ctx.tstep(("s",), _ymm)
                    def _yst(v):
                        return v.tensor_scalar(strow[0:1, 0:512],
                                               P3["row"][0:1, 0:512],
                                               1.0 / D_INNER, None, ALU.mult)
                    ctx.vstep(("t",), _yst)
                    epsc4 = ctx.constcell(EPS_LN)
                    def _ysr(s_):
                        return s_.activation(strow[0:1, 0:512],
                                             strow[0:1, 0:512], AF.Sqrt,
                                             bias=epsc4)
                    ctx.sstep(("v",), _ysr)
                    def _yrc(v):
                        return v.reciprocal(strow[0:1, 512:1024],
                                            strow[0:1, 0:512])
                    ctx.vstep(("s",), _yrc)
                    def _ybc(t):
                        return t.matmul(P3["b0"][:, 0:512], onesf[0:1, 0:128],
                                        strow[0:1, 512:1024],
                                        start=True, stop=True)
                    ctx.tstep(("v",), _ybc)
                    for fc in range(16):
                        def _yap(v, fc=fc):
                            return v.scalar_tensor_tensor(
                                zs[fc][:], zs[fc][:], nw2_sb[:, fc:fc + 1],
                                P3["b0"][:, 0:512], ALU.mult, ALU.mult)
                        ctx.vstep(("t",), _yap)
                    for oc in range(8):
                        for kg in range(4):
                            def _low(sync, dma, oc=oc, kg=kg):
                                for i in range(4):
                                    kc = kg * 4 + i
                                    dma(out=wop[i][:],
                                        in_=ab_outproj[kc * 128:(kc + 1) * 128,
                                                       oc * 128:(oc + 1) * 128])
                            ctx.gdma(("t",), _low)
                            def _om(t, kg=kg):
                                last = None
                                for i in range(4):
                                    last = t.matmul(
                                        P3["m0"][:, 0:512], wop[i][:],
                                        zs[kg * 4 + i][:, 0:512],
                                        start=(kg == 0 and i == 0),
                                        stop=(kg == 3 and i == 3))
                                return last
                            ctx.tstep(("v", "dma", "g"), _om)
                        def _oe(v, oc=oc):
                            return v.tensor_copy(y_tl[oc][:],
                                                 P3["m0"][:, 0:512])
                        ctx.vstep(("t", "dma"), _oe)
                        def _ost(sync, dma, oc=oc, t0=t0):
                            dma(out=y_d[oc * 128:(oc + 1) * 128, t0:t0 + 512],
                                in_=y_tl[oc][:])
                        ctx.dstep(("v",), _ost)
                    for c2 in range(2):
                        def _s1(t, c2=c2, t0=t0):
                            last = None
                            for dc in range(8):
                                last = t.matmul(
                                    P3["m1"][:, 0:512],
                                    wsc1[dc][:, c2 * 128:(c2 + 1) * 128],
                                    y_tl[dc][:],
                                    start=(dc == 0), stop=(dc == 7))
                            return last
                        ctx.tstep(("v",), _s1)
                        def _s1e(v, c2=c2):
                            return v.tensor_scalar(s1b[c2][:], P3["m1"][:, 0:512],
                                                   1.0, 0.0, ALU.mult, ALU.max)
                        ctx.vstep(("t",), _s1e)
                    def _s2m(t):
                        last = None
                        for c2 in range(2):
                            last = t.matmul(P3["row"][0:1, 0:512],
                                            wsc2[:, c2:c2 + 1], s1b[c2][:],
                                            start=(c2 == 0), stop=(c2 == 1))
                        return last
                    ctx.tstep(("v",), _s2m)
                    def _s2e(s_):
                        return s_.activation(scrow[:], P3["row"][0:1, 0:512],
                                             AF.Sigmoid)
                    ctx.sstep(("t", "dma"), _s2e)
                    def _s2s(sync, dma, t0=t0):
                        dma(out=cc_p_in[64:65, t0:t0 + 512], in_=scrow[:])
                    ctx.dstep(("s",), _s2s)
                    def _sf(t, t0=t0):
                        last = None
                        for dc in range(8):
                            last = t.matmul(P3["m1"][0:64, 0:512], wsumm[dc][:],
                                            y_tl[dc][:],
                                            start=(dc == 0), stop=(dc == 7))
                        return last
                    ctx.tstep(("v",), _sf)
                    def _sfe2(v):
                        return v.tensor_copy(sq[0:64, 0:512],
                                             P3["m1"][0:64, 0:512])
                    ctx.vstep(("t", "dma"), _sfe2)
                    def _sfs(sync, dma, t0=t0):
                        dma(out=cc_p_in[0:64, t0:t0 + 512], in_=sq[0:64, 0:512])
                    ctx.dstep(("v",), _sfs)
                    for tc in range(4):
                        def _stm(t, tc=tc, t0=t0):
                            last = None
                            for dc in range(8):
                                last = t.matmul(
                                    P3["m1"][:, 0:64],
                                    y_tl[dc][:, tc * 128:(tc + 1) * 128],
                                    wsumm[dc][:],
                                    start=(dc == 0), stop=(dc == 7))
                            return last
                        ctx.tstep(("v", "dma"), _stm)
                        def _ste(v):
                            return v.tensor_copy(sumst[:], P3["m1"][:, 0:64])
                        ctx.vstep(("t", "dma"), _ste)
                        def _sts(sync, dma, tc=tc, t0=t0):
                            dma(out=cc_q_in[t0 + tc * 128:t0 + (tc + 1) * 128, :],
                                in_=sumst[:])
                        ctx.dstep(("v",), _sts)
                    def _qm(t, t0=t0):
                        last = None
                        for dc in range(8):
                            last = t.matmul(P3["m1"][0:64, 0:512], wqw[dc][:],
                                            y_tl[dc][:],
                                            start=(dc == 0), stop=(dc == 7))
                        return last
                    ctx.tstep(("v", "dma"), _qm)
                    def _qe(v, t0=t0):
                        return v.tensor_copy(qret[:, t0:t0 + 512],
                                             P3["m1"][0:64, 0:512])
                    ctx.vstep(("t",), _qe)

                def emit_tile(ti, mode):
                    qb, tloc = ti // 2, ti % 2
                    t0 = ti * 512
                    LNP = {"row": P3["row"], "b0": P3["b0"], "b1": P3["b1"]}
                    emit_layernorm(ctx, LNP, X, xn, t0, t0 + 512)
                    for dc in range(8):
                        def _rsq(s_, dc=dc):
                            return s_.activation(sq[:, 0:512], xn[dc][:],
                                                 AF.Square)
                        ctx.sstep(("t", "v", "dma"), _rsq)
                        def _rmm(t, dc=dc):
                            return t.matmul(P3["row"][0:1, 0:512], onecol[:],
                                            sq[:, 0:512],
                                            start=(dc == 0), stop=(dc == 7))
                        ctx.tstep(("s",), _rmm)
                    def _rst(v):
                        return v.tensor_scalar(strow[0:1, 0:512],
                                               P3["row"][0:1, 0:512],
                                               1.0 / D, None, ALU.mult)
                    ctx.vstep(("t",), _rst)
                    epsc3 = ctx.constcell(EPS_LN)
                    def _rsqr(s_):
                        return s_.activation(strow[0:1, 0:512], strow[0:1, 0:512],
                                             AF.Sqrt, bias=epsc3)
                    ctx.sstep(("v",), _rsqr)
                    def _rrc(v):
                        return v.reciprocal(strow[0:1, 512:1024],
                                            strow[0:1, 0:512])
                    ctx.vstep(("s",), _rrc)
                    def _rbc(t):
                        return t.matmul(P3["b0"][:, 0:512], onesf[0:1, 0:128],
                                        strow[0:1, 512:1024],
                                        start=True, stop=True)
                    ctx.tstep(("v",), _rbc)
                    for dc in range(8):
                        def _rap(v, dc=dc):
                            return v.scalar_tensor_tensor(
                                xn[dc][:], xn[dc][:], nw_sb[:, dc:dc + 1],
                                P3["b0"][:, 0:512], ALU.mult, ALU.mult)
                        ctx.vstep(("t",), _rap)

                    if KD3 == "p1ln" and mode == 1:
                        return
                    blocks = [(2048, 512), (2560, 512), (3072, 512),
                              (3584, 512), (4096, 288)]
                    for bs, bw in blocks:
                        def _lwip(sync, dma, bs=bs, bw=bw):
                            for dc in range(8):
                                dma(out=wip[dc][:, 0:bw],
                                    in_=ab_inproj[dc * 128:(dc + 1) * 128,
                                                  bs:bs + bw])
                        ctx.gdma(("t",), _lwip)
                        nout = bw // 128
                        for k in range(nout):
                            col = bs + k * 128
                            ps = P3["m0"] if k % 2 == 0 else P3["m1"]
                            def _pm(t, k=k, ps=ps):
                                last = None
                                for dc in range(8):
                                    last = t.matmul(
                                        ps[:, 0:512],
                                        wip[dc][:, k * 128:(k + 1) * 128],
                                        xn[dc][:, 0:512],
                                        start=(dc == 0), stop=(dc == 7))
                                return last
                            ctx.tstep(("v", "dma"), _pm)
                            c = (col - 2048) // 128
                            def _pe(v, c=c, ps=ps):
                                return v.tensor_copy(xc[c][:, 4:516],
                                                     ps[:, 0:512])
                            ctx.vstep(("t",), _pe)
                            if tloc == 0:
                                def _phm(t, k=k, qb=qb):
                                    last = None
                                    for dc in range(8):
                                        last = t.matmul(
                                            P3["b1"][:, 0:3],
                                            wip[dc][:, k * 128:(k + 1) * 128],
                                            xnh[dc][:, qb * 3:qb * 3 + 3],
                                            start=(dc == 0), stop=(dc == 7))
                                    return last
                                ctx.tstep(("v",), _phm)
                                def _phe(v, c=c, qb=qb):
                                    return v.tensor_scalar(
                                        xc[c][:, 1:4], P3["b1"][:, 0:3],
                                        hmask[:, qb:qb + 1], None, ALU.mult)
                                ctx.vstep(("t",), _phe)
                            else:
                                def _phs(v, c=c):
                                    return v.tensor_copy(
                                        xc[c][:, 1:4],
                                        stash[:, c * 3:(c + 1) * 3])
                                ctx.vstep(("t",), _phs)
                        if bw == 288:
                            def _pdt(t):
                                last = None
                                for dc in range(8):
                                    last = t.matmul(
                                        P3["m0"][0:32, 0:512],
                                        wip[dc][:, 256:288],
                                        xn[dc][:, 0:512],
                                        start=(dc == 0), stop=(dc == 7))
                                return last
                            ctx.tstep(("v", "dma"), _pdt)
                            def _pde(v):
                                return v.tensor_copy(dtraw[:],
                                                     P3["m0"][0:32, 0:512])
                            ctx.vstep(("t",), _pde)

                    if KD3 == "p1ip" and mode == 1:
                        return
                    if tloc == 0:
                        def _stash(v):
                            last = None
                            for c in range(18):
                                last = v.tensor_copy(stash[:, c * 3:(c + 1) * 3],
                                                     xc[c][:, 513:516])
                            return last
                        ctx.vstep(("t",), _stash)

                    for c in range(18):
                        eng = "vector"
                        acc = accD
                        def _c0(e, c=c, acc=acc):
                            return e.tensor_scalar(acc[:, 0:512], xc[c][:, 1:513],
                                                   cw_sb[:, c * 4:c * 4 + 1],
                                                   None, ALU.mult)
                        ctx.step(eng, ("v", "t", "g", "s"), _c0, "v")
                        for k in range(1, 4):
                            def _ck(e, c=c, k=k, acc=acc):
                                return e.scalar_tensor_tensor(
                                    acc[:, 0:512], xc[c][:, 1 + k:513 + k],
                                    cw_sb[:, c * 4 + k:c * 4 + k + 1],
                                    acc[:, 0:512], ALU.mult, ALU.add)
                            ctx.step(eng, (), _ck, "v")
                        def _cs(s_, c=c, acc=acc):
                            return s_.activation(xc[c][:, 4:516], acc[:, 0:512],
                                                 AF.Silu, bias=cb_sb[:, c:c + 1])
                        ctx.sstep(("v", "g"), _cs)

                    if KD3 == "p1cv" and mode == 1:
                        return
                    def _d1(s_):
                        return s_.activation(dtw[:, 0:512], dtraw[:], AF.Abs,
                                             bias=dtb_sb[:])
                    ctx.sstep(("v",), _d1)
                    def _d2(s_):
                        return s_.activation(sq[0:32, 0:512], dtw[:, 0:512],
                                             AF.Exp, scale=-1.0)
                    ctx.sstep(("v",), _d2)
                    def _d3(s_):
                        return s_.activation(dtw[:, 0:512], sq[0:32, 0:512],
                                             AF.Ln, bias=1.0)
                    ctx.sstep((), _d3)
                    def _d4(v):
                        return v.tensor_scalar(dtraw[:], dtraw[:],
                                               dtb_sb[:], 0.0, ALU.add, ALU.max)
                    ctx.vstep(("s",), _d4)
                    def _d5(v):
                        return v.tensor_add(dt_sb[:], dtw[:, 0:512], dtraw[:])
                    ctx.vstep(("s",), _d5)
                    def _d6(v):
                        return v.tensor_scalar(dtraw[:], dt_sb[:], aneg_sb[:],
                                               None, ALU.mult)
                    ctx.vstep((), _d6)
                    for ci in range(2):
                        z = 2 * tloc + ci
                        def _scn(v, ci=ci):
                            return v.tensor_tensor_scan(
                                csf[:, ci * 256:(ci + 1) * 256], ones32[:],
                                dtraw[:, ci * 256:(ci + 1) * 256], 0.0,
                                ALU.mult, ALU.add)
                        ctx.vstep((), _scn)
                        def _ds(v, ci=ci, z=z, qb=qb):
                            return v.tensor_copy(
                                dssum[:, qb * 4 + z:qb * 4 + z + 1],
                                csf[:, ci * 256 + 255:ci * 256 + 256])
                        ctx.vstep((), _ds)

                    if mode == 2:
                        for zb in range(4):
                            def _lwz(sync, dma, zb=zb):
                                for dc in range(8):
                                    dma(out=wip[dc][:, 0:512],
                                        in_=ab_inproj[dc * 128:(dc + 1) * 128,
                                                      zb * 512:(zb + 1) * 512])
                            ctx.gdma(("t",), _lwz)
                            for k in range(4):
                                fc = zb * 4 + k
                                ps = P3["m0"] if k % 2 == 0 else P3["m1"]
                                def _zm(t, k=k, ps=ps):
                                    last = None
                                    for dc in range(8):
                                        last = t.matmul(
                                            ps[:, 0:512],
                                            wip[dc][:, k * 128:(k + 1) * 128],
                                            xn[dc][:, 0:512],
                                            start=(dc == 0), stop=(dc == 7))
                                    return last
                                ctx.tstep(("v", "s", "dma"), _zm)
                                def _ze(s_, fc=fc, ps=ps):
                                    return s_.activation(zs[fc][:], ps[:, 0:512],
                                                         AF.Silu)
                                ctx.sstep(("t",), _ze)

                    if KD3 == "p2z" and mode == 2:
                        return
                    emit_chunk(ti, 0, mode)
                    if KD3 == "p2c0" and mode == 2:
                        return
                    emit_chunk(ti, 1, mode)
                    if mode == 2:
                        if KD3 == "p2cc" and mode == 2:
                            return
                        emit_tile_post(ti)

                _sc_p3a = nc.enter_named_scope("p3_pass1", False)[0]
                if KD3 == "p1t0":
                    emit_tile(0, 1)
                    emit_store_out(ctx, X, out_fm)
                    return nc
                for ti in range(4):
                    emit_tile(ti, 1)
                if KD3 == "p1":
                    emit_store_out(ctx, X, out_fm)
                    return nc
                def _pzz(v):
                    v.memset(pzx[:, 0:1], 0.0)
                    return v.memset(pzx[:, 4:5], 0.0)
                ctx.vstep(("v",), _pzz)
                for qb in range(2):
                    for z2 in range(1, 4):
                        def _pzc(v, qb=qb, z2=z2):
                            return v.tensor_add(
                                pzx[:, qb * 4 + z2:qb * 4 + z2 + 1],
                                pzx[:, qb * 4 + z2 - 1:qb * 4 + z2],
                                dssum[:, qb * 4 + z2 - 1:qb * 4 + z2])
                        ctx.vstep((), _pzc)
                def _tot1(v):
                    v.tensor_add(dtw[:, 0:1], dssum[:, 0:1], dssum[:, 1:2])
                    v.tensor_add(dtw[:, 1:2], dssum[:, 2:3], dssum[:, 3:4])
                    v.tensor_add(dtw[:, 2:3], dssum[:, 4:5], dssum[:, 5:6])
                    return v.tensor_add(dtw[:, 3:4], dssum[:, 6:7], dssum[:, 7:8])
                ctx.vstep((), _tot1)
                def _tot2(v):
                    v.tensor_add(dt_sb[:, 0:1], dtw[:, 0:1], dtw[:, 1:2])
                    return v.tensor_add(dt_sb[:, 1:2], dtw[:, 2:3], dtw[:, 3:4])
                ctx.vstep((), _tot2)
                def _pze(s_):
                    return s_.activation(pzx[:], pzx[:], AF.Exp)
                ctx.sstep(("v",), _pze)
                def _dst(sync, dma):
                    dma(out=cc_d_in, in_=dt_sb[:, 0:2])
                ctx.dstep(("v",), _dst)

                nc.leave_named_scope("p3_pass1", _sc_p3a, False)
                _sc_p3c = nc.enter_named_scope("p3_sin", False)[0]
                def _cc3(g):
                    ctx.inc(g.collective_compute(
                        "AllGather", ALU.bypass, replica_groups=RG,
                        ins=[cc_s_in.opt()], outs=[cc_s_out.opt()]), "cc")
                    ctx.inc(g.collective_compute(
                        "AllGather", ALU.bypass, replica_groups=RG,
                        ins=[cc_d_in.opt()], outs=[cc_d_out.opt()]), "cc")
                    return None
                ctx.step("gpsimd", ("dma",), _cc3)

                dag = p3b.enter_context(nc.sbuf_tensor("dag", [32, 16], F32))
                ccm_sb = p3b.enter_context(nc.sbuf_tensor("ccm_sb", [8, 16], F32))
                ccb_sb = p3b.enter_context(nc.sbuf_tensor("ccb_sb", [8, 2], F32))
                def _ld3(sync, dma):
                    dma(out=dag[:].rearrange("h (s r) -> h s r", r=8),
                        in_=cc_d_out.rearrange("(r h) s -> h s r", r=8),
                        allow_slow_non_contiguous=True)
                    dma(out=ccm_sb[:], in_=cc_mats)
                    dma(out=ccb_sb[:], in_=cc_bias)
                ctx.dstep(("cc",), _ld3)

                def emit_sin(qb):
                    def _sz(v):
                        return v.memset(Sinb[:], 0.0)
                    ctx.vstep(("t",), _sz)
                    def _tg(t, qb=qb):
                        return t.transpose(P3["m1"][0:8, 0:32],
                                           dag[:, qb * 8:(qb + 1) * 8],
                                           idf[0:32, 0:32])
                    ctx.tstep(("v", "dma"), _tg)
                    def _tge(v):
                        return v.tensor_copy(dagT[:], P3["m1"][0:8, 0:32])
                    ctx.vstep(("t",), _tge)
                    def _em(t, qb=qb):
                        return t.matmul(P3["m1"][0:8, 0:32],
                                        ccm_sb[:, qb * 8:(qb + 1) * 8],
                                        dagT[:], start=True, stop=True)
                    ctx.tstep(("v",), _em)
                    def _ee(s_, qb=qb):
                        return s_.activation(wqr[:], P3["m1"][0:8, 0:32],
                                             AF.Exp, bias=ccb_sb[:, qb:qb + 1])
                    ctx.sstep(("t",), _ee)
                    for r in range(8):
                        def _lg(sync, dma, r=r, qb=qb):
                            dma(out=Lg[:],
                                in_=cc_s_out[r * 128:(r + 1) * 128,
                                             qb * 2048:(qb + 1) * 2048])
                        ctx.dstep(("v",), _lg)
                        def _wrow(sync, dma, r=r):
                            dma(out=arow[:], in_=wqr[r:r + 1, 0:32])
                        ctx.dstep(("s", "v"), _wrow)
                        def _wbc(t):
                            return t.matmul(P3["m1"][:, 0:32], onesf[0:1, 0:128],
                                            arow[:], start=True, stop=True)
                        ctx.tstep(("dma",), _wbc)
                        def _wbe(v):
                            return v.tensor_copy(pzbc[:], P3["m1"][:, 0:32])
                        ctx.vstep(("t",), _wbe)
                        for h in range(NHEADS):
                            def _acc(v, h=h):
                                return v.scalar_tensor_tensor(
                                    Sinb[:, h * 64:(h + 1) * 64],
                                    Lg[:, h * 64:(h + 1) * 64],
                                    pzbc[:, h:h + 1],
                                    Sinb[:, h * 64:(h + 1) * 64],
                                    ALU.mult, ALU.add)
                            ctx.vstep(("dma",), _acc)

                if KD3 == "sin":
                    emit_sin(0)
                    emit_sin(1)
                    emit_store_out(ctx, X, out_fm)
                    return nc
                nc.leave_named_scope("p3_sin", _sc_p3c, False)
                _sc_p3d = nc.enter_named_scope("p3_pass2", False)[0]
                emit_sin(0)
                emit_tile(0, 2)
                if KD3 == "p2t0":
                    emit_store_out(ctx, X, out_fm)
                    return nc
                emit_tile(1, 2)
                emit_sin(1)
                emit_tile(2, 2)
                emit_tile(3, 2)


                nc.leave_named_scope("p3_pass2", _sc_p3d, False)
                ctx.barrier()
                p3b.close()

                def _cc4(g):
                    ctx.inc(g.collective_compute(
                        "AllGather", ALU.bypass, replica_groups=RG,
                        ins=[cc_p_in.opt()], outs=[cc_p_out.opt()]), "cc")
                    ctx.inc(g.collective_compute(
                        "AllGather", ALU.bypass, replica_groups=RG,
                        ins=[cc_q_in.opt()], outs=[cc_q_out.opt()]), "cc")
                    return None
                ctx.step("gpsimd", ("dma",), _cc4)

                _sc_p5 = nc.enter_named_scope("p5_pool", False)[0]
                S_raw = p3.enter_context(nc.sbuf_tensor("S_raw", [128, 128], F32))
                S_work = p3.enter_context(nc.sbuf_tensor("S_work", [128, 128], F32))
                smask = p3.enter_context(nc.sbuf_tensor("smask", [128, 128], F32))
                refio = p3.enter_context(nc.sbuf_tensor("refio", [128, 128], F32))
                gio = p3.enter_context(nc.sbuf_tensor("gio", [128, 128], F32))
                neg1t = p3.enter_context(nc.sbuf_tensor("neg1t", [128, 128], F32))
                prisel = p3.enter_context(nc.sbuf_tensor("prisel", [128, 128], F32))
                fsel = p3.enter_context(nc.sbuf_tensor("fsel", [128, 128], F32))
                sw1 = p3.enter_context(nc.sbuf_tensor("sw1", [128, 8], F32))
                swr = p3.enter_context(nc.sbuf_tensor("swr", [1, 8], F32))
                mbc = p3.enter_context(nc.sbuf_tensor("mbc", [128, 4], F32))
                offsrow = p3.enter_context(nc.sbuf_tensor("offsrow", [1, 64], F32))
                offs_u = p3.enter_context(nc.sbuf_tensor("offs_u", [POOL_SIZE, 1], U32))
                pool50 = p3.enter_context(nc.sbuf_tensor("pool50", [POOL_SIZE, 64], F32))
                poolT = p3.enter_context(nc.sbuf_tensor("poolT", [64, POOL_SIZE], BF16))
                kw_sb = p3.enter_context(nc.sbuf_tensor("kw_sb", [64, 64], BF16))
                vw_sb = p3.enter_context(nc.sbuf_tensor("vw_sb", [64, 1024], BF16))
                kkT = p3.enter_context(nc.sbuf_tensor("kkT", [64, POOL_SIZE], BF16))
                vvb = p3.enter_context(nc.sbuf_tensor("vvb", [POOL_SIZE, 1024], BF16))
                msk50 = p3.enter_context(nc.sbuf_tensor("msk50", [128, POOL_SIZE], F32))
                condc = p3.enter_context(nc.sbuf_tensor("condc", [128, 1], F32))
                sio = p3.enter_context(nc.sbuf_tensor("sio", [1, 64], F32))
                rcrow = p3.enter_context(nc.sbuf_tensor("rcrow", [1, 512], F32))
                rcbc = p3.enter_context(nc.sbuf_tensor("rcbc", [128, 512], F32))
                plT = p3.enter_context(nc.sbuf_tensor("plT", [POOL_SIZE, 512], BF16))
                pl = p3.enter_context(nc.sbuf_tensor("pl", [128, POOL_SIZE], BF16))
                rs50 = p3.enter_context(nc.sbuf_tensor("rs50", [128, 4], F32))
                gb = p3.enter_context(nc.sbuf_tensor("gb", [128, 512], BF16))
                tg = p3.enter_context(nc.sbuf_tensor("tg", [128, 512], BF16))
                rb = [p3.enter_context(nc.sbuf_tensor(f"rb{c}", [128, 512], BF16))
                      for c in range(8)]

                def _p5ld(sync, dma):
                    dma(out=refio[:], in_=ref_iota)
                    dma(out=gio[:], in_=g_iota)
                    dma(out=sio[:], in_=slot_iota)
                    for r in range(8):
                        dma(out=S_raw[16 * r:16 * r + 16, :],
                            in_=cc_p_out[65 * r + 64:65 * r + 65, :].rearrange(
                                "r (p f) -> (r p) f", p=16))
                ctx.dstep(("cc",), _p5ld)
                def _p5g(g, dma):
                    dma(out=kw_sb[:], in_=ab_k_w)
                    dma(out=vw_sb[:], in_=ab_v_w)
                ctx.gdma((), _p5g)
                def _p5c(v):
                    return v.memset(neg1t[:], -1.0)
                ctx.vstep((), _p5c)
                def _p5r(v):
                    return v.tensor_scalar(refio[:], refio[:], -8388608.0, None,
                                           ALU.add)
                ctx.vstep(("dma",), _p5r)
                def _p5g2(v):
                    return v.tensor_scalar(gio[:], gio[:], -8388608.0, None,
                                           ALU.add)
                ctx.vstep(("dma",), _p5g2)
                def _p5m(v):
                    return v.tensor_single_scalar(smask[:], S_raw[:], 0.5,
                                                  ALU.is_gt)
                ctx.vstep(("dma",), _p5m)
                def _p5w(v):
                    return v.scalar_tensor_tensor(S_work[:], S_raw[:], 1.0,
                                                  smask[:], ALU.add, ALU.mult)
                ctx.vstep((), _p5w)
                def _p5w2(v):
                    return v.tensor_scalar(S_work[:], S_work[:], -1.0, None,
                                           ALU.add)
                ctx.vstep((), _p5w2)
                def _cnt1(t):
                    t.matmul(P3["m1"][0:1, 0:128], onecol[:], smask[:],
                             start=True, stop=True)
                    return t.matmul(P3["m1"][0:1, 128:256], onecol[:], S_raw[:],
                                    start=True, stop=True)
                ctx.tstep(("v", "dma"), _cnt1)
                def _cnt2(v):
                    v.reduce_sum(swr[0:1, 0:1], P3["m1"][0:1, 0:128], axis=AX.X)
                    return v.reduce_sum(swr[0:1, 1:2], P3["m1"][0:1, 128:256],
                                        axis=AX.X)
                ctx.vstep(("t",), _cnt2)
                def _cnt3(v):
                    v.tensor_scalar(swr[0:1, 2:3], swr[0:1, 1:2], 1.0 / 16384.0,
                                    0.3, ALU.mult, ALU.is_gt)
                    return v.tensor_single_scalar(swr[0:1, 3:4], swr[0:1, 0:1],
                                                  0.5, ALU.is_gt)
                ctx.vstep((), _cnt3)
                def _cnt4(v):
                    return v.tensor_mul(swr[0:1, 4:5], swr[0:1, 2:3],
                                        swr[0:1, 3:4])
                ctx.vstep((), _cnt4)
                def _cnt5(t):
                    return t.matmul(P3["m1"][:, 0:1], onesf[0:1, 0:128],
                                    swr[0:1, 4:5], start=True, stop=True)
                ctx.tstep(("v",), _cnt5)
                def _cnt6(v):
                    return v.tensor_copy(condc[:], P3["m1"][:, 0:1])
                ctx.vstep(("t",), _cnt6)
                def _sm1(v):
                    return v.tensor_scalar(swr[0:1, 5:6], swr[0:1, 0:1], 50.0,
                                           None, ALU.min)
                ctx.vstep((), _sm1)
                def _sm2(v):
                    return v.tensor_scalar(offsrow[0:1, 50:114][0:1, 0:14]
                                           if False else swr[0:1, 6:7],
                                           swr[0:1, 5:6], 1.0, None, ALU.mult)
                ctx.vstep((), _sm2)
                def _sm3(v):
                    return v.tensor_scalar(msk50[0:1, 0:POOL_SIZE],
                                           sio[0:1, 0:POOL_SIZE],
                                           swr[0:1, 5:6], None, ALU.is_lt)
                ctx.vstep((), _sm3)
                def _sm4(v):
                    return v.tensor_scalar(msk50[0:1, 0:POOL_SIZE],
                                           msk50[0:1, 0:POOL_SIZE],
                                           4.0e9, -4.0e9, ALU.mult, ALU.add)
                ctx.vstep((), _sm4)
                def _sm5(t):
                    return t.matmul(P3["m1"][:, 0:POOL_SIZE], onesf[0:1, 0:128],
                                    msk50[0:1, 0:POOL_SIZE],
                                    start=True, stop=True)
                ctx.tstep(("v",), _sm5)
                def _sm6(v):
                    return v.tensor_copy(msk50[:], P3["m1"][:, 0:POOL_SIZE])
                ctx.vstep(("t",), _sm6)

                for it in range(POOL_SIZE):
                    def _i1(v):
                        return v.reduce_max(sw1[:, 0:1], S_work[:], axis=AX.X)
                    ctx.vstep(("t",), _i1)
                    def _i2(t):
                        return t.transpose(P3["m1"][0:1, 0:128], sw1[:, 0:1],
                                           idf[:])
                    ctx.tstep(("v",), _i2)
                    def _i3(v):
                        return v.reduce_max(swr[0:1, 7:8], P3["m1"][0:1, 0:128],
                                            axis=AX.X)
                    ctx.vstep(("t",), _i3)
                    def _i4(t):
                        return t.matmul(P3["m1"][:, 0:1], onesf[0:1, 0:128],
                                        swr[0:1, 7:8], start=True, stop=True)
                    ctx.tstep(("v",), _i4)
                    def _i5(v):
                        return v.tensor_copy(mbc[:, 0:1], P3["m1"][:, 0:1])
                    ctx.vstep(("t",), _i5)
                    def _i6(v):
                        return v.tensor_scalar(fsel[:], S_work[:], mbc[:, 0:1],
                                               None, ALU.is_ge)
                    ctx.vstep((), _i6)
                    def _i7(v):
                        return v.tensor_mul(prisel[:], fsel[:], refio[:])
                    ctx.vstep((), _i7)
                    def _i7b(v):
                        return v.tensor_scalar(prisel[:], prisel[:], 8388608.0,
                                               None, ALU.add)
                    ctx.vstep((), _i7b)
                    def _i8(v):
                        return v.tensor_reduce(sw1[:, 1:2], prisel[:], AX.X,
                                               ALU.min)
                    ctx.vstep((), _i8)
                    def _i9(t):
                        return t.transpose(P3["m1"][0:1, 0:128], sw1[:, 1:2],
                                           idf[:])
                    ctx.tstep(("v",), _i9)
                    def _i10(v):
                        return v.tensor_reduce(swr[0:1, 7:8],
                                               P3["m1"][0:1, 0:128], AX.X,
                                               ALU.min)
                    ctx.vstep(("t",), _i10)
                    def _i11(t):
                        return t.matmul(P3["m1"][:, 0:1], onesf[0:1, 0:128],
                                        swr[0:1, 7:8], start=True, stop=True)
                    ctx.tstep(("v",), _i11)
                    def _i12(v):
                        return v.tensor_copy(mbc[:, 1:2], P3["m1"][:, 0:1])
                    ctx.vstep(("t",), _i12)
                    def _i13(v):
                        return v.tensor_scalar(fsel[:], prisel[:], mbc[:, 1:2],
                                               None, ALU.is_equal)
                    ctx.vstep((), _i13)
                    def _i14(v):
                        return v.tensor_mul(prisel[:], fsel[:], gio[:])
                    ctx.vstep((), _i14)
                    def _i14b(v):
                        return v.tensor_scalar(prisel[:], prisel[:], 8388608.0,
                                               None, ALU.add)
                    ctx.vstep((), _i14b)
                    def _i15(v):
                        return v.tensor_reduce(sw1[:, 2:3], prisel[:], AX.X,
                                               ALU.min)
                    ctx.vstep((), _i15)
                    def _i16(t):
                        return t.transpose(P3["m1"][0:1, 0:128], sw1[:, 2:3],
                                           idf[:])
                    ctx.tstep(("v",), _i16)
                    def _i17(v, it=it):
                        return v.tensor_reduce(offsrow[0:1, it:it + 1],
                                               P3["m1"][0:1, 0:128], AX.X,
                                               ALU.min)
                    ctx.vstep(("t",), _i17)
                    def _i18(v):
                        return v.scalar_tensor_tensor(S_work[:], fsel[:],
                                                      -1.0e4, S_work[:],
                                                      ALU.mult, ALU.add)
                    ctx.vstep((), _i18)

                def _o1(t):
                    return t.transpose(P3["m1"][0:POOL_SIZE, 0:1],
                                       offsrow[0:1, 0:POOL_SIZE],
                                       idf[0:1, 0:1])
                ctx.tstep(("v",), _o1)
                def _o2(v):
                    return v.tensor_copy(offs_u[:], P3["m1"][0:POOL_SIZE, 0:1])
                ctx.vstep(("t",), _o2)
                def _o3(g):
                    i0 = g.indirect_dma_start(
                        out=pool50[:], out_offset=None, in_=cc_q_out,
                        in_offset=bass.IndirectOffsetOnAxis(ap=offs_u[:], axis=0))
                    i0.then_inc(ctx.sem["gd"], 16)
                    ctx.cnt["gd"] += 16
                    return None
                ctx.step("gpsimd", ("v", "cc"), _o3)
                def _o4(t):
                    return t.transpose(P3["m1"][0:64, 0:POOL_SIZE], pool50[:],
                                       idf[0:POOL_SIZE, 0:POOL_SIZE])
                ctx.tstep(("v", "dma"), _o4)
                def _o5(v):
                    return v.tensor_copy(poolT[:], P3["m1"][0:64, 0:POOL_SIZE])
                ctx.vstep(("t",), _o5)
                def _o6(t):
                    return t.matmul(P3["m1"][0:64, 0:POOL_SIZE], kw_sb[:],
                                    poolT[:], start=True, stop=True)
                ctx.tstep(("v",), _o6)
                def _o7(v):
                    return v.tensor_copy(kkT[:], P3["m1"][0:64, 0:POOL_SIZE])
                ctx.vstep(("t",), _o7)
                for half in range(2):
                    def _o8(t, half=half):
                        return t.matmul(P3["m0"][0:POOL_SIZE, 0:512], poolT[:],
                                        vw_sb[:, half * 512:(half + 1) * 512],
                                        start=True, stop=True)
                    ctx.tstep(("v",), _o8)
                    def _o9(v, half=half):
                        return v.tensor_copy(vvb[:, half * 512:(half + 1) * 512],
                                             P3["m0"][0:POOL_SIZE, 0:512])
                    ctx.vstep(("t",), _o9)

                if KD3 == "pool":
                    emit_store_out(ctx, X, out_fm)
                    return nc

                nc.leave_named_scope("p5_pool", _sc_p5, False)
                _sc_p6 = nc.enter_named_scope("p6_retr", False)[0]
                for ti in range(4):
                    t0 = ti * 512
                    def _yl(sync, dma, t0=t0):
                        for c in range(8):
                            dma(out=y_tl[c][:],
                                in_=y_d[c * 128:(c + 1) * 128, t0:t0 + 512])
                    ctx.dstep(("t", "v"), _yl)
                    LNP = {"row": P3["row"], "b0": P3["b0"], "b1": P3["b1"]}
                    emit_layernorm(ctx, LNP, X, xn, t0, t0 + 512)
                    for tc in range(4):
                        def _r1(t, t0=t0, tc=tc):
                            return t.matmul(
                                P3["m1"][:, 0:POOL_SIZE],
                                qret[0:64, t0 + tc * 128:t0 + (tc + 1) * 128],
                                kkT[:], start=True, stop=True)
                        ctx.tstep(("v", "s"), _r1)
                        def _r2(v):
                            return v.tensor_add(P3["m1"][:, 0:POOL_SIZE],
                                                P3["m1"][:, 0:POOL_SIZE],
                                                msk50[:])
                        ctx.vstep(("t",), _r2)
                        def _r3(v):
                            return v.tensor_reduce(sw1[:, 3:4],
                                                   P3["m1"][:, 0:POOL_SIZE],
                                                   AX.X, ALU.max, negate=True)
                        ctx.vstep((), _r3)
                        def _r4(v):
                            return v.tensor_scalar(sw1[:, 4:5], sw1[:, 3:4],
                                                   0.25, None, ALU.mult)
                        ctx.vstep((), _r4)
                        def _r5(s_, tc=tc):
                            return s_.activation(pl[:], P3["m1"][:, 0:POOL_SIZE],
                                                 AF.Exp, bias=sw1[:, 4:5],
                                                 scale=0.25,
                                                 accum_out=rs50[:, tc:tc + 1])
                        ctx.sstep(("v",), _r5)
                        def _r6(t, tc=tc):
                            return t.transpose(PSB3[:, 0:128]
                                               if False else
                                               PSB3[0:POOL_SIZE, 0:128],
                                               pl[:], idb[:])
                        ctx.tstep(("s", "v"), _r6)
                        def _r7(v, tc=tc):
                            return v.tensor_copy(plT[:, tc * 128:(tc + 1) * 128],
                                                 PSB3[0:POOL_SIZE, 0:128])
                        ctx.vstep(("t",), _r7)
                        def _r8(v, tc=tc):
                            return v.reciprocal(sw1[:, 5:6], rs50[:, tc:tc + 1])
                        ctx.vstep(("s",), _r8)
                        def _r9(t, tc=tc):
                            return t.transpose(P3["m1"][0:1, 0:128],
                                               sw1[:, 5:6], idf[:])
                        ctx.tstep(("v",), _r9)
                        def _r10(v, tc=tc):
                            return v.tensor_copy(rcrow[0:1, tc * 128:(tc + 1) * 128],
                                                 P3["m1"][0:1, 0:128])
                        ctx.vstep(("t",), _r10)
                    def _rb1(t):
                        return t.matmul(P3["b0"][:, 0:512], onesf[0:1, 0:128],
                                        rcrow[0:1, 0:512], start=True, stop=True)
                    ctx.tstep(("v",), _rb1)
                    def _rb2(v):
                        return v.tensor_copy(rcbc[:], P3["b0"][:, 0:512])
                    ctx.vstep(("t",), _rb2)
                    for oc in range(8):
                        def _rv1(t, oc=oc):
                            return t.matmul(P3["m0"][:, 0:512],
                                            vvb[:, oc * 128:(oc + 1) * 128],
                                            plT[:], start=True, stop=True)
                        ctx.tstep(("v",), _rv1)
                        def _rv2(v, oc=oc):
                            return v.tensor_mul(rb[oc][:], P3["m0"][:, 0:512],
                                                rcbc[:])
                        ctx.vstep(("t",), _rv2)
                    for ocb in range(2):
                        def _gl(sync, dma, ocb=ocb):
                            for kc in range(16):
                                dma(out=zs[kc][:],
                                    in_=ab_gate_w[kc * 128:(kc + 1) * 128,
                                                  ocb * 512:(ocb + 1) * 512])
                        ctx.gdma(("t",), _gl)
                        for ocq in range(4):
                            oc = ocb * 4 + ocq
                            def _gm(t, ocq=ocq):
                                last = None
                                for dc in range(8):
                                    last = t.matmul(
                                        P3["m0"][:, 0:512],
                                        zs[dc][:, ocq * 128:(ocq + 1) * 128],
                                        y_tl[dc][:],
                                        start=(dc == 0), stop=False)
                                for dc in range(8):
                                    last = t.matmul(
                                        P3["m0"][:, 0:512],
                                        zs[8 + dc][:, ocq * 128:(ocq + 1) * 128],
                                        rb[dc][:],
                                        start=False, stop=(dc == 7))
                                return last
                            ctx.tstep(("v", "s", "dma"), _gm)
                            def _ge(s_):
                                return s_.activation(gb[:], P3["m0"][:, 0:512],
                                                     AF.Sigmoid)
                            ctx.sstep(("t",), _ge)
                            def _f1(v, oc=oc):
                                return v.tensor_mul(tg[:], gb[:], rb[oc][:])
                            ctx.vstep(("s",), _f1)
                            def _f2(v, oc=oc):
                                return v.scalar_tensor_tensor(
                                    tg[:], tg[:], condc[:, 0:1], xn[oc][:],
                                    ALU.mult, ALU.add)
                            ctx.vstep((), _f2)
                            def _f3(v, oc=oc, t0=t0):
                                return v.tensor_add(X[oc][:, t0:t0 + 512],
                                                    X[oc][:, t0:t0 + 512],
                                                    tg[:])
                            ctx.vstep((), _f3)
                            def _f4(v, oc=oc, t0=t0):
                                return v.tensor_add(X[oc][:, t0:t0 + 512],
                                                    X[oc][:, t0:t0 + 512],
                                                    y_tl[oc][:])
                            ctx.vstep((), _f4)

                nc.leave_named_scope("p6_retr", _sc_p6, False)
                if KD3 == "y1":
                    def _ydl(sync, dma):
                        for c in range(8):
                            dma(out=X[c][:],
                                in_=y_d[c * 128:(c + 1) * 128, :])
                    def _ydlg(g, dma):
                        for c in range(8):
                            dma(out=X[c][:], in_=y_d[c * 128:(c + 1) * 128, :])
                    ctx.gdma(ALLC, _ydlg)
                    def _sto(sync, dma):
                        for c in range(8):
                            dma(out=out_fm[c * 128:(c + 1) * 128, :], in_=X[c][:])
                    ctx.dstep(("dma",), _sto)
                    ctx.step("sync", ("dma",), lambda sync: None)
                    return nc

        if stop_after == "attn_b":
            emit_store_out(ctx, X, out_fm)
            return nc

        ctx.barrier()
        with contextlib.ExitStack() as p4:
            p4.enter_context(nc.named_scope("ph4_ffn2"))
            lnf = [p4.enter_context(nc.sbuf_tensor(f"l4f{c}", [128, 512], BF16))
                   for c in range(8)]
            a1 = [p4.enter_context(nc.sbuf_tensor(f"b1_{c}", [128, 512], BF16))
                  for c in range(32)]
            a2 = [p4.enter_context(nc.sbuf_tensor(f"b2_{c}", [128, 512], BF16))
                  for c in range(32)]
            wff = [[p4.enter_context(
                nc.sbuf_tensor(f"w4f{p}_{c}", [128, 512], BF16))
                for c in range(8)] for p in range(2)]
            w1b = [[p4.enter_context(
                nc.sbuf_tensor(f"w4b{p}_{c}", [128, 256], BF16))
                for c in range(32)] for p in range(2)]
            scl = p4.enter_context(nc.sbuf_tensor("scl4", [128, 3], F32))
            srow = p4.enter_context(nc.sbuf_tensor("srow4", [1, 4], F32))
            with contextlib.ExitStack() as pq:
                P4 = {
                    "row": pq.enter_context(nc.psum_tensor("f_row", [1, 1024], F32)),
                    "b0": pq.enter_context(nc.psum_tensor("f_b0", [128, 512], F32)),
                    "b1": pq.enter_context(nc.psum_tensor("f_b1", [128, 512], F32)),
                    "m0": pq.enter_context(nc.psum_tensor("f_m0", [128, 512], F32)),
                    "m1": pq.enter_context(nc.psum_tensor("f_m1", [128, 512], F32)),
                    "u": pq.enter_context(nc.psum_tensor("f_u", [128, 512], F32)),
                }
                def _lds(sync, dma):
                    dma(out=srow[0:1, 0:1], in_=s0in)
                    dma(out=srow[0:1, 1:2], in_=s1in)
                    dma(out=srow[0:1, 2:3], in_=s2in)
                ctx.dstep((), _lds)
                def _s2(v):
                    return v.tensor_scalar(srow[0:1, 0:3], srow[0:1, 0:3], 2.0,
                                           None, ALU.mult)
                ctx.vstep(("dma",), _s2)
                def _sbc(t):
                    return t.matmul(P4["u"][:, 0:3], onesf[0:1, 0:128],
                                    srow[0:1, 0:3], start=True, stop=True)
                ctx.tstep(("v",), _sbc)
                def _scp(v):
                    return v.tensor_copy(scl[:], P4["u"][:, 0:3])
                ctx.vstep(("t",), _scp)
                emit_ffn(ctx, nc, X, out_fm, (w0h_d, w1_d, w2_d), scl, P4,
                         {"lnf": lnf, "a1": a1, "a2": a2, "wff": wff, "w1b": w1b})

        emit_store_out(ctx, X, out_fm)
        return nc
    return nc


def make_host_constants():
    cores = []
    dm = np.zeros((128, 4 * 512), np.float32)
    for mi in range(4):
        delta = mi * 128
        i = np.arange(128)[:, None]
        j = np.arange(512)[None, :]
        dm[:, mi * 512:(mi + 1) * 512] = np.where(j < i + delta, NEG, 0.0)
    identf = np.eye(128, dtype=np.float32)
    identb = np.eye(128, dtype=ml_dtypes.bfloat16)
    EPSf = float(np.finfo(np.float32).eps)
    idx = np.arange(8192, dtype=np.float32)
    sc = 1.0 / (10000.0 ** (np.arange(0, DH, 2, dtype=np.float32) / DH) + EPSf)
    ang = idx[:, None] * sc[None, :]
    pos = np.stack([np.sin(ang), np.cos(ang)], axis=-1).reshape(8192, DH)
    pos = pos.astype(np.float32)

    trineg = np.zeros((128, 512), np.float32)
    for scn in range(2):
        s = np.arange(128)[:, None] + scn * 128
        l = np.arange(256)[None, :]
        trineg[:, scn * 256:(scn + 1) * 256] = np.where(s <= l, 0.0, NEG)

    for j in range(NCORES):
        c = {}
        ho = np.zeros((128, 16), np.uint32)
        p_ = np.arange(128, dtype=np.uint32)
        for ch in range(8):
            if j > 0:
                ho[:, ch] = 2048 * (j - 1) + ch * 128 + p_
            if j < 7:
                ho[:, 8 + ch] = 2048 * (j + 1) + 1024 + ch * 128 + p_
        c["halo_offs"] = ho
        hm = np.zeros((128, 2), np.float32)
        hm[:, 0] = 1.0 if j > 0 else 0.0
        hm[:, 1] = 1.0 if j < 7 else 0.0
        c["halo_mask"] = hm
        cm = np.zeros((8, 16), np.float32)
        cb = np.full((8, 2), NEG, np.float32)
        for r in range(8):
            for k in range(8):
                if r < k < j:
                    cm[k, r] = 1.0
                if j < k < r:
                    cm[k, 8 + r] = 1.0
            if r < j:
                cb[r, 0] = 0.0
            if r > j:
                cb[r, 1] = 0.0
        c["cc_mats"] = cm
        c["cc_bias"] = cb
        pp_ = np.arange(128)[:, None]
        ff_ = np.arange(128)[None, :]
        rr = pp_ // 16
        tloc = (pp_ % 16) * 128 + ff_
        g = rr * 2048 + tloc
        sA = rr * 1024 + tloc
        sB = 1024 * (7 - rr) + (tloc - 1024)
        ref = np.where(tloc < 1024, sA, 8192 + sB)
        c["ref_iota"] = (ref + 0.0).astype(np.float32) + 0 * ff_
        c["g_iota"] = (g + 0.0).astype(np.float32) + 0 * ff_
        si = np.zeros((1, 64), np.float32)
        si[0, :] = np.arange(64)
        c["slot_iota"] = si
        posA = pos[1024 * j:1024 * (j + 1)]
        posB = pos[1024 * (7 - j):1024 * (8 - j)]
        p2 = np.concatenate([posA.T, posB.T], axis=1)
        c["pos2"] = np.concatenate([p2, p2], axis=0).astype(np.float32)
        d = np.zeros((128, 256), np.float32)
        for qb in range(2):
            posq = j if qb == 0 else 7 - j
            for kp in range(8):
                live = kp < posq
                if not live:
                    d[:, (qb * 8 + kp) * 16:(qb * 8 + kp + 1) * 16] = NEG
        c["dead"] = d
        c["diagmasks"] = dm
        c["identf"] = identf
        c["identb"] = identb
        c["trineg"] = trineg
        cores.append(c)
    return cores


def build_in_maps(inputs):
    x = np.ascontiguousarray(inputs["L0_attn_a_x"])
    consts = make_host_constants()
    in_maps = []
    for j in range(NCORES):
        xa = x[0, 1024 * j:1024 * (j + 1), :]
        xb = x[1, 1024 * (7 - j):1024 * (8 - j), :]
        x_fm = np.ascontiguousarray(
            np.concatenate([xa, xb], axis=0).T)
        m = {
            "x_fm": x_fm,
            "wq": inputs["L0_attn_a_Wq"],
            "wk": inputs["L0_attn_a_Wk"],
            "wv": inputs["L0_attn_a_Wv"],
            "wo": inputs["L0_attn_a_Wo"],
            "k0u8": np.ascontiguousarray(inputs["L0_ffn_k0"]).view(np.uint8),
            "k1u8": np.ascontiguousarray(inputs["L0_ffn_k1"]).view(np.uint8),
            "k2u8": np.ascontiguousarray(inputs["L0_ffn_k2"]).view(np.uint8),
            "s0in": np.asarray(inputs["L0_ffn_s0"], np.float32).reshape(1, 1),
            "s1in": np.asarray(inputs["L0_ffn_s1"], np.float32).reshape(1, 1),
            "s2in": np.asarray(inputs["L0_ffn_s2"], np.float32).reshape(1, 1),
            "ab_norm_w": np.ascontiguousarray(
                np.asarray(inputs["L0_attn_b_norm_w"], np.float32)
                .reshape(8, 128).T),
            "ab_inproj": inputs["L0_attn_b_in_proj_w"],
            "ab_conv_w": np.ascontiguousarray(
                np.asarray(inputs["L0_attn_b_conv_w"], np.float32)
                .reshape(18, 128, 4).transpose(1, 0, 2).reshape(128, 72)),
            "ab_conv_b": np.ascontiguousarray(
                np.asarray(inputs["L0_attn_b_conv_b"], np.float32)
                .reshape(18, 128).T),
            "ab_dt_bias": np.asarray(inputs["L0_attn_b_dt_bias"], np.float32).reshape(NHEADS, 1),
            "ab_A_log": np.asarray(inputs["L0_attn_b_A_log"], np.float32).reshape(NHEADS, 1),
            "ab_D": np.asarray(inputs["L0_attn_b_D"], np.float32).reshape(NHEADS, 1),
            "ab_ssm_norm_w": np.ascontiguousarray(
                np.asarray(inputs["L0_attn_b_ssm_norm_w"], np.float32)
                .reshape(16, 128).T),
            "ab_outproj": inputs["L0_attn_b_out_proj_w"],
            "ab_scorer_w1": inputs["L0_attn_b_scorer_w1"],
            "ab_scorer_w2": np.ascontiguousarray(
                np.asarray(inputs["L0_attn_b_scorer_w2"], np.float32)
                .reshape(2, 128).T),
            "ab_summ_w": inputs["L0_attn_b_summ_w"],
            "ab_q_w": inputs["L0_attn_b_q_w"],
            "ab_k_w": inputs["L0_attn_b_k_w"],
            "ab_v_w": inputs["L0_attn_b_v_w"],
            "ab_gate_w": inputs["L0_attn_b_gate_w"],
        }
        m.update(consts[j])
        for name, shape, dt in (
            ("esel", (NHEADS, NHEADS * 128), np.float32),
            ("psel", (NHEADS, 8 * 128), np.float32),
            ("halo_offs", (128, 16), np.uint32),
            ("halo_mask", (128, 2), np.float32),
            ("cc_mats", (8, 16), np.float32),
            ("cc_bias", (8, 2), np.float32),

        ):
            if name not in m:
                m[name] = np.zeros(shape, dt)
        in_maps.append({k: v for k, v in m.items() if k in _EIN_NAMES})
    return in_maps


def kernel(**inputs):
    nc = build_program(stop_after=_STOP_AFTER[0])
    in_maps = build_in_maps(inputs)
    res = run_bass_kernel_spmd(nc, in_maps, core_ids=list(range(NCORES)))
    out = np.zeros((2, 8192, D), np.float32)
    for j in range(NCORES):
        o = res.results[j]["out_fm"]
        out[0, 1024 * j:1024 * (j + 1), :] = o[:, 0:1024].T
        out[1, 1024 * (7 - j):1024 * (8 - j), :] = o[:, 1024:2048].T
    return out


_STOP_AFTER = [None]
_EIN_NAMES = []


# version 11
# speedup vs baseline: 1.1336x; 1.1336x over previous
import contextlib
import numpy as np
import ml_dtypes

import concourse.bass as bass
import concourse.mybir as mybir
from concourse.bass_utils import run_bass_kernel_spmd

F32 = mybir.dt.float32
BF16 = mybir.dt.bfloat16
U8 = mybir.dt.uint8
U32 = mybir.dt.uint32
AF = mybir.ActivationFunctionType
ALU = mybir.AluOpType
AX = mybir.AxisListType

NCORES = 8
D = 1024
TPC = 2048
NH = 16
DH = 64
NEG = -1.0e30
EPS_LN = 1e-5

D_INNER = 2048
D_STATE = 128
NHEADS = 32
CONV_DIM = 2304
D_IN_PROJ = 4384
POOL_SIZE = 50
SCORER_HID = 256
SUMM = 64


class Ctx:

    def __init__(self, nc, block, st):
        self.nc = nc
        self.block = block
        self.st = st
        self.sem = {}
        self.cnt = {}
        self._cc = {}
        self.cpool = None
        self._waited = {e: {} for e in
                        ("tensor", "vector", "scalar", "gpsimd", "sync")}
        self._cur_eng = None

    def barrier(self):
        names = [n for n in ("t", "v", "s", "g", "dma", "gd", "cc", "pre")
                 if n in self.sem]
        for eng in ("tensor", "vector", "scalar", "gpsimd", "sync"):
            def body(e, names=names, eng=eng):
                self._cur_eng = eng
                self.wait(e, *names)
            getattr(self.block, eng)(body)

    def constcell(self, value):
        key = float(value)
        if key not in self._cc:
            col = len(self._cc)
            self._cc[key] = col
            cell = self.cpool[0:1, col:col + 1]
            def _set(v, cell=cell, value=value):
                return v.memset(cell, float(value))
            self.step("vector", (), _set, "v")
        return self.cpool[0:1, self._cc[key]:self._cc[key] + 1]

    def mksem(self, *names):
        for n in names:
            self.sem[n] = self.st.enter_context(self.nc.semaphore(n))
            self.cnt[n] = 0

    def inc(self, inst, name, k=1):
        inst.then_inc(self.sem[name], k)
        self.cnt[name] += k
        return inst

    def wait(self, eng, *names):
        seen = []
        for n in names:
            seen.append(n)
            if n == "dma" and "gd" in self.sem and "gd" not in names:
                seen.append("gd")
        wl = self._waited.get(self._cur_eng, {})
        for n in seen:
            v = self.cnt[n]
            if v > 0 and wl.get(n, -1) < v:
                eng.wait_ge(self.sem[n], v)
                wl[n] = v

    _OWN = {"tensor": "t", "vector": "v", "scalar": "s", "gpsimd": "g"}

    def snap(self, name):
        return self.cnt[name]

    def step(self, engname, deps, fn, inc=None, inck=1, waits=()):
        own = self._OWN.get(engname)
        if own is not None and own not in deps and engname != "tensor":
            deps = tuple(deps) + (own,)
        def body(eng):
            self._cur_eng = engname
            wl = self._waited.get(engname, {})
            for n, val in waits:
                if val > 0 and wl.get(n, -1) < val:
                    eng.wait_ge(self.sem[n], val)
                    wl[n] = val
            self.wait(eng, *deps)
            last = fn(eng)
            if inc is not None:
                assert last is not None, f"step for {inc} returned no inst"
                self.inc(last, inc, inck)
        getattr(self.block, engname)(body)

    def tstep(self, deps, fn, waits=()):
        self.step("tensor", deps, fn, "t", waits=waits)

    def vstep(self, deps, fn, waits=()):
        self.step("vector", deps, fn, "v", waits=waits)

    def sstep(self, deps, fn, waits=()):
        self.step("scalar", deps, fn, "s", waits=waits)

    def gstep(self, deps, fn, inc="g", inck=1):
        self.step("gpsimd", deps, fn, inc, inck)

    def _dma_step(self, engname, deps, fn):
        own = self._OWN.get(engname)
        if own is not None and own not in deps:
            deps = tuple(deps) + (own,)
        sem = "gd" if engname == "gpsimd" else "dma"
        def body(eng):
            self._cur_eng = engname
            self.wait(eng, *deps)
            def dma(**kw):
                inst = eng.dma_start(**kw)
                inst.then_inc(self.sem[sem], 16)
                self.cnt[sem] += 16
                return inst
            fn(eng, dma)
        getattr(self.block, engname)(body)

    def dstep(self, deps, fn, n=None):
        self._dma_step("sync", deps, fn)

    def gdma(self, deps, fn, n=None):
        self._dma_step("gpsimd", deps, fn)


ALLC = ("t", "v", "s", "g", "dma", "cc")


def emit_layernorm(ctx, P, Xc, out_c, t0, t1, ln_stats=None):
    nc = ctx.nc
    T = ctx.T
    sq = T["sq"]
    strow = T["strow"]
    onecol = T["onecol"]
    onesf = T["onesf"]
    for tt0 in range(t0, t1, 512):
        w = min(512, t1 - tt0)
        def _sumx(t, tt0=tt0, w=w):
            last = None
            for dc in range(8):
                last = t.matmul(P["row"][0:1, 0:w], onecol[:],
                                Xc[dc][:, tt0:tt0 + w],
                                start=(dc == 0), stop=(dc == 7))
            return last
        ctx.tstep(("v", "s", "dma"), _sumx)
        for dc in range(8):
            def _sq(s, dc=dc, tt0=tt0, w=w):
                return s.activation(sq[:, 0:w], Xc[dc][:, tt0:tt0 + w], AF.Square)
            ctx.sstep(("t", "v", "dma"), _sq)
            def _sumsq(t, dc=dc, w=w):
                return t.matmul(P["row"][0:1, 512:512 + w], onecol[:], sq[:, 0:w],
                                start=(dc == 0), stop=(dc == 7))
            ctx.tstep(("s",), _sumsq)
        def _st1a(v, w=w):
            v.tensor_scalar(strow[:, 0:w], P["row"][0:1, 0:w], 1.0 / D, None, ALU.mult)
            return v.tensor_scalar(strow[:, 512:512 + w], P["row"][0:1, 512:512 + w],
                                   1.0 / D, None, ALU.mult)
        ctx.vstep(("t",), _st1a)
        def _st1b(v, w=w):
            return v.tensor_mul(strow[:, 1024:1024 + w], strow[:, 0:w],
                                strow[:, 0:w])
        ctx.vstep((), _st1b)
        def _st1c(v, w=w):
            return v.tensor_sub(strow[:, 1024:1024 + w], strow[:, 512:512 + w],
                                strow[:, 1024:1024 + w])
        ctx.vstep((), _st1c)
        epsc = ctx.constcell(EPS_LN)
        def _st2(s, w=w, epsc=epsc):
            return s.activation(strow[:, 1024:1024 + w], strow[:, 1024:1024 + w],
                                AF.Sqrt, bias=epsc)
        ctx.sstep(("v",), _st2)
        def _st3a(v, w=w):
            return v.reciprocal(strow[:, 1536:1536 + w], strow[:, 1024:1024 + w])
        ctx.vstep(("s",), _st3a)
        def _st3b(v, w=w):
            return v.tensor_mul(strow[:, 512:512 + w], strow[:, 0:w],
                                strow[:, 1536:1536 + w])
        ctx.vstep((), _st3b)
        def _bc(t, w=w):
            t.matmul(P["b0"][:, 0:w], onesf[0:1, 0:128], strow[0:1, 1536:1536 + w],
                     start=True, stop=True)
            return t.matmul(P["b1"][:, 0:w], onesf[0:1, 0:128],
                            strow[0:1, 512:512 + w], start=True, stop=True)
        ctx.tstep(("v",), _bc)
        for dc in range(8):
            def _ap1(v, dc=dc, tt0=tt0, w=w):
                return v.scalar_tensor_tensor(sq[:, 0:w], Xc[dc][:, tt0:tt0 + w],
                                              1.0, P["b0"][:, 0:w],
                                              ALU.mult, ALU.mult)
            ctx.vstep(("t",), _ap1)
            def _ap2(v, dc=dc, tt0=tt0, w=w):
                return v.tensor_sub(out_c[dc][:, tt0 - t0:tt0 - t0 + w],
                                    sq[:, 0:w], P["b1"][:, 0:w])
            ctx.vstep((), _ap2)
        if ln_stats is not None:
            def _keep(v, tt0=tt0, w=w):
                v.tensor_copy(ln_stats[:, tt0:tt0 + w], strow[:, 1536:1536 + w])
                return v.tensor_copy(ln_stats[:, TPC + tt0:TPC + tt0 + w],
                                     strow[:, 512:512 + w])
            ctx.vstep((), _keep)


def emit_store_dbg(ctx, chunks, out_fm):
    Xc = ctx.T["X"]
    def _cvt(v):
        last = None
        for c in range(8):
            last = v.tensor_copy(Xc[c][:], chunks[c][:])
        return last
    ctx.vstep(ALLC, _cvt)
    def _st(sync, dma):
        for c in range(8):
            dma(out=out_fm[c * 128:(c + 1) * 128, :], in_=Xc[c][:])
    ctx.dstep(("v",), _st)
    ctx.step("sync", ("dma",), lambda sync: None)


def emit_store_out(ctx, Xc, out_fm):
    def _st(sync, dma):
        for c in range(8):
            dma(out=out_fm[c * 128:(c + 1) * 128, :], in_=Xc[c][:])
    ctx.dstep(ALLC, _st)
    ctx.step("sync", ("dma",), lambda sync: None)



def emit_ffn(ctx, nc, X, out_fm, wdram, scl_cells, P, bufs):
    w0h_d, w1_d, w2_d = wdram
    lnf, a1, a2 = bufs["lnf"], bufs["a1"], bufs["a2"]
    wff, w1b = bufs["wff"], bufs["w1b"]
    scl = scl_cells
    for tt in range(4):
        t0 = tt * 512
        emit_layernorm(ctx, P, X, lnf, t0, t0 + 512)
        snaps = {}
        def load0(nb, pp):
            def _lw(sync, dma, nb=nb, pp=pp):
                for dc in range(8):
                    dma(out=wff[pp][dc][:],
                        in_=w0h_d[dc * 128:(dc + 1) * 128,
                                  nb * 512:(nb + 1) * 512])
            ctx.dstep(("pre", "t"), _lw)
            snaps[("l0", nb)] = ctx.snap("dma")
        load0(0, 0)
        for nb in range(8):
            pp = nb % 2
            if nb + 1 < 8:
                load0(nb + 1, (nb + 1) % 2)
            for ncq in range(4):
                nch = nb * 4 + ncq
                ps = P["m0"] if ncq % 2 == 0 else P["m1"]
                def _mm(t, ncq=ncq, ps=ps, pp=pp):
                    last = None
                    for dc in range(8):
                        last = t.matmul(ps[:, 0:512],
                                        wff[pp][dc][:, ncq * 128:(ncq + 1) * 128],
                                        lnf[dc][:, 0:512],
                                        start=(dc == 0), stop=(dc == 7))
                    return last
                ctx.tstep(("v",), _mm, waits=[("dma", snaps[("l0", nb)])])
                def _ev(v, nch=nch, ps=ps):
                    return v.tensor_scalar(a1[nch][:, 0:512], ps[:, 0:512],
                                           scl[:, 0:1], 0.0, ALU.mult, ALU.max)
                ctx.vstep(("t",), _ev)
        def load1(nb, pp):
            def _lw1(sync, dma, nb=nb, pp=pp):
                for kc in range(32):
                    dma(out=w1b[pp][kc][:],
                        in_=w1_d[kc * 128:(kc + 1) * 128,
                                 nb * 256:(nb + 1) * 256])
            ctx.dstep(("pre", "t"), _lw1)
            snaps[("l1", nb)] = ctx.snap("dma")
        load1(0, 0)
        for nb in range(16):
            pp = nb % 2
            if nb + 1 < 16:
                load1(nb + 1, (nb + 1) % 2)
            for ncq in range(2):
                nch = nb * 2 + ncq
                ps = P["m0"] if ncq % 2 == 0 else P["m1"]
                def _mm1(t, ncq=ncq, ps=ps, pp=pp):
                    last = None
                    for kc in range(32):
                        last = t.matmul(ps[:, 0:512],
                                        w1b[pp][kc][:, ncq * 128:(ncq + 1) * 128],
                                        a1[kc][:, 0:512],
                                        start=(kc == 0), stop=(kc == 31))
                    return last
                ctx.tstep(("v",), _mm1, waits=[("dma", snaps[("l1", nb)])])
                def _ev1(v, nch=nch, ps=ps):
                    return v.tensor_scalar(a2[nch][:, 0:512], ps[:, 0:512],
                                           scl[:, 1:2], 0.0, ALU.mult, ALU.max)
                ctx.vstep(("t",), _ev1)
        def load2(nb, pp):
            def _lw2(sync, dma, nb=nb, pp=pp):
                for kc in range(32):
                    dma(out=w1b[pp][kc][:],
                        in_=w2_d[kc * 128:(kc + 1) * 128,
                                 nb * 256:(nb + 1) * 256])
            ctx.dstep(("pre", "t"), _lw2)
            snaps[("l2", nb)] = ctx.snap("dma")
        load2(0, 0)
        for nb in range(4):
            pp = nb % 2
            if nb + 1 < 4:
                load2(nb + 1, (nb + 1) % 2)
            for ncq in range(2):
                oc = nb * 2 + ncq
                ps = P["m0"] if ncq % 2 == 0 else P["m1"]
                def _mm2(t, ncq=ncq, ps=ps, pp=pp):
                    last = None
                    for kc in range(32):
                        last = t.matmul(ps[:, 0:512],
                                        w1b[pp][kc][:, ncq * 128:(ncq + 1) * 128],
                                        a2[kc][:, 0:512],
                                        start=(kc == 0), stop=(kc == 31))
                    return last
                ctx.tstep(("v",), _mm2, waits=[("dma", snaps[("l2", nb)])])
                def _ev2(v, oc=oc, ps=ps, t0=t0):
                    return v.scalar_tensor_tensor(
                        X[oc][:, t0:t0 + 512], ps[:, 0:512], scl[:, 2:3],
                        X[oc][:, t0:t0 + 512], ALU.mult, ALU.add)
                ctx.vstep(("t",), _ev2)


def build_program(stop_after=None):
    _EIN_NAMES.clear()
    nc = bass.Bass("TRN2", target_bir_lowering=False)

    def ein(name, shape, dtype=F32):
        _EIN_NAMES.append(name)
        return nc.dram_tensor(name, shape, dtype, kind="ExternalInput").ap()

    def idram(name, shape, dtype=F32):
        return nc.dram_tensor(name, shape, dtype, kind="Internal").ap()

    import os as _osd0
    _kd0 = set(_osd0.environ.get("KDECL", "all").split(","))
    def _want0(g):
        return "all" in _kd0 or g in _kd0
    x_fm = ein("x_fm", [D, TPC])
    if _want0("aw"):
        wq = ein("wq", [D, D])
        wk = ein("wk", [D, D])
        wv = ein("wv", [D, D])
        wo = ein("wo", [D, D])
    else:
        wq = wk = wv = wo = None
    if _want0("msk"):
        pos2 = ein("pos2", [128, TPC])
        dead = ein("dead", [128, 256])
        diagmasks = ein("diagmasks", [128, 4 * 512])
        identf = ein("identf", [128, 128])
        identb = ein("identb", [128, 128], BF16)
    else:
        pos2 = dead = diagmasks = identf = identb = None
    import os as _osd
    _kd = set(_osd.environ.get("KDECL", "all").split(","))
    def _want(g):
        return "all" in _kd or g in _kd
    if _want("u8"):
        k0u8 = ein("k0u8", [2 * D, 4096], U8)
        k1u8 = ein("k1u8", [4096, 4096], U8)
        k2u8 = ein("k2u8", [4096, D], U8)
    else:
        k0u8 = k1u8 = k2u8 = None
    if _want("sc"):
        s0in = ein("s0in", [1, 1])
        s1in = ein("s1in", [1, 1])
        s2in = ein("s2in", [1, 1])
    else:
        s0in = s1in = s2in = None

    if _want("ab"):
        ab_norm_w = ein("ab_norm_w", [128, 8])
        ab_inproj = ein("ab_inproj", [D, D_IN_PROJ])
        ab_conv_w = ein("ab_conv_w", [128, 18 * 4])
        ab_conv_b = ein("ab_conv_b", [128, 18])
        ab_dt_bias = ein("ab_dt_bias", [NHEADS, 1])
        ab_A_log = ein("ab_A_log", [NHEADS, 1])
        ab_D = ein("ab_D", [NHEADS, 1])
        ab_ssm_norm_w = ein("ab_ssm_norm_w", [128, 16])
        ab_outproj = ein("ab_outproj", [D_INNER, D])
        ab_scorer_w1 = ein("ab_scorer_w1", [D, SCORER_HID])
        ab_scorer_w2 = ein("ab_scorer_w2", [128, 2])
        ab_summ_w = ein("ab_summ_w", [D, SUMM])
        ab_q_w = ein("ab_q_w", [D, SUMM])
        ab_k_w = ein("ab_k_w", [SUMM, SUMM])
        ab_v_w = ein("ab_v_w", [SUMM, D])
        ab_gate_w = ein("ab_gate_w", [2 * D, D])
    else:
        ab_norm_w = ab_inproj = ab_conv_w = ab_conv_b = ab_dt_bias = None
        ab_A_log = ab_D = ab_ssm_norm_w = ab_outproj = ab_scorer_w1 = None
        ab_scorer_w2 = ab_summ_w = ab_q_w = ab_k_w = ab_v_w = ab_gate_w = None

    if _want("sched"):
        esel = ein("esel", [NHEADS, NHEADS * 128])
        psel = ein("psel", [NHEADS, 8 * 128])
        trineg = ein("trineg", [128, 2 * 256])
        halo_offs = ein("halo_offs", [128, 16], U32)
        halo_mask = ein("halo_mask", [128, 2])
        cc_mats = ein("cc_mats", [8, 16])
        cc_bias = ein("cc_bias", [8, 2])
        ref_iota = ein("ref_iota", [128, 128])
        g_iota = ein("g_iota", [128, 128])
        slot_iota = ein("slot_iota", [1, 64])
    else:
        esel = psel = trineg = halo_offs = halo_mask = cc_mats = cc_bias = None
        ref_iota = g_iota = slot_iota = None

    out_fm = nc.dram_tensor("out_fm", [D, TPC], F32, kind="ExternalOutput").ap()

    if _want0("wd"):
        w0h_d = idram("w0h_d", [D, 4096], BF16)
        w1_d = idram("w1_d", [4096, 4096], BF16)
        w2_d = idram("w2_d", [4096, D], BF16)
    else:
        w0h_d = w1_d = w2_d = None
    cc_k_in = idram("cc_k_in", [TPC, D], BF16)
    cc_k_out = idram("cc_k_out", [NCORES * TPC, D], BF16)
    ccv_i = idram("ccv_i", [TPC, NH * 65], BF16)
    ccv_o = idram("ccv_o", [NCORES * TPC, NH * 65], BF16)
    cc_n_in = idram("cc_n_in", [NH, 1])
    cc_n_out = idram("cc_n_out", [NCORES * NH, 1])
    cc_t_in = None if not _want0("ccx") else idram("cc_t_in", [2 * D, 3])
    cc_t_out = None if not _want0("ccx") else idram("cc_t_out", [NCORES * 2 * D, 3])
    cc_s_in = None if not _want0("ccx") else idram("cc_s_in", [128, 4096], BF16)
    lz_d = idram("lz_d", [128, 6 * 2048], BF16)
    y_d = idram("y_d", [D, TPC], BF16)
    q_d = idram("q_d", [D, TPC], BF16)
    cc_s_out = None if not _want0("ccx") else idram("cc_s_out", [NCORES * 128, 4096], BF16)
    cc_d_in = None if not _want0("ccx") else idram("cc_d_in", [NHEADS, 2])
    cc_d_out = None if not _want0("ccx") else idram("cc_d_out", [NCORES * NHEADS, 2])
    cc_p_in = None if not _want0("ccx") else idram("cc_p_in", [65, TPC])
    cc_p_out = None if not _want0("ccx") else idram("cc_p_out", [NCORES * 65, TPC])
    cc_q_in = None if not _want0("ccx") else idram("cc_q_in", [TPC, SUMM])
    cc_q_out = None if not _want0("ccx") else idram("cc_q_out", [NCORES * TPC, SUMM])

    RG = [list(range(NCORES))]

    with contextlib.ExitStack() as st:
        block = st.enter_context(nc.Block())
        ctx = Ctx(nc, block, st)
        ctx.mksem("t", "v", "s", "g", "dma", "gd", "cc", "pre")
        ctx.cpool = st.enter_context(nc.sbuf_tensor("cpool", [1, 64], F32))

        X = [st.enter_context(nc.sbuf_tensor(f"x{c}", [128, TPC], F32))
             for c in range(8)]
        idf = st.enter_context(nc.sbuf_tensor("idf", [128, 128], F32))
        idb = st.enter_context(nc.sbuf_tensor("idb", [128, 128], BF16))
        onesf = st.enter_context(nc.sbuf_tensor("onesf", [1, 128], F32))
        onecol = st.enter_context(nc.sbuf_tensor("onecol", [128, 1], F32))
        sq = st.enter_context(nc.sbuf_tensor("sq", [128, 512], F32))
        strow = st.enter_context(nc.sbuf_tensor("strow", [1, 4 * 512], F32))
        ctx.T = {"sq": sq, "strow": strow, "onecol": onecol, "onesf": onesf, "X": X}

        import os as _osz
        def _ld(sync, dma):
            for c in range(8):
                dma(out=X[c][:], in_=x_fm[c * 128:(c + 1) * 128, :])
            if identf is None:
                return
            dma(out=idf[:], in_=identf)
            dma(out=idb[:], in_=identb)
        if _osz.environ.get("KSKIP_LD", "") != "1":
            ctx.dstep((), _ld)
        def _ones(v):
            v.memset(onesf[:], 1.0)
            return v.memset(onecol[:], 1.0)
        ctx.vstep((), _ones)

        import os as _os0
        if _os0.environ.get("KSTAGE", "") == "mini":
            def _ccm(g):
                ctx.inc(g.collective_compute(
                    "AllGather", ALU.bypass, replica_groups=RG,
                    ins=[cc_k_in.opt()], outs=[cc_k_out.opt()]), "cc")
                ctx.inc(g.collective_compute(
                    "AllGather", ALU.bypass, replica_groups=RG,
                    ins=[ccv_i.opt()], outs=[ccv_o.opt()]), "cc")
                ctx.inc(g.collective_compute(
                    "AllGather", ALU.bypass, replica_groups=RG,
                    ins=[cc_n_in.opt()], outs=[cc_n_out.opt()]), "cc")
                return None
            ctx.step("gpsimd", ("dma",), _ccm)
            def _ccw(g):
                ctx.wait(g, "cc")
                return None
            ctx.step("gpsimd", ("cc",), _ccw)
            emit_store_out(ctx, X, out_fm)
            return nc
        with contextlib.ExitStack() as p1:
            p1.enter_context(nc.named_scope("ph1_attn"))
            dead_sb = p1.enter_context(nc.sbuf_tensor("dead_sb", [128, 256], F32))
            dmask_sb = p1.enter_context(
                nc.sbuf_tensor("dmask_sb", [128, 4 * 512], BF16))
            def _ldg(g, dma):
                dma(out=dmask_sb[:], in_=diagmasks)
            ctx.gdma((), _ldg)
            def _ldd(sync, dma):
                dma(out=dead_sb[:], in_=dead)
            ctx.dstep((), _ldd)
            wsb = [p1.enter_context(nc.sbuf_tensor(f"wsb{c}", [128, 1024], BF16))
                   for c in range(8)]
            qrow = p1.enter_context(nc.sbuf_tensor("qrow", [1, 16], F32))
            krow = p1.enter_context(nc.sbuf_tensor("krow", [1, 16], F32))
            knorm = p1.enter_context(nc.sbuf_tensor("knorm", [NH, 1], F32))
            ktmp = p1.enter_context(nc.sbuf_tensor("ktmp", [16, 8], F32))
            btab = p1.enter_context(nc.sbuf_tensor("btab", [128, 256], F32))
            livetab = p1.enter_context(nc.sbuf_tensor("livetab", [128, 16], F32))

            with contextlib.ExitStack() as pp:
                pos_sb = pp.enter_context(
                    nc.sbuf_tensor("pos_sb", [128, TPC], BF16))
                brow = pp.enter_context(nc.sbuf_tensor("brow", [1, 256], F32))
                qsb = [pp.enter_context(nc.sbuf_tensor(f"q{c}", [128, TPC], BF16))
                       for c in range(8)]
                vstage = pp.enter_context(
                    nc.sbuf_tensor("vstage", [128, NH * 65], BF16))
                def _ldpos(g, dma):
                    dma(out=pos_sb[:], in_=pos2)
                ctx.gdma((), _ldpos)
                ln = [pp.enter_context(nc.sbuf_tensor(f"ln{c}", [128, TPC], BF16))
                      for c in range(8)]
                kch1 = pp.enter_context(nc.sbuf_tensor("kch1", [128, TPC], BF16))
                pcv_u = pp.enter_context(nc.sbuf_tensor("pcv_u", [128, 2 * 4096], U8))
                pcv_b = pp.enter_context(nc.sbuf_tensor("pcv_b", [128, 4096], BF16))
                import os as _os
                _nopre = _os.environ.get("KNOPRE", "") == "1"
                def _preconv(g):
                    for c in range(8):
                        if ctx.cnt["pre"]:
                            g.wait_ge(ctx.sem["pre"], ctx.cnt["pre"])
                        i0 = g.dma_start(out=pcv_u[:, 0:4096],
                                         in_=k0u8[c * 128:(c + 1) * 128, :])
                        i1 = g.dma_start(
                            out=pcv_u[:, 4096:8192],
                            in_=k0u8[1024 + c * 128:1024 + (c + 1) * 128, :])
                        i0.then_inc(ctx.sem["pre"], 16)
                        i1.then_inc(ctx.sem["pre"], 16)
                        ctx.cnt["pre"] += 32
                        g.wait_ge(ctx.sem["pre"], ctx.cnt["pre"])
                        cv = g.tensor_tensor(pcv_b[:], pcv_u[:, 0:4096],
                                             pcv_u[:, 4096:8192], ALU.subtract)
                        cv.then_inc(ctx.sem["g"], 1)
                        ctx.cnt["g"] += 1
                        g.wait_ge(ctx.sem["g"], ctx.cnt["g"])
                        st = g.dma_start(out=w0h_d[c * 128:(c + 1) * 128, :],
                                         in_=pcv_b[:])
                        st.then_inc(ctx.sem["pre"], 16)
                        ctx.cnt["pre"] += 16
                    for (src, dst, rows, ncols) in ((k1u8, w1_d, 4096, 4096),
                                                    (k2u8, w2_d, 4096, 1024)):
                        for c in range(rows // 128):
                            g.wait_ge(ctx.sem["pre"], ctx.cnt["pre"])
                            i0 = g.dma_start(out=pcv_u[:, 0:ncols],
                                             in_=src[c * 128:(c + 1) * 128, :])
                            i0.then_inc(ctx.sem["pre"], 16)
                            ctx.cnt["pre"] += 16
                            g.wait_ge(ctx.sem["pre"], ctx.cnt["pre"])
                            cv = g.tensor_scalar(pcv_b[:, 0:ncols],
                                                 pcv_u[:, 0:ncols],
                                                 2.0, -1.0, ALU.mult, ALU.add)
                            cv.then_inc(ctx.sem["g"], 1)
                            ctx.cnt["g"] += 1
                            g.wait_ge(ctx.sem["g"], ctx.cnt["g"])
                            st = g.dma_start(out=dst[c * 128:(c + 1) * 128, :],
                                             in_=pcv_b[:, 0:ncols])
                            st.then_inc(ctx.sem["pre"], 16)
                            ctx.cnt["pre"] += 16
                    return None
                if not _nopre:
                    ctx.step("gpsimd", ("dma",), _preconv)
                P = {
                    "row": pp.enter_context(nc.psum_tensor("p_row", [1, 1024], F32)),
                    "b0": pp.enter_context(nc.psum_tensor("p_b0", [128, 512], F32)),
                    "b1": pp.enter_context(nc.psum_tensor("p_b1", [128, 512], F32)),
                    "m0": pp.enter_context(nc.psum_tensor("p_m0", [128, 512], F32)),
                    "m1": pp.enter_context(nc.psum_tensor("p_m1", [128, 512], F32)),
                    "n": pp.enter_context(nc.psum_tensor("p_n", [1, 512], F32)),
                }

                import os as _os
                _kstage = _os.environ.get("KSTAGE", "")
                emit_layernorm(ctx, P, X, ln, 0, TPC)
                if _kstage == "ln":
                    if _os.environ.get("KDBG", "") == "ln":
                        emit_store_dbg(ctx, ln, out_fm)
                    else:
                        emit_store_out(ctx, X, out_fm)
                    return nc

                def _ldw(which):
                    def f(g, dma):
                        for c in range(8):
                            dma(out=wsb[c][:], in_=which[c * 128:(c + 1) * 128, :])
                    return f
                ctx.gdma(("t", "v"), _ldw(wq))

                def emit_proj_fm(out_chunks, add_pos):
                    for oc in range(8):
                        for tt in range(4):
                            ps = P["m0"] if (oc + tt) % 2 == 0 else P["m1"]
                            def _mm(t, oc=oc, tt=tt, ps=ps):
                                last = None
                                for dc in range(8):
                                    last = t.matmul(
                                        ps[:, 0:512],
                                        wsb[dc][:, oc * 128:(oc + 1) * 128],
                                        ln[dc][:, tt * 512:(tt + 1) * 512],
                                        start=(dc == 0), stop=(dc == 7))
                                return last
                            ctx.tstep(("v", "dma"), _mm)
                            if add_pos:
                                def _ev(v, oc=oc, tt=tt, ps=ps):
                                    return v.scalar_tensor_tensor(
                                        out_chunks[oc][:, tt * 512:(tt + 1) * 512],
                                        ps[:, 0:512], 1.0,
                                        pos_sb[:, tt * 512:(tt + 1) * 512],
                                        ALU.mult, ALU.add)
                            else:
                                def _ev(v, oc=oc, tt=tt, ps=ps):
                                    return v.tensor_copy(
                                        out_chunks[oc][:, tt * 512:(tt + 1) * 512],
                                        ps[:, 0:512])
                            ctx.vstep(("t",), _ev)

                emit_proj_fm(qsb, True)

                def emit_chunk_maxsq(src, dc, out_row):
                    for half in range(4):
                        def _sq(s_, half=half):
                            return s_.activation(sq[:, 0:512],
                                                 src[:, half * 512:(half + 1) * 512],
                                                 AF.Square)
                        ctx.sstep(("t", "v"), _sq)
                        def _nm(t, half=half):
                            t.matmul(P["n"][0:1, 0:512],
                                     onecol[0:64, 0:1], sq[0:64, 0:512],
                                     start=True, stop=True)
                            return t.matmul(P["row"][0:1, 0:512],
                                            onecol[64:128, 0:1], sq[64:128, 0:512],
                                            start=True, stop=True)
                        ctx.tstep(("s",), _nm)
                        def _mxr(v, half=half):
                            v.tensor_reduce(strow[0:1, 0:1],
                                            P["n"][0:1, 0:512],
                                            AX.X, ALU.max)
                            return v.tensor_reduce(strow[0:1, 1:2],
                                                   P["row"][0:1, 0:512],
                                                   AX.X, ALU.max)
                        ctx.vstep(("t",), _mxr)
                        def _mxc(v, dc=dc, half=half):
                            if half == 0:
                                v.tensor_copy(out_row[0:1, 2 * dc:2 * dc + 1],
                                              strow[0:1, 0:1])
                                return v.tensor_copy(
                                    out_row[0:1, 2 * dc + 1:2 * dc + 2],
                                    strow[0:1, 1:2])
                            v.tensor_max(out_row[0:1, 2 * dc:2 * dc + 1],
                                         out_row[0:1, 2 * dc:2 * dc + 1],
                                         strow[0:1, 0:1])
                            return v.tensor_max(
                                out_row[0:1, 2 * dc + 1:2 * dc + 2],
                                out_row[0:1, 2 * dc + 1:2 * dc + 2],
                                strow[0:1, 1:2])
                        ctx.vstep((), _mxc)

                for dc in range(8):
                    emit_chunk_maxsq(qsb[dc], dc, qrow)

                ctx.gdma(("t", "v"), _ldw(wk))
                for oc in range(8):
                    for tt in range(4):
                        ps = P["m0"] if tt % 2 == 0 else P["m1"]
                        def _mm(t, oc=oc, tt=tt, ps=ps):
                            last = None
                            for dc in range(8):
                                last = t.matmul(
                                    ps[:, 0:512],
                                    wsb[dc][:, oc * 128:(oc + 1) * 128],
                                    ln[dc][:, tt * 512:(tt + 1) * 512],
                                    start=(dc == 0), stop=(dc == 7))
                            return last
                        ctx.tstep(("v", "dma"), _mm)
                        def _ev(v, oc=oc, tt=tt, ps=ps):
                            return v.scalar_tensor_tensor(
                                kch1[:, tt * 512:(tt + 1) * 512],
                                ps[:, 0:512], 1.0,
                                pos_sb[:, tt * 512:(tt + 1) * 512],
                                ALU.mult, ALU.add)
                        ctx.vstep(("t",), _ev)
                    emit_chunk_maxsq(kch1, oc, krow)
                    def _stk(sync, dma, oc=oc):
                        dma(out=cc_k_in[oc * 128:(oc + 1) * 128, :],
                            in_=kch1[:, 0:1024])
                        dma(out=cc_k_in[1024 + oc * 128:1024 + (oc + 1) * 128, :],
                            in_=kch1[:, 1024:2048])
                    ctx.dstep(("v", "s"), _stk)
                def _trn(t):
                    return t.transpose(P["m0"][0:16, 0:1], krow[:, 0:16],
                                       idf[0:1, 0:1])
                ctx.tstep(("v",), _trn)
                def _evn(v):
                    return v.tensor_copy(knorm[:], P["m0"][0:16, 0:1])
                ctx.vstep(("t",), _evn)
                def _stn(sync, dma):
                    dma(out=cc_n_in, in_=knorm[:])
                ctx.dstep(("v",), _stn)

                ctx.gdma(("t", "v"), _ldw(wv))
                def _vones(v):
                    return v.memset(
                        vstage[:].rearrange("p (h c) -> p h c", c=65)[:, :, 64:65],
                        1.0)
                ctx.vstep((), _vones)
                for tc in range(16):
                    for half in range(2):
                        ps = P["m0"] if half == 0 else P["m1"]
                        def _mmv(t, tc=tc, half=half, ps=ps):
                            last = None
                            for dc in range(8):
                                last = t.matmul(
                                    ps[:, 0:512],
                                    ln[dc][:, tc * 128:(tc + 1) * 128],
                                    wsb[dc][:, half * 512:(half + 1) * 512],
                                    start=(dc == 0), stop=(dc == 7))
                            return last
                        ctx.tstep(("dma", "v"), _mmv)
                        def _evv(v, half=half, ps=ps):
                            dst = vstage[:].rearrange("p (h c) -> p h c", c=65)[
                                :, half * 8:(half + 1) * 8, 0:64]
                            return v.tensor_copy(
                                dst, ps[:, 0:512].rearrange("p (h c) -> p h c", c=64))
                        ctx.vstep(("t",), _evv)
                    def _stv(sync, dma, tc=tc):
                        dma(out=ccv_i[tc * 128:(tc + 1) * 128, :], in_=vstage[:])
                    ctx.dstep(("v",), _stv)

                def _stq(sync, dma):
                    for hp in range(8):
                        dma(out=q_d[hp * 128:(hp + 1) * 128, :],
                            in_=qsb[hp][:])
                ctx.dstep(("v", "s"), _stq)

                if _kstage == "qkv":
                    if _os.environ.get("KDBG", "") == "q":
                        emit_store_dbg(ctx, qsb, out_fm)
                    else:
                        emit_store_out(ctx, X, out_fm)
                    return nc
                def _cc1(g):
                    ctx.inc(g.collective_compute(
                        "AllGather", ALU.bypass, replica_groups=RG,
                        ins=[cc_k_in.opt()], outs=[cc_k_out.opt()]), "cc")
                    ctx.inc(g.collective_compute(
                        "AllGather", ALU.bypass, replica_groups=RG,
                        ins=[ccv_i.opt()], outs=[ccv_o.opt()]), "cc")
                    ctx.inc(g.collective_compute(
                        "AllGather", ALU.bypass, replica_groups=RG,
                        ins=[cc_n_in.opt()], outs=[cc_n_out.opt()]), "cc")
                    return None
                ctx.step("gpsimd", ("dma",), _cc1)

                if _kstage == "cc":
                    def _ccw(g):
                        ctx.wait(g, "cc")
                        return None
                    ctx.step("gpsimd", ("cc",), _ccw)
                    emit_store_out(ctx, X, out_fm)
                    return nc
                def _ldkn(sync, dma):
                    dma(out=ktmp[:],
                        in_=cc_n_out.rearrange("(r h) c -> h (r c)", r=8),
                        allow_slow_non_contiguous=True)
                ctx.dstep(("cc",), _ldkn)
                if _kstage == "ldkn":
                    emit_store_out(ctx, X, out_fm)
                    return nc
                def _b1(v):
                    return v.tensor_reduce(ktmp[:, 0:1], ktmp[:], AX.X, ALU.max)
                ctx.vstep(("dma",), _b1)
                def _b2(t):
                    return t.transpose(P["m0"][0:1, 0:16], ktmp[0:16, 0:1],
                                       idf[0:16, 0:16])
                ctx.tstep(("v",), _b2)
                def _b3a(v):
                    return v.tensor_copy(krow[:, 0:16], P["m0"][0:1, 0:16])
                ctx.vstep(("t",), _b3a)
                def _b3b(v):
                    return v.tensor_mul(krow[:, 0:16], krow[:, 0:16], qrow[:, 0:16])
                ctx.vstep((), _b3b)
                def _b4(s):
                    return s.activation(krow[:, 0:16], krow[:, 0:16], AF.Sqrt)
                ctx.sstep(("v",), _b4)
                def _b5(v):
                    last = None
                    for i in range(16):
                        last = v.tensor_scalar(brow[:, i * 16:(i + 1) * 16],
                                               krow[:, 0:16], -1.0 / 32.0, None,
                                               ALU.mult)
                    return last
                ctx.vstep(("s",), _b5)
                def _b6(t):
                    return t.matmul(P["b0"][:, 0:256], onesf[0:1, 0:128], brow[:],
                                    start=True, stop=True)
                ctx.tstep(("v",), _b6)
                def _b7(v):
                    v.tensor_copy(livetab[:], P["b0"][:, 0:16])
                    return v.tensor_add(btab[:], P["b0"][:, 0:256], dead_sb[:])
                ctx.vstep(("t",), _b7)

            if _kstage == "bias":
                emit_store_out(ctx, X, out_fm)
                return nc
            ctx.barrier()

            with contextlib.ExitStack() as pp:
                SA = pp.enter_context(nc.psum_tensor("p_sa", [128, 1024], F32))
                SB = pp.enter_context(nc.psum_tensor("p_sb", [128, 1024], F32))
                OA = pp.enter_context(nc.psum_tensor("p_oa", [128, 512], F32))
                OB = pp.enter_context(nc.psum_tensor("p_ob", [128, 512], F32))
                U6 = pp.enter_context(nc.psum_tensor("p_u6", [128, 512], F32))
                U7 = pp.enter_context(nc.psum_tensor("p_u7", [128, 512], F32))
                kkb = [pp.enter_context(
                    nc.sbuf_tensor(f"kkb{p}", [128, 8 * 1024], BF16))
                    for p in range(2)]
                vkb = [pp.enter_context(
                    nc.sbuf_tensor(f"vkb{p}", [128, 8 * 1040], BF16))
                    for p in range(2)]
                ptile = pp.enter_context(nc.sbuf_tensor("ptile", [128, 2048], BF16))
                qg = [pp.enter_context(nc.sbuf_tensor(f"qg{c}", [128, 512], BF16))
                      for c in range(8)]
                acc = pp.enter_context(nc.sbuf_tensor("acc", [65, NH * 512], F32))
                nrmst = pp.enter_context(nc.sbuf_tensor("nrmst", [128, 512], BF16))

                def kload(slot_src, base, pb):
                    def f(sync, dma):
                        for c in range(8):
                            dma(out=kkb[pb][:, c * 1024:(c + 1) * 1024],
                                in_=cc_k_out[base + c * 128:base + (c + 1) * 128, :]
                                if slot_src == "g" else
                                cc_k_in[base + c * 128:base + (c + 1) * 128, :])
                            dma(out=vkb[pb][:, c * 1040:(c + 1) * 1040],
                                in_=ccv_o[base + c * 128:base + (c + 1) * 128, :]
                                if slot_src == "g" else
                                ccv_i[base + c * 128:base + (c + 1) * 128, :])
                    return f

                def kload_args(qb, pkind, kp):
                    if pkind == "diag":
                        return ("o", qb * 1024, ("t", "s"))
                    slot = 2 * kp if qb == 0 else 2 * (7 - kp) + 1
                    return ("g", slot * 1024, ("t", "s", "cc"))

                ctx.gdma(("t",), _ldw(wo))

                import os
                _qbr = int(os.environ.get("KQB", "2"))
                _tqr = int(os.environ.get("KTQ", "2"))
                plist0 = [("diag", None)] + [("bulk", kp) for kp in range(7)]
                sched = [(qb, tqh, pk, kp) for qb in range(_qbr)
                         for tqh in range(_tqr) for pk, kp in plist0]
                kv_ready = {}
                src0, base0, ldeps0 = kload_args(sched[0][0], sched[0][2],
                                                 sched[0][3])
                ctx.dstep(ldeps0, kload(src0, base0, 0))
                kv_ready[0] = ctx.snap("dma")
                pidx = 0
                sprev = {0: 0, 1: 0}
                for qb in range(_qbr):
                    for tqh in range(_tqr):
                        tq0 = (qb * 2 + tqh) * 512
                        def _ldqg(sync, dma, tq0=tq0):
                            for hp in range(8):
                                dma(out=qg[hp][:],
                                    in_=q_d[hp * 128:(hp + 1) * 128,
                                            tq0:tq0 + 512])
                        ctx.dstep(("t",), _ldqg)
                        qgsnap = ctx.snap("dma")
                        accsnap = {}
                        def _zacc(v):
                            return v.memset(acc[:], 0.0)
                        ctx.vstep(("t",), _zacc)
                        for pkind, kp in plist0:
                            pb = pidx % 2
                            if pidx + 1 < len(sched):
                                nqb, _ntq, npk, nkp = sched[pidx + 1]
                                srcn, basen, ldepsn = kload_args(nqb, npk, nkp)
                                ctx.dstep(ldepsn,
                                          kload(srcn, basen, (pidx + 1) % 2))
                                kv_ready[pidx + 1] = ctx.snap("dma")
                            if pkind == "diag":
                                tks = list(range(0, 4)) if tqh == 0 else list(range(8))
                            else:
                                tks = list(range(8))
                            kvsnap = max(kv_ready[pidx], qgsnap)
                            pidx += 1
                            for hp in range(8):
                                PO_A = OA if hp % 2 == 0 else U6
                                PO_B = OB if hp % 2 == 0 else U7
                                accw = accsnap.get(hp % 2, 0)
                                qksnap = {}
                                def emit_qk(ti2):
                                    tk2 = tks[ti2]
                                    par2 = tk2 % 2
                                    def _qk(t, hp=hp, tk=tk2, pb=pb,
                                            par=par2):
                                        last = None
                                        for sub in range(2):
                                            ps_s = SA if sub == 0 else SB
                                            last = t.matmul(
                                                ps_s[:, par * 512:(par + 1) * 512],
                                                kkb[pb][sub * 64:sub * 64 + 64,
                                                    hp * 1024 + tk * 128:
                                                    hp * 1024 + (tk + 1) * 128],
                                                qg[hp][sub * 64:sub * 64 + 64,
                                                       0:512],
                                                start=True, stop=True,
                                                tile_position=(sub * 64, 0))
                                        return last
                                    qdeps = (("v",) if (pkind == "diag"
                                                        and ti2 == 0) else ())
                                    ctx.tstep(qdeps, _qk,
                                              waits=[("dma", kvsnap),
                                                     ("s", sprev[par2])])
                                    qksnap[ti2] = ctx.snap("t")
                                emit_qk(0)
                                for ti, tk in enumerate(tks):
                                    par = tk % 2
                                    first = ti == 0
                                    is_last = ti == len(tks) - 1
                                    if not is_last:
                                        emit_qk(ti + 1)
                                    delta = 128 * tk - 512 * tqh
                                    has_partial = (pkind == "diag"
                                                   and 0 <= delta <= 384)
                                    if has_partial:
                                        def _msk(v, par=par, delta=delta):
                                            mi = delta // 128
                                            last = None
                                            for sub in range(2):
                                                ps_s = SA if sub == 0 else SB
                                                last = v.tensor_add(
                                                    ps_s[:, par * 512:(par + 1) * 512],
                                                    ps_s[:, par * 512:(par + 1) * 512],
                                                    dmask_sb[:, mi * 512:(mi + 1) * 512])
                                            return last
                                        ctx.vstep((), _msk,
                                                  waits=[("t", qksnap[ti])])
                                    def _exp(s, hp=hp, par=par,
                                             pkind=pkind, kp=kp, qb=qb):
                                        last = None
                                        for sub in range(2):
                                            h = 2 * hp + sub
                                            ps_s = SA if sub == 0 else SB
                                            if pkind == "diag":
                                                bt, bc = livetab, h
                                            else:
                                                bt, bc = btab, (qb * 8 + kp) * 16 + h
                                            last = s.activation(
                                                ptile[:, sub * 1024 + par * 512:
                                                      sub * 1024 + par * 512 + 512],
                                                ps_s[:, par * 512:(par + 1) * 512],
                                                AF.Exp, bias=bt[:, bc:bc + 1],
                                                scale=1.0 / 32.0)
                                        return last
                                    ctx.sstep(("v",) if has_partial
                                              else (), _exp,
                                              waits=[("t", qksnap[ti])])
                                    sprev[par] = ctx.snap("s")
                                    def _av(t, hp=hp, tk=tk, par=par,
                                            first=first, is_last=is_last,
                                            PO_A=PO_A, PO_B=PO_B, pb=pb):
                                        last = None
                                        for sub in range(2):
                                            ps_o = PO_A if sub == 0 else PO_B
                                            h = 2 * hp + sub
                                            last = t.matmul(
                                                ps_o[0:65, 0:512],
                                                vkb[pb][:, tk * 1040 + h * 65:
                                                    tk * 1040 + (h + 1) * 65],
                                                ptile[:, sub * 1024 + par * 512:
                                                      sub * 1024 + par * 512 + 512],
                                                start=first, stop=is_last)
                                        return last
                                    ctx.tstep(("s",), _av,
                                              waits=[("v", accw)]
                                              if first else ())
                                def _acc(v, hp=hp, PO_A=PO_A, PO_B=PO_B):
                                    last = None
                                    for sub in range(2):
                                        ps_o = PO_A if sub == 0 else PO_B
                                        h = 2 * hp + sub
                                        last = v.tensor_add(
                                            acc[:, h * 512:(h + 1) * 512],
                                            acc[:, h * 512:(h + 1) * 512],
                                            ps_o[0:65, 0:512])
                                    return last
                                ctx.vstep(("t",), _acc)
                                accsnap[hp % 2] = ctx.snap("v")
                        gslots = [SA[:, 0:512], SA[:, 512:1024],
                                  SB[:, 0:512], SB[:, 512:1024]]
                        for ocg in range(2):
                            for h in range(NH):
                                def _rc(v, h=h):
                                    return v.reciprocal(
                                        strow[0:1, 0:512],
                                        acc[64:65, h * 512:(h + 1) * 512])
                                ctx.vstep(("t",), _rc)
                                def _bc(t, h=h):
                                    return t.matmul(U6[0:64, 0:512],
                                                    onesf[0:1, 0:64],
                                                    strow[0:1, 0:512],
                                                    start=True, stop=True)
                                ctx.tstep(("v", "s"), _bc)
                                par = (h % 2) * 64
                                def _nm(v, h=h, par=par):
                                    return v.tensor_mul(
                                        nrmst[par:par + 64, :],
                                        acc[0:64, h * 512:(h + 1) * 512],
                                        U6[0:64, 0:512])
                                ctx.vstep(("t",), _nm)
                                def _wo(t, h=h, ocg=ocg, par=par):
                                    last = None
                                    for oi in range(4):
                                        oc = ocg * 4 + oi
                                        last = t.matmul(
                                            gslots[oi],
                                            wsb[h // 2][par:par + 64,
                                                        oc * 128:(oc + 1) * 128],
                                            nrmst[par:par + 64, :],
                                            start=(h == 0), stop=(h == NH - 1))
                                    return last
                                ctx.tstep(("v", "gd"), _wo)
                            for oi in range(4):
                                oc = ocg * 4 + oi
                                def _upd(v, oc=oc, oi=oi, tq0=tq0):
                                    return v.tensor_add(X[oc][:, tq0:tq0 + 512],
                                                        X[oc][:, tq0:tq0 + 512],
                                                        gslots[oi])
                                ctx.vstep(("t",), _upd)

        if stop_after == "attn_a":
            emit_store_out(ctx, X, out_fm)
            return nc

        ctx.barrier()
        with contextlib.ExitStack() as p2:
            p2.enter_context(nc.named_scope("ph2_ffn1"))
            lnf = [p2.enter_context(nc.sbuf_tensor(f"lnf{c}", [128, 512], BF16))
                   for c in range(8)]
            a1 = [p2.enter_context(nc.sbuf_tensor(f"a1_{c}", [128, 512], BF16))
                  for c in range(32)]
            a2 = [p2.enter_context(nc.sbuf_tensor(f"a2_{c}", [128, 512], BF16))
                  for c in range(32)]
            wff = [[p2.enter_context(
                nc.sbuf_tensor(f"wff{p}_{c}", [128, 512], BF16))
                for c in range(8)] for p in range(2)]
            w1b = [[p2.enter_context(
                nc.sbuf_tensor(f"w1b{p}_{c}", [128, 256], BF16))
                for c in range(32)] for p in range(2)]
            scl = p2.enter_context(nc.sbuf_tensor("scl", [128, 3], F32))
            srow = p2.enter_context(nc.sbuf_tensor("srow", [1, 4], F32))
            with contextlib.ExitStack() as pq:
                P2 = {
                    "row": pq.enter_context(nc.psum_tensor("q_row", [1, 1024], F32)),
                    "b0": pq.enter_context(nc.psum_tensor("q_b0", [128, 512], F32)),
                    "b1": pq.enter_context(nc.psum_tensor("q_b1", [128, 512], F32)),
                    "m0": pq.enter_context(nc.psum_tensor("q_m0", [128, 512], F32)),
                    "m1": pq.enter_context(nc.psum_tensor("q_m1", [128, 512], F32)),
                    "u": pq.enter_context(nc.psum_tensor("q_u", [128, 512], F32)),
                }
                def _lds(sync, dma):
                    dma(out=srow[0:1, 0:1], in_=s0in)
                    dma(out=srow[0:1, 1:2], in_=s1in)
                    dma(out=srow[0:1, 2:3], in_=s2in)
                ctx.dstep((), _lds)
                def _s2(v):
                    return v.tensor_scalar(srow[0:1, 0:3], srow[0:1, 0:3], 2.0,
                                           None, ALU.mult)
                ctx.vstep(("dma",), _s2)
                def _sbc(t):
                    return t.matmul(P2["u"][:, 0:3], onesf[0:1, 0:128],
                                    srow[0:1, 0:3], start=True, stop=True)
                ctx.tstep(("v",), _sbc)
                def _scp(v):
                    return v.tensor_copy(scl[:], P2["u"][:, 0:3])
                ctx.vstep(("t",), _scp)

                emit_ffn(ctx, nc, X, out_fm, (w0h_d, w1_d, w2_d), scl, P2,
                         {"lnf": lnf, "a1": a1, "a2": a2, "wff": wff, "w1b": w1b})

        if stop_after == "ffn1":
            emit_store_out(ctx, X, out_fm)
            return nc

        ctx.barrier()
        import os as _os3
        KD3 = _os3.environ.get("KDBG3", "")
        with contextlib.ExitStack() as p3:
            p3.enter_context(nc.named_scope("ph3_attnb"))
            y_tl = [p3.enter_context(nc.sbuf_tensor(f"yt{c}", [128, 512], BF16))
                    for c in range(8)]
            qret = p3.enter_context(nc.sbuf_tensor("qret", [64, TPC], BF16))
            nw_sb = p3.enter_context(nc.sbuf_tensor("nw_sb", [128, 8], F32))
            cw_sb = p3.enter_context(nc.sbuf_tensor("cw_sb", [128, 72], F32))
            cb_sb = p3.enter_context(nc.sbuf_tensor("cb_sb", [128, 18], F32))
            nw2_sb = p3.enter_context(nc.sbuf_tensor("nw2_sb", [128, 16], F32))
            dtb_sb = p3.enter_context(nc.sbuf_tensor("dtb_sb", [NHEADS, 1], F32))
            alog_sb = p3.enter_context(nc.sbuf_tensor("alog_sb", [NHEADS, 1], F32))
            aneg_sb = p3.enter_context(nc.sbuf_tensor("aneg_sb", [NHEADS, 1], F32))
            dcap_sb = p3.enter_context(nc.sbuf_tensor("dcap_sb", [NHEADS, 1], F32))
            drep = p3.enter_context(nc.sbuf_tensor("drep", [128, NHEADS], F32))
            wsc1 = [p3.enter_context(nc.sbuf_tensor(f"wsc1_{c}", [128, 256], BF16))
                    for c in range(8)]
            wsc2 = p3.enter_context(nc.sbuf_tensor("wsc2", [128, 2], BF16))
            wsumm = [p3.enter_context(nc.sbuf_tensor(f"wsm{c}", [128, 64], BF16))
                     for c in range(8)]
            wqw = [p3.enter_context(nc.sbuf_tensor(f"wqw{c}", [128, 64], BF16))
                   for c in range(8)]
            onesq = p3.enter_context(nc.sbuf_tensor("onesq", [128, 128], BF16))
            ones32 = p3.enter_context(nc.sbuf_tensor("ones32", [32, 256], F32))
            trin_sb = p3.enter_context(nc.sbuf_tensor("trin_sb", [128, 512], BF16))

            if KD3 == "alloc":
                emit_store_out(ctx, X, out_fm)
                return nc
            def _ldsmall(sync, dma):
                dma(out=nw_sb[:], in_=ab_norm_w)
                dma(out=cw_sb[:], in_=ab_conv_w)
                dma(out=cb_sb[:], in_=ab_conv_b)
                dma(out=nw2_sb[:], in_=ab_ssm_norm_w)
                dma(out=dtb_sb[:], in_=ab_dt_bias)
                dma(out=alog_sb[:], in_=ab_A_log)
                dma(out=dcap_sb[:], in_=ab_D)
            if _os3.environ.get("K3A", "") != "skip":
                ctx.dstep((), _ldsmall)

            def _ldg2(g, dma):
                grp = _os3.environ.get("K3G", "all")
                if grp in ("all", "trin"):
                    dma(out=trin_sb[:], in_=trineg)
                if grp in ("all", "wsc2"):
                    dma(out=wsc2[:], in_=ab_scorer_w2)
                for c in range(8):
                    if grp in ("all", "wsc1"):
                        dma(out=wsc1[c][:], in_=ab_scorer_w1[c * 128:(c + 1) * 128, :])
                    if grp in ("all", "wsm"):
                        dma(out=wsumm[c][:], in_=ab_summ_w[c * 128:(c + 1) * 128, :])
                        dma(out=wqw[c][:], in_=ab_q_w[c * 128:(c + 1) * 128, :])
            if _os3.environ.get("K3B", "") != "skip":
                ctx.gdma((), _ldg2)

            def _mset(v):
                v.memset(onesq[:], 1.0)
                return v.memset(ones32[:], 1.0)
            if _os3.environ.get("K3C", "") != "skip":
                ctx.vstep((), _mset)

            xh = p3.enter_context(nc.sbuf_tensor("xh", [128, 48], F32))
            lnh = [p3.enter_context(nc.sbuf_tensor(f"lnh{c}", [128, 6], BF16))
                   for c in range(8)]
            xnh = [p3.enter_context(nc.sbuf_tensor(f"xnh{c}", [128, 6], BF16))
                   for c in range(8)]
            hmask = p3.enter_context(nc.sbuf_tensor("hmask", [128, 2], F32))
            hoffs = p3.enter_context(nc.sbuf_tensor("hoffs", [128, 16], U32))

            def _tst(sync, dma):
                dma(out=hmask[:], in_=halo_mask)
                dma(out=hoffs[:], in_=halo_offs)
                for c in range(8):
                    dma(out=cc_t_in[c * 128:(c + 1) * 128, :],
                        in_=X[c][:, 1021:1024])
                    dma(out=cc_t_in[1024 + c * 128:1024 + (c + 1) * 128, :],
                        in_=X[c][:, 2045:2048])
            ctx.dstep(("v",), _tst)

            if KD3 == "pre3":
                emit_store_out(ctx, X, out_fm)
                return nc
            def _cc2(g):
                ctx.inc(g.collective_compute(
                    "AllGather", ALU.bypass, replica_groups=RG,
                    ins=[cc_t_in.opt()], outs=[cc_t_out.opt()]), "cc")
                return None
            ctx.step("gpsimd", ("dma",), _cc2)

            def _hgather(g):
                for c in range(8):
                    i0 = g.indirect_dma_start(
                        out=xh[:, c * 6:c * 6 + 3], out_offset=None,
                        in_=cc_t_out,
                        in_offset=bass.IndirectOffsetOnAxis(
                            ap=hoffs[:, c:c + 1], axis=0))
                    i0.then_inc(ctx.sem["gd"], 16)
                    ctx.cnt["gd"] += 16
                    i1 = g.indirect_dma_start(
                        out=xh[:, c * 6 + 3:c * 6 + 6], out_offset=None,
                        in_=cc_t_out,
                        in_offset=bass.IndirectOffsetOnAxis(
                            ap=hoffs[:, 8 + c:9 + c], axis=0))
                    i1.then_inc(ctx.sem["gd"], 16)
                    ctx.cnt["gd"] += 16
                return None
            ctx.step("gpsimd", ("cc",), _hgather)

            with contextlib.ExitStack() as pq:
                P3 = {
                    "row": pq.enter_context(nc.psum_tensor("r_row", [1, 1024], F32)),
                    "b0": pq.enter_context(nc.psum_tensor("r_b0", [128, 512], F32)),
                    "b1": pq.enter_context(nc.psum_tensor("r_b1", [128, 512], F32)),
                    "m0": pq.enter_context(nc.psum_tensor("r_m0", [128, 512], F32)),
                    "m1": pq.enter_context(nc.psum_tensor("r_m1", [128, 512], F32)),
                }
                PSB3 = pq.enter_context(nc.psum_tensor("r_psb", [128, 1024], BF16))

                xhc = [xh[:, c * 6:(c + 1) * 6] for c in range(8)]
                emit_layernorm(ctx, P3, xhc, lnh, 0, 6)
                for dc in range(8):
                    def _hsq(s_, dc=dc):
                        return s_.activation(sq[:, 0:6], lnh[dc][:], AF.Square)
                    ctx.sstep(("t", "v", "dma"), _hsq)
                    def _hmm(t, dc=dc):
                        return t.matmul(P3["row"][0:1, 0:6], onecol[:], sq[:, 0:6],
                                        start=(dc == 0), stop=(dc == 7))
                    ctx.tstep(("s",), _hmm)
                def _hst(v):
                    return v.tensor_scalar(strow[0:1, 0:6], P3["row"][0:1, 0:6],
                                           1.0 / D, None, ALU.mult)
                ctx.vstep(("t",), _hst)
                epsc2 = ctx.constcell(EPS_LN)
                def _hsq2(s_):
                    return s_.activation(strow[0:1, 0:6], strow[0:1, 0:6],
                                         AF.Sqrt, bias=epsc2)
                ctx.sstep(("v",), _hsq2)
                def _hrc(v):
                    return v.reciprocal(strow[0:1, 0:6], strow[0:1, 0:6])
                ctx.vstep(("s",), _hrc)
                def _hbc(t):
                    return t.matmul(P3["b0"][:, 0:6], onesf[0:1, 0:128],
                                    strow[0:1, 0:6], start=True, stop=True)
                ctx.tstep(("v",), _hbc)
                for dc in range(8):
                    def _hap(v, dc=dc):
                        return v.scalar_tensor_tensor(
                            xnh[dc][:], lnh[dc][:], nw_sb[:, dc:dc + 1],
                            P3["b0"][:, 0:6], ALU.mult, ALU.mult)
                    ctx.vstep(("t",), _hap)

                if KD3 == "halo":
                    emit_store_out(ctx, X, out_fm)
                    return nc
                xn = [p3.enter_context(nc.sbuf_tensor(f"xn{c}", [128, 512], BF16))
                      for c in range(8)]
                zs = [p3.enter_context(nc.sbuf_tensor(f"zs{c}", [128, 512], BF16))
                      for c in range(16)]
                p3b = contextlib.ExitStack()
                xc = [p3b.enter_context(nc.sbuf_tensor(f"xc{c}", [128, 520], BF16))
                      for c in range(18)]
                stash = p3b.enter_context(nc.sbuf_tensor("stash", [128, 54], BF16))
                wip = [p3b.enter_context(nc.sbuf_tensor(f"wip{c}", [128, 512], BF16))
                       for c in range(8)]
                accD = p3b.enter_context(nc.sbuf_tensor("accD", [128, 512], F32))
                accP = p3b.enter_context(nc.sbuf_tensor("accP", [128, 512], F32))
                dtraw = p3b.enter_context(nc.sbuf_tensor("dtraw", [32, 512], F32))
                dtw = p3b.enter_context(nc.sbuf_tensor("dtw", [32, 512], F32))
                dt_sb = p3b.enter_context(nc.sbuf_tensor("dt_sb", [32, 512], F32))
                csf = p3b.enter_context(nc.sbuf_tensor("csf", [32, 512], F32))
                decf = p3b.enter_context(nc.sbuf_tensor("decf", [32, 256], F32))
                ddf = p3b.enter_context(nc.sbuf_tensor("ddf", [32, 256], F32))
                dssum = p3b.enter_context(nc.sbuf_tensor("dssum", [32, 8], F32))
                pzx = p3b.enter_context(nc.sbuf_tensor("pzx", [32, 8], F32))
                arow = p3b.enter_context(nc.sbuf_tensor("arow", [1, 32], F32))
                abc = p3b.enter_context(nc.sbuf_tensor("abc", [128, 32], F32))
                cstm = p3b.enter_context(nc.sbuf_tensor("cstm", [128, 64], F32))
                ncstm = p3b.enter_context(nc.sbuf_tensor("ncstm", [128, 64], F32))
                exptm = p3b.enter_context(nc.sbuf_tensor("exptm", [128, 64], F32))
                dttm = p3b.enter_context(nc.sbuf_tensor("dttm", [128, 64], F32))
                ddtm = p3b.enter_context(nc.sbuf_tensor("ddtm", [128, 64], F32))
                bmt = p3b.enter_context(nc.sbuf_tensor("bmt", [128, 256], BF16))
                cbts = p3b.enter_context(nc.sbuf_tensor("cbts", [128, 512], BF16))
                xstm = [p3b.enter_context(
                    nc.sbuf_tensor(f"xstm{p}", [128, 256], BF16))
                    for p in range(2)]
                xdbuf = [p3b.enter_context(
                    nc.sbuf_tensor(f"xdbuf{p}", [128, 128], BF16))
                    for p in range(2)]
                xddbuf = p3b.enter_context(nc.sbuf_tensor("xddbuf", [128, 128], BF16))
                colw = [p3b.enter_context(
                    nc.sbuf_tensor(f"colw{p}", [128, 128], F32))
                    for p in range(2)]
                mseg = [p3b.enter_context(
                    nc.sbuf_tensor(f"mseg{p}", [128, 512], F32))
                    for p in range(2)]
                expsg = [p3b.enter_context(
                    nc.sbuf_tensor(f"expsg{p}", [128, 512], BF16))
                    for p in range(2)]
                msM = [p3b.enter_context(
                    nc.sbuf_tensor(f"msM{p}", [128, 512], BF16))
                    for p in range(2)]
                cms = [p3b.enter_context(
                    nc.sbuf_tensor(f"cms{p}", [128, 256], BF16))
                    for p in range(2)]
                tybuf = [p3b.enter_context(
                    nc.sbuf_tensor(f"tybuf{p}", [128, 256], BF16))
                    for p in range(2)]
                Lrun = p3b.enter_context(nc.sbuf_tensor("Lrun", [128, 2048], BF16))
                Lg = p3b.enter_context(nc.sbuf_tensor("Lg", [128, 2048], BF16))
                Sinb = p3b.enter_context(nc.sbuf_tensor("Sinb", [128, 2048], BF16))
                prevw = p3b.enter_context(nc.sbuf_tensor("prevw", [128, 2048], BF16))
                pzbc = p3b.enter_context(nc.sbuf_tensor("pzbc", [128, 32], F32))
                wop = [p3b.enter_context(nc.sbuf_tensor(f"wop{c}", [128, 128], BF16))
                       for c in range(4)]
                sumst = p3b.enter_context(nc.sbuf_tensor("sumst", [128, 64], F32))
                scrow = p3b.enter_context(nc.sbuf_tensor("scrow", [1, 512], F32))
                s1b = [p3b.enter_context(nc.sbuf_tensor(f"s1b{c}", [128, 512], BF16))
                       for c in range(2)]
                dagT = p3b.enter_context(nc.sbuf_tensor("dagT", [8, 32], F32))
                wqr = p3b.enter_context(nc.sbuf_tensor("wqr", [8, 32], F32))

                def _an1(s_):
                    return s_.activation(aneg_sb[:], alog_sb[:], AF.Exp)
                ctx.sstep(("dma",), _an1)
                def _an2(v):
                    return v.tensor_scalar(aneg_sb[:], aneg_sb[:], -1.0, None,
                                           ALU.mult)
                ctx.vstep(("s",), _an2)
                def _dr1(t):
                    return t.transpose(P3["m1"][0:1, 0:32], dcap_sb[:],
                                       idf[0:32, 0:32])
                ctx.tstep(("dma", "v"), _dr1)
                def _dr2(v):
                    return v.tensor_copy(arow[:], P3["m1"][0:1, 0:32])
                ctx.vstep(("t",), _dr2)
                def _dr3(t):
                    return t.matmul(P3["m1"][:, 0:32], onesf[0:1, 0:128], arow[:],
                                    start=True, stop=True)
                ctx.tstep(("v",), _dr3)
                def _dr4(v):
                    return v.tensor_copy(drep[:], P3["m1"][:, 0:32])
                ctx.vstep(("t",), _dr4)

                def emit_chunk(ti, ci, mode):
                    qb, tloc = ti // 2, ti % 2
                    z = 2 * tloc + ci
                    c0 = ci * 256
                    if mode == 1 and z >= 1:
                        slot = qb * 3 + (z - 1)
                        def _lzs(sync, dma, slot=slot):
                            dma(out=lz_d[:, slot * 2048:(slot + 1) * 2048],
                                in_=Lrun[:])
                        ctx.dstep(("v",), _lzs)
                    if mode == 2 and z >= 1:
                        slot = qb * 3 + (z - 1)
                        def _lzl(sync, dma, slot=slot):
                            dma(out=Lg[:],
                                in_=lz_d[:, slot * 2048:(slot + 1) * 2048])
                        ctx.dstep(("v",), _lzl)
                    for lc in range(2):
                        def _ct(t, lc=lc, c0=c0):
                            return t.transpose(
                                P3["m1"][0:128, lc * 32:(lc + 1) * 32],
                                csf[0:32, c0 + lc * 128:c0 + (lc + 1) * 128],
                                idf[0:32, 0:32])
                        ctx.tstep(("v", "s"), _ct)
                    def _cte1(v):
                        return v.tensor_copy(cstm[:], P3["m1"][0:128, 0:64])
                    ctx.vstep(("t",), _cte1)
                    def _cte2(v):
                        return v.tensor_scalar(ncstm[:], P3["m1"][0:128, 0:64],
                                               -1.0, None, ALU.mult)
                    ctx.vstep((), _cte2)
                    if mode == 2 and _os3.environ.get("K3X", "") != "noetm":
                        def _etm(s_):
                            return s_.activation(exptm[:], cstm[:], AF.Exp)
                        ctx.sstep(("v",), _etm)
                    def _dec(s_, c0=c0):
                        return s_.activation(
                            decf[:], csf[:, c0:c0 + 256], AF.Exp, scale=-1.0,
                            bias=csf[:, c0 + 255:c0 + 256])
                    ctx.sstep(("v",), _dec)
                    def _dd(v, c0=c0):
                        return v.tensor_mul(ddf[:], dt_sb[:, c0:c0 + 256],
                                            decf[:])
                    ctx.vstep(("s",), _dd)
                    for lc in range(2):
                        def _dt1(t, lc=lc):
                            return t.transpose(
                                P3["m1"][0:128, 64 + lc * 32:64 + (lc + 1) * 32],
                                ddf[0:32, lc * 128:(lc + 1) * 128],
                                idf[0:32, 0:32])
                        ctx.tstep(("v", "s"), _dt1)
                    def _dte(v):
                        return v.tensor_copy(ddtm[:], P3["m1"][0:128, 64:128])
                    ctx.vstep(("t",), _dte)
                    if mode == 2:
                        for lc in range(2):
                            def _dt2(t, lc=lc, c0=c0):
                                return t.transpose(
                                    P3["m1"][0:128, lc * 32:(lc + 1) * 32],
                                    dt_sb[0:32, c0 + lc * 128:c0 + (lc + 1) * 128],
                                    idf[0:32, 0:32])
                            ctx.tstep(("v", "s"), _dt2)
                        def _dte2(v):
                            return v.tensor_copy(dttm[:], P3["m1"][0:128, 0:64])
                        ctx.vstep(("t",), _dte2)
                    for lc in range(2):
                        def _bt(t, lc=lc, c0=c0):
                            return t.transpose(
                                PSB3[:, lc * 128:(lc + 1) * 128],
                                xc[16][:, 4 + c0 + lc * 128:4 + c0 + (lc + 1) * 128],
                                idb[:])
                        ctx.tstep(("v", "s"), _bt)
                    def _bte(v):
                        return v.tensor_copy(bmt[:], PSB3[:, 0:256])
                    ctx.vstep(("t",), _bte)
                    if mode == 1:
                        def _aex(s_, c0=c0):
                            return s_.activation(dtw[:, 0:1],
                                                 csf[:, c0 + 255:c0 + 256],
                                                 AF.Exp)
                        ctx.sstep(("v",), _aex)
                        def _atr(t):
                            return t.transpose(P3["m1"][0:1, 0:32],
                                               dtw[0:32, 0:1], idf[0:32, 0:32])
                        ctx.tstep(("s",), _atr)
                        def _are(v):
                            return v.tensor_copy(arow[:], P3["m1"][0:1, 0:32])
                        ctx.vstep(("t",), _are)
                        def _abc(t):
                            return t.matmul(P3["m1"][:, 0:32], onesf[0:1, 0:128],
                                            arow[:], start=True, stop=True)
                        ctx.tstep(("v",), _abc)
                        def _abe(v):
                            return v.tensor_copy(abc[:], P3["m1"][:, 0:32])
                        ctx.vstep(("t",), _abe)
                    if mode == 2:
                        _skip_cb = _os3.environ.get("K3X", "") == "nocb"
                        for sc in range(2 if not _skip_cb else 0):
                            def _cb(t, sc=sc, c0=c0):
                                return t.matmul(
                                    P3["m0"][:, sc * 256:(sc + 1) * 256],
                                    xc[16][:, 4 + c0 + sc * 128:4 + c0 + (sc + 1) * 128],
                                    xc[17][:, 4 + c0:4 + c0 + 256],
                                    start=True, stop=True)
                            ctx.tstep(("v", "s"), _cb)
                        if not _skip_cb:
                            def _cbe(v):
                                return v.tensor_copy(cbts[:], P3["m0"][:, 0:512])
                            ctx.vstep(("t",), _cbe)
                        Sin = Sinb
                        if _os3.environ.get("K3X", "") == "nopv":
                            pass
                        elif z == 0:
                            def _pv0(v, Sin=Sin):
                                return v.tensor_copy(prevw[:], Sin[:])
                            ctx.vstep(("cc",), _pv0)
                        else:
                            slot = qb * 3 + (z - 1)
                            def _pz1(t, qb=qb, z=z):
                                return t.transpose(
                                    P3["m1"][0:1, 0:32],
                                    pzx[0:32, qb * 4 + z:qb * 4 + z + 1],
                                    idf[0:32, 0:32])
                            ctx.tstep(("v", "s"), _pz1)
                            def _pz2(v):
                                return v.tensor_copy(arow[:], P3["m1"][0:1, 0:32])
                            ctx.vstep(("t",), _pz2)
                            def _pz3(t):
                                return t.matmul(P3["m1"][:, 0:32],
                                                onesf[0:1, 0:128], arow[:],
                                                start=True, stop=True)
                            ctx.tstep(("v",), _pz3)
                            def _pz4(v):
                                return v.tensor_copy(pzbc[:], P3["m1"][:, 0:32])
                            ctx.vstep(("t",), _pz4)
                            for h in range(NHEADS):
                                def _pv(v, h=h, Sin=Sin):
                                    return v.scalar_tensor_tensor(
                                        prevw[:, h * 64:(h + 1) * 64],
                                        Sin[:, h * 64:(h + 1) * 64],
                                        pzbc[:, h:h + 1],
                                        Lg[:, h * 64:(h + 1) * 64],
                                        ALU.mult, ALU.add)
                                ctx.vstep(("dma",), _pv)

                    if mode == 1:
                        for hp in range(16):
                            for lc in range(2):
                                def _xt(t, hp=hp, lc=lc, c0=c0):
                                    return t.transpose(
                                        PSB3[:, 256 + lc * 128:256 + (lc + 1) * 128],
                                        xc[hp][:, 4 + c0 + lc * 128:4 + c0 + (lc + 1) * 128],
                                        idb[:])
                                ctx.tstep(("v", "s", "g"), _xt)
                            def _xte(v, hp=hp):
                                return v.tensor_copy(xstm[0][:], PSB3[:, 256:512])
                            ctx.vstep(("t",), _xte)
                            for sub in range(2):
                                h = 2 * hp + sub
                                def _xd(g, sub=sub, h=h):
                                    last = None
                                    for lc in range(2):
                                        last = g.tensor_scalar(
                                            xddbuf[:, lc * 64:(lc + 1) * 64],
                                            xstm[0][:, lc * 128 + sub * 64:
                                                 lc * 128 + sub * 64 + 64],
                                            ddtm[:, lc * 32 + h:lc * 32 + h + 1],
                                            None, ALU.mult)
                                    return last
                                ctx.vstep(("t",), _xd)
                                def _st(t, h=h):
                                    last = None
                                    for lc in range(2):
                                        last = t.matmul(
                                            P3["b1"][0:128, 0:64],
                                            bmt[:, lc * 128:(lc + 1) * 128],
                                            xddbuf[:, lc * 64:(lc + 1) * 64],
                                            start=(lc == 0), stop=(lc == 1))
                                    return last
                                ctx.tstep(("g", "v"), _st)
                                if z == 0:
                                    def _l0(v, h=h):
                                        return v.tensor_copy(
                                            Lrun[:, h * 64:(h + 1) * 64],
                                            P3["b1"][0:128, 0:64])
                                    ctx.vstep(("t", "dma"), _l0)
                                else:
                                    def _lz(v, h=h):
                                        return v.scalar_tensor_tensor(
                                            Lrun[:, h * 64:(h + 1) * 64],
                                            Lrun[:, h * 64:(h + 1) * 64],
                                            abc[:, h:h + 1],
                                            P3["b1"][0:128, 0:64],
                                            ALU.mult, ALU.add)
                                    ctx.vstep(("t", "dma"), _lz)
                    else:
                        def stages_for(h, c0=c0):
                            p = h % 2
                            hp = h // 2
                            sub = h % 2
                            out = []
                            if sub == 0:
                                for lc in range(2):
                                    def _xt(t, hp=hp, lc=lc, c0=c0):
                                        return t.transpose(
                                            PSB3[:, 256 + lc * 128:
                                                 256 + (lc + 1) * 128],
                                            xc[hp][:, 4 + c0 + lc * 128:
                                                   4 + c0 + (lc + 1) * 128],
                                            idb[:])
                                    out.append(("tensor", ("v", "s", "g"), _xt))
                                def _xte(v, hp=hp):
                                    return v.tensor_copy(xstm[hp % 2][:],
                                                         PSB3[:, 256:512])
                                out.append(("vector", ("t",), _xte))
                            PB = P3["b0"] if p == 0 else P3["m0"]
                            PY = P3["b1"] if p == 0 else P3["m1"]
                            def _xd2(g, h=h, hp=hp, sub=sub, p=p):
                                last = None
                                for lc in range(2):
                                    last = g.tensor_scalar(
                                        xdbuf[p][:, lc * 64:(lc + 1) * 64],
                                        xstm[hp % 2][:, lc * 128 + sub * 64:
                                             lc * 128 + sub * 64 + 64],
                                        dttm[:, lc * 32 + h:lc * 32 + h + 1],
                                        None, ALU.mult)
                                return last
                            out.append(("vector", ("t",), _xd2))
                            for lh in range(2):
                                def _cw1(g, lh=lh, h=h, p=p):
                                    return g.tensor_scalar(
                                        colw[p][:], onesq[:],
                                        cstm[:, lh * 32 + h:lh * 32 + h + 1],
                                        None, ALU.mult)
                                out.append(("vector", ("t",), _cw1))
                                def _cwt(t, lh=lh, p=p, PB=PB):
                                    return t.transpose(
                                        PB[:, lh * 128:(lh + 1) * 128],
                                        colw[p][:], idf[:])
                                out.append(("tensor", ("v",), _cwt))
                                def _cw2(g, lh=lh, h=h, p=p):
                                    return g.tensor_scalar(
                                        colw[p][:], onesq[:],
                                        exptm[:, lh * 32 + h:lh * 32 + h + 1],
                                        None, ALU.mult)
                                out.append(("vector", ("t",), _cw2))
                                def _cwt2(t, lh=lh, p=p, PB=PB):
                                    return t.transpose(
                                        PB[:, 256 + lh * 128:
                                           256 + (lh + 1) * 128],
                                        colw[p][:], idf[:])
                                out.append(("tensor", ("v",), _cwt2))
                            for sc in range(2):
                                def _mk(v, sc=sc, p=p, PB=PB):
                                    return v.tensor_add(
                                        mseg[p][:, sc * 256:(sc + 1) * 256],
                                        PB[:, 0:256],
                                        trin_sb[:, sc * 256:(sc + 1) * 256])
                                out.append(("vector", ("t",), _mk))
                                def _ex(s_, sc=sc, h=h, p=p):
                                    return s_.activation(
                                        expsg[p][:, sc * 256:(sc + 1) * 256],
                                        mseg[p][:, sc * 256:(sc + 1) * 256],
                                        AF.Exp,
                                        bias=ncstm[:, sc * 32 + h:
                                                   sc * 32 + h + 1])
                                out.append(("scalar", ("v",), _ex))
                                def _mm(g, sc=sc, p=p):
                                    return g.tensor_mul(
                                        msM[p][:, sc * 256:(sc + 1) * 256],
                                        expsg[p][:, sc * 256:(sc + 1) * 256],
                                        cbts[:, sc * 256:(sc + 1) * 256])
                                out.append(("gpsimd", ("s", "v"), _mm))
                            def _cm(v, c0=c0, p=p, PB=PB):
                                return v.tensor_mul(
                                    cms[p][:], xc[17][:, 4 + c0:4 + c0 + 256],
                                    PB[:, 256:512])
                            out.append(("vector", ("t",), _cm))
                            def _ym(t, h=h, p=p, PY=PY):
                                t.matmul(PY[0:64, 0:256],
                                         xdbuf[p][:, 0:64], msM[p][:, 0:256],
                                         start=True, stop=False)
                                t.matmul(PY[0:64, 0:256],
                                         xdbuf[p][:, 64:128], msM[p][:, 256:512],
                                         start=False, stop=False)
                                return t.matmul(
                                    PY[0:64, 0:256],
                                    prevw[:, h * 64:(h + 1) * 64], cms[p][:],
                                    start=False, stop=True)
                            out.append(("tensor", ("g", "v"), _ym))
                            par = sub * 64
                            def _ye1(v, hp=hp, h=h, c0=c0, par=par, p=p, PY=PY):
                                return v.scalar_tensor_tensor(
                                    tybuf[p][par:par + 64, :],
                                    xc[hp][par:par + 64, 4 + c0:4 + c0 + 256],
                                    drep[par:par + 64, h:h + 1],
                                    PY[0:64, 0:256],
                                    ALU.mult, ALU.add)
                            out.append(("vector", ("t",), _ye1))
                            def _ye2(v, hp=hp, c0=c0, par=par, p=p):
                                return v.tensor_mul(
                                    zs[hp][par:par + 64, c0:c0 + 256],
                                    tybuf[p][par:par + 64, :],
                                    zs[hp][par:par + 64, c0:c0 + 256])
                            out.append(("vector", (), _ye2))
                            return out

                        for h0 in range(0, NHEADS, 2):
                            sa = stages_for(h0)
                            sb = [None] * 3 + stages_for(h0 + 1)
                            n = max(len(sa), len(sb))
                            for k in range(n):
                                if k < len(sa):
                                    eng, deps, fn = sa[k]
                                    ctx.step(eng, deps, fn,
                                             ctx._OWN.get(eng, "v"))
                                if k < len(sb) and sb[k] is not None:
                                    eng, deps, fn = sb[k]
                                    ctx.step(eng, deps, fn,
                                             ctx._OWN.get(eng, "v"))
                    if mode == 1 and z == 3:
                        def _lst(sync, dma, qb=qb):
                            dma(out=cc_s_in[:, qb * 2048:(qb + 1) * 2048],
                                in_=Lrun[:])
                        ctx.dstep(("v",), _lst)

                def emit_tile_post(ti):
                    qb, tloc = ti // 2, ti % 2
                    t0 = ti * 512
                    for fc in range(16):
                        def _ysq(s_, fc=fc):
                            return s_.activation(sq[:, 0:512], zs[fc][:],
                                                 AF.Square)
                        ctx.sstep(("t", "v", "dma"), _ysq)
                        def _ymm(t, fc=fc):
                            return t.matmul(P3["row"][0:1, 0:512], onecol[:],
                                            sq[:, 0:512],
                                            start=(fc == 0), stop=(fc == 15))
                        ctx.tstep(("s",), _ymm)
                    def _yst(v):
                        return v.tensor_scalar(strow[0:1, 0:512],
                                               P3["row"][0:1, 0:512],
                                               1.0 / D_INNER, None, ALU.mult)
                    ctx.vstep(("t",), _yst)
                    epsc4 = ctx.constcell(EPS_LN)
                    def _ysr(s_):
                        return s_.activation(strow[0:1, 0:512],
                                             strow[0:1, 0:512], AF.Sqrt,
                                             bias=epsc4)
                    ctx.sstep(("v",), _ysr)
                    def _yrc(v):
                        return v.reciprocal(strow[0:1, 512:1024],
                                            strow[0:1, 0:512])
                    ctx.vstep(("s",), _yrc)
                    def _ybc(t):
                        return t.matmul(P3["b0"][:, 0:512], onesf[0:1, 0:128],
                                        strow[0:1, 512:1024],
                                        start=True, stop=True)
                    ctx.tstep(("v",), _ybc)
                    for fc in range(16):
                        def _yap(v, fc=fc):
                            return v.scalar_tensor_tensor(
                                zs[fc][:], zs[fc][:], nw2_sb[:, fc:fc + 1],
                                P3["b0"][:, 0:512], ALU.mult, ALU.mult)
                        ctx.vstep(("t",), _yap)
                    for oc in range(8):
                        for kg in range(4):
                            def _low(sync, dma, oc=oc, kg=kg):
                                for i in range(4):
                                    kc = kg * 4 + i
                                    dma(out=wop[i][:],
                                        in_=ab_outproj[kc * 128:(kc + 1) * 128,
                                                       oc * 128:(oc + 1) * 128])
                            ctx.gdma(("t",), _low)
                            def _om(t, kg=kg):
                                last = None
                                for i in range(4):
                                    last = t.matmul(
                                        P3["m0"][:, 0:512], wop[i][:],
                                        zs[kg * 4 + i][:, 0:512],
                                        start=(kg == 0 and i == 0),
                                        stop=(kg == 3 and i == 3))
                                return last
                            ctx.tstep(("v", "dma", "g"), _om)
                        def _oe(v, oc=oc):
                            return v.tensor_copy(y_tl[oc][:],
                                                 P3["m0"][:, 0:512])
                        ctx.vstep(("t", "dma"), _oe)
                        def _ost(sync, dma, oc=oc, t0=t0):
                            dma(out=y_d[oc * 128:(oc + 1) * 128, t0:t0 + 512],
                                in_=y_tl[oc][:])
                        ctx.dstep(("v",), _ost)
                    for c2 in range(2):
                        def _s1(t, c2=c2, t0=t0):
                            last = None
                            for dc in range(8):
                                last = t.matmul(
                                    P3["m1"][:, 0:512],
                                    wsc1[dc][:, c2 * 128:(c2 + 1) * 128],
                                    y_tl[dc][:],
                                    start=(dc == 0), stop=(dc == 7))
                            return last
                        ctx.tstep(("v",), _s1)
                        def _s1e(v, c2=c2):
                            return v.tensor_scalar(s1b[c2][:], P3["m1"][:, 0:512],
                                                   1.0, 0.0, ALU.mult, ALU.max)
                        ctx.vstep(("t",), _s1e)
                    def _s2m(t):
                        last = None
                        for c2 in range(2):
                            last = t.matmul(P3["row"][0:1, 0:512],
                                            wsc2[:, c2:c2 + 1], s1b[c2][:],
                                            start=(c2 == 0), stop=(c2 == 1))
                        return last
                    ctx.tstep(("v",), _s2m)
                    def _s2e(s_):
                        return s_.activation(scrow[:], P3["row"][0:1, 0:512],
                                             AF.Sigmoid)
                    ctx.sstep(("t", "dma"), _s2e)
                    def _s2s(sync, dma, t0=t0):
                        dma(out=cc_p_in[64:65, t0:t0 + 512], in_=scrow[:])
                    ctx.dstep(("s",), _s2s)
                    def _sf(t, t0=t0):
                        last = None
                        for dc in range(8):
                            last = t.matmul(P3["m1"][0:64, 0:512], wsumm[dc][:],
                                            y_tl[dc][:],
                                            start=(dc == 0), stop=(dc == 7))
                        return last
                    ctx.tstep(("v",), _sf)
                    def _sfe2(v):
                        return v.tensor_copy(sq[0:64, 0:512],
                                             P3["m1"][0:64, 0:512])
                    ctx.vstep(("t", "dma"), _sfe2)
                    def _sfs(sync, dma, t0=t0):
                        dma(out=cc_p_in[0:64, t0:t0 + 512], in_=sq[0:64, 0:512])
                    ctx.dstep(("v",), _sfs)
                    for tc in range(4):
                        def _stm(t, tc=tc, t0=t0):
                            last = None
                            for dc in range(8):
                                last = t.matmul(
                                    P3["m1"][:, 0:64],
                                    y_tl[dc][:, tc * 128:(tc + 1) * 128],
                                    wsumm[dc][:],
                                    start=(dc == 0), stop=(dc == 7))
                            return last
                        ctx.tstep(("v", "dma"), _stm)
                        def _ste(v):
                            return v.tensor_copy(sumst[:], P3["m1"][:, 0:64])
                        ctx.vstep(("t", "dma"), _ste)
                        def _sts(sync, dma, tc=tc, t0=t0):
                            dma(out=cc_q_in[t0 + tc * 128:t0 + (tc + 1) * 128, :],
                                in_=sumst[:])
                        ctx.dstep(("v",), _sts)
                    def _qm(t, t0=t0):
                        last = None
                        for dc in range(8):
                            last = t.matmul(P3["m1"][0:64, 0:512], wqw[dc][:],
                                            y_tl[dc][:],
                                            start=(dc == 0), stop=(dc == 7))
                        return last
                    ctx.tstep(("v", "dma"), _qm)
                    def _qe(v, t0=t0):
                        return v.tensor_copy(qret[:, t0:t0 + 512],
                                             P3["m1"][0:64, 0:512])
                    ctx.vstep(("t",), _qe)

                def emit_tile(ti, mode):
                    qb, tloc = ti // 2, ti % 2
                    t0 = ti * 512
                    LNP = {"row": P3["row"], "b0": P3["b0"], "b1": P3["b1"]}
                    emit_layernorm(ctx, LNP, X, xn, t0, t0 + 512)
                    for dc in range(8):
                        def _rsq(s_, dc=dc):
                            return s_.activation(sq[:, 0:512], xn[dc][:],
                                                 AF.Square)
                        ctx.sstep(("t", "v", "dma"), _rsq)
                        def _rmm(t, dc=dc):
                            return t.matmul(P3["row"][0:1, 0:512], onecol[:],
                                            sq[:, 0:512],
                                            start=(dc == 0), stop=(dc == 7))
                        ctx.tstep(("s",), _rmm)
                    def _rst(v):
                        return v.tensor_scalar(strow[0:1, 0:512],
                                               P3["row"][0:1, 0:512],
                                               1.0 / D, None, ALU.mult)
                    ctx.vstep(("t",), _rst)
                    epsc3 = ctx.constcell(EPS_LN)
                    def _rsqr(s_):
                        return s_.activation(strow[0:1, 0:512], strow[0:1, 0:512],
                                             AF.Sqrt, bias=epsc3)
                    ctx.sstep(("v",), _rsqr)
                    def _rrc(v):
                        return v.reciprocal(strow[0:1, 512:1024],
                                            strow[0:1, 0:512])
                    ctx.vstep(("s",), _rrc)
                    def _rbc(t):
                        return t.matmul(P3["b0"][:, 0:512], onesf[0:1, 0:128],
                                        strow[0:1, 512:1024],
                                        start=True, stop=True)
                    ctx.tstep(("v",), _rbc)
                    for dc in range(8):
                        def _rap(v, dc=dc):
                            return v.scalar_tensor_tensor(
                                xn[dc][:], xn[dc][:], nw_sb[:, dc:dc + 1],
                                P3["b0"][:, 0:512], ALU.mult, ALU.mult)
                        ctx.vstep(("t",), _rap)

                    if KD3 == "p1ln" and mode == 1:
                        return
                    blocks = [(2048, 512), (2560, 512), (3072, 512),
                              (3584, 512), (4096, 288)]
                    for bs, bw in blocks:
                        def _lwip(sync, dma, bs=bs, bw=bw):
                            for dc in range(8):
                                dma(out=wip[dc][:, 0:bw],
                                    in_=ab_inproj[dc * 128:(dc + 1) * 128,
                                                  bs:bs + bw])
                        ctx.gdma(("t",), _lwip)
                        nout = bw // 128
                        for k in range(nout):
                            col = bs + k * 128
                            ps = P3["m0"] if k % 2 == 0 else P3["m1"]
                            def _pm(t, k=k, ps=ps):
                                last = None
                                for dc in range(8):
                                    last = t.matmul(
                                        ps[:, 0:512],
                                        wip[dc][:, k * 128:(k + 1) * 128],
                                        xn[dc][:, 0:512],
                                        start=(dc == 0), stop=(dc == 7))
                                return last
                            ctx.tstep(("v", "dma"), _pm)
                            c = (col - 2048) // 128
                            def _pe(v, c=c, ps=ps):
                                return v.tensor_copy(xc[c][:, 4:516],
                                                     ps[:, 0:512])
                            ctx.vstep(("t",), _pe)
                            if tloc == 0:
                                def _phm(t, k=k, qb=qb):
                                    last = None
                                    for dc in range(8):
                                        last = t.matmul(
                                            P3["b1"][:, 0:3],
                                            wip[dc][:, k * 128:(k + 1) * 128],
                                            xnh[dc][:, qb * 3:qb * 3 + 3],
                                            start=(dc == 0), stop=(dc == 7))
                                    return last
                                ctx.tstep(("v",), _phm)
                                def _phe(v, c=c, qb=qb):
                                    return v.tensor_scalar(
                                        xc[c][:, 1:4], P3["b1"][:, 0:3],
                                        hmask[:, qb:qb + 1], None, ALU.mult)
                                ctx.vstep(("t",), _phe)
                            else:
                                def _phs(v, c=c):
                                    return v.tensor_copy(
                                        xc[c][:, 1:4],
                                        stash[:, c * 3:(c + 1) * 3])
                                ctx.vstep(("t",), _phs)
                        if bw == 288:
                            def _pdt(t):
                                last = None
                                for dc in range(8):
                                    last = t.matmul(
                                        P3["m0"][0:32, 0:512],
                                        wip[dc][:, 256:288],
                                        xn[dc][:, 0:512],
                                        start=(dc == 0), stop=(dc == 7))
                                return last
                            ctx.tstep(("v", "dma"), _pdt)
                            def _pde(v):
                                return v.tensor_copy(dtraw[:],
                                                     P3["m0"][0:32, 0:512])
                            ctx.vstep(("t",), _pde)

                    if KD3 == "p1ip" and mode == 1:
                        return
                    if tloc == 0:
                        def _stash(v):
                            last = None
                            for c in range(18):
                                last = v.tensor_copy(stash[:, c * 3:(c + 1) * 3],
                                                     xc[c][:, 513:516])
                            return last
                        ctx.vstep(("t",), _stash)

                    for c in range(18):
                        eng = "vector"
                        acc = accD
                        def _c0(e, c=c, acc=acc):
                            return e.tensor_scalar(acc[:, 0:512], xc[c][:, 1:513],
                                                   cw_sb[:, c * 4:c * 4 + 1],
                                                   None, ALU.mult)
                        ctx.step(eng, ("v", "t", "g", "s"), _c0, "v")
                        for k in range(1, 4):
                            def _ck(e, c=c, k=k, acc=acc):
                                return e.scalar_tensor_tensor(
                                    acc[:, 0:512], xc[c][:, 1 + k:513 + k],
                                    cw_sb[:, c * 4 + k:c * 4 + k + 1],
                                    acc[:, 0:512], ALU.mult, ALU.add)
                            ctx.step(eng, (), _ck, "v")
                        def _cs(s_, c=c, acc=acc):
                            return s_.activation(xc[c][:, 4:516], acc[:, 0:512],
                                                 AF.Silu, bias=cb_sb[:, c:c + 1])
                        ctx.sstep(("v", "g"), _cs)

                    if KD3 == "p1cv" and mode == 1:
                        return
                    def _d1(s_):
                        return s_.activation(dtw[:, 0:512], dtraw[:], AF.Abs,
                                             bias=dtb_sb[:])
                    ctx.sstep(("v",), _d1)
                    def _d2(s_):
                        return s_.activation(sq[0:32, 0:512], dtw[:, 0:512],
                                             AF.Exp, scale=-1.0)
                    ctx.sstep(("v",), _d2)
                    def _d3(s_):
                        return s_.activation(dtw[:, 0:512], sq[0:32, 0:512],
                                             AF.Ln, bias=1.0)
                    ctx.sstep((), _d3)
                    def _d4(v):
                        return v.tensor_scalar(dtraw[:], dtraw[:],
                                               dtb_sb[:], 0.0, ALU.add, ALU.max)
                    ctx.vstep(("s",), _d4)
                    def _d5(v):
                        return v.tensor_add(dt_sb[:], dtw[:, 0:512], dtraw[:])
                    ctx.vstep(("s",), _d5)
                    def _d6(v):
                        return v.tensor_scalar(dtraw[:], dt_sb[:], aneg_sb[:],
                                               None, ALU.mult)
                    ctx.vstep((), _d6)
                    for ci in range(2):
                        z = 2 * tloc + ci
                        def _scn(v, ci=ci):
                            return v.tensor_tensor_scan(
                                csf[:, ci * 256:(ci + 1) * 256], ones32[:],
                                dtraw[:, ci * 256:(ci + 1) * 256], 0.0,
                                ALU.mult, ALU.add)
                        ctx.vstep((), _scn)
                        def _ds(v, ci=ci, z=z, qb=qb):
                            return v.tensor_copy(
                                dssum[:, qb * 4 + z:qb * 4 + z + 1],
                                csf[:, ci * 256 + 255:ci * 256 + 256])
                        ctx.vstep((), _ds)

                    if mode == 2:
                        for zb in range(4):
                            def _lwz(sync, dma, zb=zb):
                                for dc in range(8):
                                    dma(out=wip[dc][:, 0:512],
                                        in_=ab_inproj[dc * 128:(dc + 1) * 128,
                                                      zb * 512:(zb + 1) * 512])
                            ctx.gdma(("t",), _lwz)
                            for k in range(4):
                                fc = zb * 4 + k
                                ps = P3["m0"] if k % 2 == 0 else P3["m1"]
                                def _zm(t, k=k, ps=ps):
                                    last = None
                                    for dc in range(8):
                                        last = t.matmul(
                                            ps[:, 0:512],
                                            wip[dc][:, k * 128:(k + 1) * 128],
                                            xn[dc][:, 0:512],
                                            start=(dc == 0), stop=(dc == 7))
                                    return last
                                ctx.tstep(("v", "s", "dma"), _zm)
                                def _ze(s_, fc=fc, ps=ps):
                                    return s_.activation(zs[fc][:], ps[:, 0:512],
                                                         AF.Silu)
                                ctx.sstep(("t",), _ze)

                    if KD3 == "p2z" and mode == 2:
                        return
                    emit_chunk(ti, 0, mode)
                    if KD3 == "p2c0" and mode == 2:
                        return
                    emit_chunk(ti, 1, mode)
                    if mode == 2:
                        if KD3 == "p2cc" and mode == 2:
                            return
                        emit_tile_post(ti)

                _sc_p3a = nc.enter_named_scope("p3_pass1", False)[0]
                if KD3 == "p1t0":
                    emit_tile(0, 1)
                    emit_store_out(ctx, X, out_fm)
                    return nc
                for ti in range(4):
                    emit_tile(ti, 1)
                if KD3 == "p1":
                    emit_store_out(ctx, X, out_fm)
                    return nc
                def _pzz(v):
                    v.memset(pzx[:, 0:1], 0.0)
                    return v.memset(pzx[:, 4:5], 0.0)
                ctx.vstep(("v",), _pzz)
                for qb in range(2):
                    for z2 in range(1, 4):
                        def _pzc(v, qb=qb, z2=z2):
                            return v.tensor_add(
                                pzx[:, qb * 4 + z2:qb * 4 + z2 + 1],
                                pzx[:, qb * 4 + z2 - 1:qb * 4 + z2],
                                dssum[:, qb * 4 + z2 - 1:qb * 4 + z2])
                        ctx.vstep((), _pzc)
                def _tot1(v):
                    v.tensor_add(dtw[:, 0:1], dssum[:, 0:1], dssum[:, 1:2])
                    v.tensor_add(dtw[:, 1:2], dssum[:, 2:3], dssum[:, 3:4])
                    v.tensor_add(dtw[:, 2:3], dssum[:, 4:5], dssum[:, 5:6])
                    return v.tensor_add(dtw[:, 3:4], dssum[:, 6:7], dssum[:, 7:8])
                ctx.vstep((), _tot1)
                def _tot2(v):
                    v.tensor_add(dt_sb[:, 0:1], dtw[:, 0:1], dtw[:, 1:2])
                    return v.tensor_add(dt_sb[:, 1:2], dtw[:, 2:3], dtw[:, 3:4])
                ctx.vstep((), _tot2)
                def _pze(s_):
                    return s_.activation(pzx[:], pzx[:], AF.Exp)
                ctx.sstep(("v",), _pze)
                def _dst(sync, dma):
                    dma(out=cc_d_in, in_=dt_sb[:, 0:2])
                ctx.dstep(("v",), _dst)

                nc.leave_named_scope("p3_pass1", _sc_p3a, False)
                _sc_p3c = nc.enter_named_scope("p3_sin", False)[0]
                def _cc3(g):
                    ctx.inc(g.collective_compute(
                        "AllGather", ALU.bypass, replica_groups=RG,
                        ins=[cc_s_in.opt()], outs=[cc_s_out.opt()]), "cc")
                    ctx.inc(g.collective_compute(
                        "AllGather", ALU.bypass, replica_groups=RG,
                        ins=[cc_d_in.opt()], outs=[cc_d_out.opt()]), "cc")
                    return None
                ctx.step("gpsimd", ("dma",), _cc3)

                dag = p3b.enter_context(nc.sbuf_tensor("dag", [32, 16], F32))
                ccm_sb = p3b.enter_context(nc.sbuf_tensor("ccm_sb", [8, 16], F32))
                ccb_sb = p3b.enter_context(nc.sbuf_tensor("ccb_sb", [8, 2], F32))
                def _ld3(sync, dma):
                    dma(out=dag[:].rearrange("h (s r) -> h s r", r=8),
                        in_=cc_d_out.rearrange("(r h) s -> h s r", r=8),
                        allow_slow_non_contiguous=True)
                    dma(out=ccm_sb[:], in_=cc_mats)
                    dma(out=ccb_sb[:], in_=cc_bias)
                ctx.dstep(("cc",), _ld3)

                def emit_sin(qb):
                    def _sz(v):
                        return v.memset(Sinb[:], 0.0)
                    ctx.vstep(("t",), _sz)
                    def _tg(t, qb=qb):
                        return t.transpose(P3["m1"][0:8, 0:32],
                                           dag[:, qb * 8:(qb + 1) * 8],
                                           idf[0:32, 0:32])
                    ctx.tstep(("v", "dma"), _tg)
                    def _tge(v):
                        return v.tensor_copy(dagT[:], P3["m1"][0:8, 0:32])
                    ctx.vstep(("t",), _tge)
                    def _em(t, qb=qb):
                        return t.matmul(P3["m1"][0:8, 0:32],
                                        ccm_sb[:, qb * 8:(qb + 1) * 8],
                                        dagT[:], start=True, stop=True)
                    ctx.tstep(("v",), _em)
                    def _ee(s_, qb=qb):
                        return s_.activation(wqr[:], P3["m1"][0:8, 0:32],
                                             AF.Exp, bias=ccb_sb[:, qb:qb + 1])
                    ctx.sstep(("t",), _ee)
                    for r in range(8):
                        def _lg(sync, dma, r=r, qb=qb):
                            dma(out=Lg[:],
                                in_=cc_s_out[r * 128:(r + 1) * 128,
                                             qb * 2048:(qb + 1) * 2048])
                        ctx.dstep(("v",), _lg)
                        def _wrow(sync, dma, r=r):
                            dma(out=arow[:], in_=wqr[r:r + 1, 0:32])
                        ctx.dstep(("s", "v"), _wrow)
                        def _wbc(t):
                            return t.matmul(P3["m1"][:, 0:32], onesf[0:1, 0:128],
                                            arow[:], start=True, stop=True)
                        ctx.tstep(("dma",), _wbc)
                        def _wbe(v):
                            return v.tensor_copy(pzbc[:], P3["m1"][:, 0:32])
                        ctx.vstep(("t",), _wbe)
                        for h in range(NHEADS):
                            def _acc(v, h=h):
                                return v.scalar_tensor_tensor(
                                    Sinb[:, h * 64:(h + 1) * 64],
                                    Lg[:, h * 64:(h + 1) * 64],
                                    pzbc[:, h:h + 1],
                                    Sinb[:, h * 64:(h + 1) * 64],
                                    ALU.mult, ALU.add)
                            ctx.vstep(("dma",), _acc)

                if KD3 == "sin":
                    emit_sin(0)
                    emit_sin(1)
                    emit_store_out(ctx, X, out_fm)
                    return nc
                nc.leave_named_scope("p3_sin", _sc_p3c, False)
                _sc_p3d = nc.enter_named_scope("p3_pass2", False)[0]
                emit_sin(0)
                emit_tile(0, 2)
                if KD3 == "p2t0":
                    emit_store_out(ctx, X, out_fm)
                    return nc
                emit_tile(1, 2)
                emit_sin(1)
                emit_tile(2, 2)
                emit_tile(3, 2)


                nc.leave_named_scope("p3_pass2", _sc_p3d, False)
                ctx.barrier()
                p3b.close()

                def _cc4(g):
                    ctx.inc(g.collective_compute(
                        "AllGather", ALU.bypass, replica_groups=RG,
                        ins=[cc_p_in.opt()], outs=[cc_p_out.opt()]), "cc")
                    ctx.inc(g.collective_compute(
                        "AllGather", ALU.bypass, replica_groups=RG,
                        ins=[cc_q_in.opt()], outs=[cc_q_out.opt()]), "cc")
                    return None
                ctx.step("gpsimd", ("dma",), _cc4)

                _sc_p5 = nc.enter_named_scope("p5_pool", False)[0]
                S_raw = p3.enter_context(nc.sbuf_tensor("S_raw", [128, 128], F32))
                S_work = p3.enter_context(nc.sbuf_tensor("S_work", [128, 128], F32))
                smask = p3.enter_context(nc.sbuf_tensor("smask", [128, 128], F32))
                refio = p3.enter_context(nc.sbuf_tensor("refio", [128, 128], F32))
                gio = p3.enter_context(nc.sbuf_tensor("gio", [128, 128], F32))
                neg1t = p3.enter_context(nc.sbuf_tensor("neg1t", [128, 128], F32))
                prisel = p3.enter_context(nc.sbuf_tensor("prisel", [128, 128], F32))
                fsel = p3.enter_context(nc.sbuf_tensor("fsel", [128, 128], F32))
                sw1 = p3.enter_context(nc.sbuf_tensor("sw1", [128, 8], F32))
                swr = p3.enter_context(nc.sbuf_tensor("swr", [1, 8], F32))
                mbc = p3.enter_context(nc.sbuf_tensor("mbc", [128, 4], F32))
                offsrow = p3.enter_context(nc.sbuf_tensor("offsrow", [1, 64], F32))
                offs_u = p3.enter_context(nc.sbuf_tensor("offs_u", [POOL_SIZE, 1], U32))
                pool50 = p3.enter_context(nc.sbuf_tensor("pool50", [POOL_SIZE, 64], F32))
                poolT = p3.enter_context(nc.sbuf_tensor("poolT", [64, POOL_SIZE], BF16))
                kw_sb = p3.enter_context(nc.sbuf_tensor("kw_sb", [64, 64], BF16))
                vw_sb = p3.enter_context(nc.sbuf_tensor("vw_sb", [64, 1024], BF16))
                kkT = p3.enter_context(nc.sbuf_tensor("kkT", [64, POOL_SIZE], BF16))
                vvb = p3.enter_context(nc.sbuf_tensor("vvb", [POOL_SIZE, 1024], BF16))
                msk50 = p3.enter_context(nc.sbuf_tensor("msk50", [128, POOL_SIZE], F32))
                condc = p3.enter_context(nc.sbuf_tensor("condc", [128, 1], F32))
                sio = p3.enter_context(nc.sbuf_tensor("sio", [1, 64], F32))
                rcrow = p3.enter_context(nc.sbuf_tensor("rcrow", [1, 512], F32))
                rcbc = p3.enter_context(nc.sbuf_tensor("rcbc", [128, 512], F32))
                plT = p3.enter_context(nc.sbuf_tensor("plT", [POOL_SIZE, 512], BF16))
                pl = p3.enter_context(nc.sbuf_tensor("pl", [128, POOL_SIZE], BF16))
                rs50 = p3.enter_context(nc.sbuf_tensor("rs50", [128, 4], F32))
                gb = p3.enter_context(nc.sbuf_tensor("gb", [128, 512], BF16))
                tg = p3.enter_context(nc.sbuf_tensor("tg", [128, 512], BF16))
                rb = [p3.enter_context(nc.sbuf_tensor(f"rb{c}", [128, 512], BF16))
                      for c in range(8)]

                def _p5ld(sync, dma):
                    dma(out=refio[:], in_=ref_iota)
                    dma(out=gio[:], in_=g_iota)
                    dma(out=sio[:], in_=slot_iota)
                    for r in range(8):
                        dma(out=S_raw[16 * r:16 * r + 16, :],
                            in_=cc_p_out[65 * r + 64:65 * r + 65, :].rearrange(
                                "r (p f) -> (r p) f", p=16))
                ctx.dstep(("cc",), _p5ld)
                def _p5g(g, dma):
                    dma(out=kw_sb[:], in_=ab_k_w)
                    dma(out=vw_sb[:], in_=ab_v_w)
                ctx.gdma((), _p5g)
                def _p5c(v):
                    return v.memset(neg1t[:], -1.0)
                ctx.vstep((), _p5c)
                def _p5r(v):
                    return v.tensor_scalar(refio[:], refio[:], -8388608.0, None,
                                           ALU.add)
                ctx.vstep(("dma",), _p5r)
                def _p5g2(v):
                    return v.tensor_scalar(gio[:], gio[:], -8388608.0, None,
                                           ALU.add)
                ctx.vstep(("dma",), _p5g2)
                def _p5m(v):
                    return v.tensor_single_scalar(smask[:], S_raw[:], 0.5,
                                                  ALU.is_gt)
                ctx.vstep(("dma",), _p5m)
                def _p5w(v):
                    return v.scalar_tensor_tensor(S_work[:], S_raw[:], 1.0,
                                                  smask[:], ALU.add, ALU.mult)
                ctx.vstep((), _p5w)
                def _p5w2(v):
                    return v.tensor_scalar(S_work[:], S_work[:], -1.0, None,
                                           ALU.add)
                ctx.vstep((), _p5w2)
                def _cnt1(t):
                    t.matmul(P3["m1"][0:1, 0:128], onecol[:], smask[:],
                             start=True, stop=True)
                    return t.matmul(P3["m1"][0:1, 128:256], onecol[:], S_raw[:],
                                    start=True, stop=True)
                ctx.tstep(("v", "dma"), _cnt1)
                def _cnt2(v):
                    v.reduce_sum(swr[0:1, 0:1], P3["m1"][0:1, 0:128], axis=AX.X)
                    return v.reduce_sum(swr[0:1, 1:2], P3["m1"][0:1, 128:256],
                                        axis=AX.X)
                ctx.vstep(("t",), _cnt2)
                def _cnt3(v):
                    v.tensor_scalar(swr[0:1, 2:3], swr[0:1, 1:2], 1.0 / 16384.0,
                                    0.3, ALU.mult, ALU.is_gt)
                    return v.tensor_single_scalar(swr[0:1, 3:4], swr[0:1, 0:1],
                                                  0.5, ALU.is_gt)
                ctx.vstep((), _cnt3)
                def _cnt4(v):
                    return v.tensor_mul(swr[0:1, 4:5], swr[0:1, 2:3],
                                        swr[0:1, 3:4])
                ctx.vstep((), _cnt4)
                def _cnt5(t):
                    return t.matmul(P3["m1"][:, 0:1], onesf[0:1, 0:128],
                                    swr[0:1, 4:5], start=True, stop=True)
                ctx.tstep(("v",), _cnt5)
                def _cnt6(v):
                    return v.tensor_copy(condc[:], P3["m1"][:, 0:1])
                ctx.vstep(("t",), _cnt6)
                def _sm1(v):
                    return v.tensor_scalar(swr[0:1, 5:6], swr[0:1, 0:1], 50.0,
                                           None, ALU.min)
                ctx.vstep((), _sm1)
                def _sm2(v):
                    return v.tensor_scalar(offsrow[0:1, 50:114][0:1, 0:14]
                                           if False else swr[0:1, 6:7],
                                           swr[0:1, 5:6], 1.0, None, ALU.mult)
                ctx.vstep((), _sm2)
                def _sm3(v):
                    return v.tensor_scalar(msk50[0:1, 0:POOL_SIZE],
                                           sio[0:1, 0:POOL_SIZE],
                                           swr[0:1, 5:6], None, ALU.is_lt)
                ctx.vstep((), _sm3)
                def _sm4(v):
                    return v.tensor_scalar(msk50[0:1, 0:POOL_SIZE],
                                           msk50[0:1, 0:POOL_SIZE],
                                           4.0e9, -4.0e9, ALU.mult, ALU.add)
                ctx.vstep((), _sm4)
                def _sm5(t):
                    return t.matmul(P3["m1"][:, 0:POOL_SIZE], onesf[0:1, 0:128],
                                    msk50[0:1, 0:POOL_SIZE],
                                    start=True, stop=True)
                ctx.tstep(("v",), _sm5)
                def _sm6(v):
                    return v.tensor_copy(msk50[:], P3["m1"][:, 0:POOL_SIZE])
                ctx.vstep(("t",), _sm6)

                for it in range(POOL_SIZE):
                    def _i1(v):
                        return v.reduce_max(sw1[:, 0:1], S_work[:], axis=AX.X)
                    ctx.vstep(("t",), _i1)
                    def _i2(t):
                        return t.transpose(P3["m1"][0:1, 0:128], sw1[:, 0:1],
                                           idf[:])
                    ctx.tstep(("v",), _i2)
                    def _i3(v):
                        return v.reduce_max(swr[0:1, 7:8], P3["m1"][0:1, 0:128],
                                            axis=AX.X)
                    ctx.vstep(("t",), _i3)
                    def _i4(t):
                        return t.matmul(P3["m1"][:, 0:1], onesf[0:1, 0:128],
                                        swr[0:1, 7:8], start=True, stop=True)
                    ctx.tstep(("v",), _i4)
                    def _i5(v):
                        return v.tensor_copy(mbc[:, 0:1], P3["m1"][:, 0:1])
                    ctx.vstep(("t",), _i5)
                    def _i6(v):
                        return v.tensor_scalar(fsel[:], S_work[:], mbc[:, 0:1],
                                               None, ALU.is_ge)
                    ctx.vstep((), _i6)
                    def _i7(v):
                        return v.tensor_mul(prisel[:], fsel[:], refio[:])
                    ctx.vstep((), _i7)
                    def _i7b(v):
                        return v.tensor_scalar(prisel[:], prisel[:], 8388608.0,
                                               None, ALU.add)
                    ctx.vstep((), _i7b)
                    def _i8(v):
                        return v.tensor_reduce(sw1[:, 1:2], prisel[:], AX.X,
                                               ALU.min)
                    ctx.vstep((), _i8)
                    def _i9(t):
                        return t.transpose(P3["m1"][0:1, 0:128], sw1[:, 1:2],
                                           idf[:])
                    ctx.tstep(("v",), _i9)
                    def _i10(v):
                        return v.tensor_reduce(swr[0:1, 7:8],
                                               P3["m1"][0:1, 0:128], AX.X,
                                               ALU.min)
                    ctx.vstep(("t",), _i10)
                    def _i11(t):
                        return t.matmul(P3["m1"][:, 0:1], onesf[0:1, 0:128],
                                        swr[0:1, 7:8], start=True, stop=True)
                    ctx.tstep(("v",), _i11)
                    def _i12(v):
                        return v.tensor_copy(mbc[:, 1:2], P3["m1"][:, 0:1])
                    ctx.vstep(("t",), _i12)
                    def _i13(v):
                        return v.tensor_scalar(fsel[:], prisel[:], mbc[:, 1:2],
                                               None, ALU.is_equal)
                    ctx.vstep((), _i13)
                    def _i14(v):
                        return v.tensor_mul(prisel[:], fsel[:], gio[:])
                    ctx.vstep((), _i14)
                    def _i14b(v):
                        return v.tensor_scalar(prisel[:], prisel[:], 8388608.0,
                                               None, ALU.add)
                    ctx.vstep((), _i14b)
                    def _i15(v):
                        return v.tensor_reduce(sw1[:, 2:3], prisel[:], AX.X,
                                               ALU.min)
                    ctx.vstep((), _i15)
                    def _i16(t):
                        return t.transpose(P3["m1"][0:1, 0:128], sw1[:, 2:3],
                                           idf[:])
                    ctx.tstep(("v",), _i16)
                    def _i17(v, it=it):
                        return v.tensor_reduce(offsrow[0:1, it:it + 1],
                                               P3["m1"][0:1, 0:128], AX.X,
                                               ALU.min)
                    ctx.vstep(("t",), _i17)
                    def _i18(v):
                        return v.scalar_tensor_tensor(S_work[:], fsel[:],
                                                      -1.0e4, S_work[:],
                                                      ALU.mult, ALU.add)
                    ctx.vstep((), _i18)

                def _o1(t):
                    return t.transpose(P3["m1"][0:POOL_SIZE, 0:1],
                                       offsrow[0:1, 0:POOL_SIZE],
                                       idf[0:1, 0:1])
                ctx.tstep(("v",), _o1)
                def _o2(v):
                    return v.tensor_copy(offs_u[:], P3["m1"][0:POOL_SIZE, 0:1])
                ctx.vstep(("t",), _o2)
                def _o3(g):
                    i0 = g.indirect_dma_start(
                        out=pool50[:], out_offset=None, in_=cc_q_out,
                        in_offset=bass.IndirectOffsetOnAxis(ap=offs_u[:], axis=0))
                    i0.then_inc(ctx.sem["gd"], 16)
                    ctx.cnt["gd"] += 16
                    return None
                ctx.step("gpsimd", ("v", "cc"), _o3)
                def _o4(t):
                    return t.transpose(P3["m1"][0:64, 0:POOL_SIZE], pool50[:],
                                       idf[0:POOL_SIZE, 0:POOL_SIZE])
                ctx.tstep(("v", "dma"), _o4)
                def _o5(v):
                    return v.tensor_copy(poolT[:], P3["m1"][0:64, 0:POOL_SIZE])
                ctx.vstep(("t",), _o5)
                def _o6(t):
                    return t.matmul(P3["m1"][0:64, 0:POOL_SIZE], kw_sb[:],
                                    poolT[:], start=True, stop=True)
                ctx.tstep(("v",), _o6)
                def _o7(v):
                    return v.tensor_copy(kkT[:], P3["m1"][0:64, 0:POOL_SIZE])
                ctx.vstep(("t",), _o7)
                for half in range(2):
                    def _o8(t, half=half):
                        return t.matmul(P3["m0"][0:POOL_SIZE, 0:512], poolT[:],
                                        vw_sb[:, half * 512:(half + 1) * 512],
                                        start=True, stop=True)
                    ctx.tstep(("v",), _o8)
                    def _o9(v, half=half):
                        return v.tensor_copy(vvb[:, half * 512:(half + 1) * 512],
                                             P3["m0"][0:POOL_SIZE, 0:512])
                    ctx.vstep(("t",), _o9)

                if KD3 == "pool":
                    emit_store_out(ctx, X, out_fm)
                    return nc

                nc.leave_named_scope("p5_pool", _sc_p5, False)
                _sc_p6 = nc.enter_named_scope("p6_retr", False)[0]
                for ti in range(4):
                    t0 = ti * 512
                    def _yl(sync, dma, t0=t0):
                        for c in range(8):
                            dma(out=y_tl[c][:],
                                in_=y_d[c * 128:(c + 1) * 128, t0:t0 + 512])
                    ctx.dstep(("t", "v"), _yl)
                    LNP = {"row": P3["row"], "b0": P3["b0"], "b1": P3["b1"]}
                    emit_layernorm(ctx, LNP, X, xn, t0, t0 + 512)
                    for tc in range(4):
                        def _r1(t, t0=t0, tc=tc):
                            return t.matmul(
                                P3["m1"][:, 0:POOL_SIZE],
                                qret[0:64, t0 + tc * 128:t0 + (tc + 1) * 128],
                                kkT[:], start=True, stop=True)
                        ctx.tstep(("v", "s"), _r1)
                        def _r2(v):
                            return v.tensor_add(P3["m1"][:, 0:POOL_SIZE],
                                                P3["m1"][:, 0:POOL_SIZE],
                                                msk50[:])
                        ctx.vstep(("t",), _r2)
                        def _r3(v):
                            return v.tensor_reduce(sw1[:, 3:4],
                                                   P3["m1"][:, 0:POOL_SIZE],
                                                   AX.X, ALU.max, negate=True)
                        ctx.vstep((), _r3)
                        def _r4(v):
                            return v.tensor_scalar(sw1[:, 4:5], sw1[:, 3:4],
                                                   0.25, None, ALU.mult)
                        ctx.vstep((), _r4)
                        def _r5(s_, tc=tc):
                            return s_.activation(pl[:], P3["m1"][:, 0:POOL_SIZE],
                                                 AF.Exp, bias=sw1[:, 4:5],
                                                 scale=0.25,
                                                 accum_out=rs50[:, tc:tc + 1])
                        ctx.sstep(("v",), _r5)
                        def _r6(t, tc=tc):
                            return t.transpose(PSB3[:, 0:128]
                                               if False else
                                               PSB3[0:POOL_SIZE, 0:128],
                                               pl[:], idb[:])
                        ctx.tstep(("s", "v"), _r6)
                        def _r7(v, tc=tc):
                            return v.tensor_copy(plT[:, tc * 128:(tc + 1) * 128],
                                                 PSB3[0:POOL_SIZE, 0:128])
                        ctx.vstep(("t",), _r7)
                        def _r8(v, tc=tc):
                            return v.reciprocal(sw1[:, 5:6], rs50[:, tc:tc + 1])
                        ctx.vstep(("s",), _r8)
                        def _r9(t, tc=tc):
                            return t.transpose(P3["m1"][0:1, 0:128],
                                               sw1[:, 5:6], idf[:])
                        ctx.tstep(("v",), _r9)
                        def _r10(v, tc=tc):
                            return v.tensor_copy(rcrow[0:1, tc * 128:(tc + 1) * 128],
                                                 P3["m1"][0:1, 0:128])
                        ctx.vstep(("t",), _r10)
                    def _rb1(t):
                        return t.matmul(P3["b0"][:, 0:512], onesf[0:1, 0:128],
                                        rcrow[0:1, 0:512], start=True, stop=True)
                    ctx.tstep(("v",), _rb1)
                    def _rb2(v):
                        return v.tensor_copy(rcbc[:], P3["b0"][:, 0:512])
                    ctx.vstep(("t",), _rb2)
                    for oc in range(8):
                        def _rv1(t, oc=oc):
                            return t.matmul(P3["m0"][:, 0:512],
                                            vvb[:, oc * 128:(oc + 1) * 128],
                                            plT[:], start=True, stop=True)
                        ctx.tstep(("v",), _rv1)
                        def _rv2(v, oc=oc):
                            return v.tensor_mul(rb[oc][:], P3["m0"][:, 0:512],
                                                rcbc[:])
                        ctx.vstep(("t",), _rv2)
                    for ocb in range(2):
                        def _gl(sync, dma, ocb=ocb):
                            for kc in range(16):
                                dma(out=zs[kc][:],
                                    in_=ab_gate_w[kc * 128:(kc + 1) * 128,
                                                  ocb * 512:(ocb + 1) * 512])
                        ctx.gdma(("t",), _gl)
                        for ocq in range(4):
                            oc = ocb * 4 + ocq
                            def _gm(t, ocq=ocq):
                                last = None
                                for dc in range(8):
                                    last = t.matmul(
                                        P3["m0"][:, 0:512],
                                        zs[dc][:, ocq * 128:(ocq + 1) * 128],
                                        y_tl[dc][:],
                                        start=(dc == 0), stop=False)
                                for dc in range(8):
                                    last = t.matmul(
                                        P3["m0"][:, 0:512],
                                        zs[8 + dc][:, ocq * 128:(ocq + 1) * 128],
                                        rb[dc][:],
                                        start=False, stop=(dc == 7))
                                return last
                            ctx.tstep(("v", "s", "dma"), _gm)
                            def _ge(s_):
                                return s_.activation(gb[:], P3["m0"][:, 0:512],
                                                     AF.Sigmoid)
                            ctx.sstep(("t",), _ge)
                            def _f1(v, oc=oc):
                                return v.tensor_mul(tg[:], gb[:], rb[oc][:])
                            ctx.vstep(("s",), _f1)
                            def _f2(v, oc=oc):
                                return v.scalar_tensor_tensor(
                                    tg[:], tg[:], condc[:, 0:1], xn[oc][:],
                                    ALU.mult, ALU.add)
                            ctx.vstep((), _f2)
                            def _f3(v, oc=oc, t0=t0):
                                return v.tensor_add(X[oc][:, t0:t0 + 512],
                                                    X[oc][:, t0:t0 + 512],
                                                    tg[:])
                            ctx.vstep((), _f3)
                            def _f4(v, oc=oc, t0=t0):
                                return v.tensor_add(X[oc][:, t0:t0 + 512],
                                                    X[oc][:, t0:t0 + 512],
                                                    y_tl[oc][:])
                            ctx.vstep((), _f4)

                nc.leave_named_scope("p6_retr", _sc_p6, False)
                if KD3 == "y1":
                    def _ydl(sync, dma):
                        for c in range(8):
                            dma(out=X[c][:],
                                in_=y_d[c * 128:(c + 1) * 128, :])
                    def _ydlg(g, dma):
                        for c in range(8):
                            dma(out=X[c][:], in_=y_d[c * 128:(c + 1) * 128, :])
                    ctx.gdma(ALLC, _ydlg)
                    def _sto(sync, dma):
                        for c in range(8):
                            dma(out=out_fm[c * 128:(c + 1) * 128, :], in_=X[c][:])
                    ctx.dstep(("dma",), _sto)
                    ctx.step("sync", ("dma",), lambda sync: None)
                    return nc

        if stop_after == "attn_b":
            emit_store_out(ctx, X, out_fm)
            return nc

        ctx.barrier()
        with contextlib.ExitStack() as p4:
            p4.enter_context(nc.named_scope("ph4_ffn2"))
            lnf = [p4.enter_context(nc.sbuf_tensor(f"l4f{c}", [128, 512], BF16))
                   for c in range(8)]
            a1 = [p4.enter_context(nc.sbuf_tensor(f"b1_{c}", [128, 512], BF16))
                  for c in range(32)]
            a2 = [p4.enter_context(nc.sbuf_tensor(f"b2_{c}", [128, 512], BF16))
                  for c in range(32)]
            wff = [[p4.enter_context(
                nc.sbuf_tensor(f"w4f{p}_{c}", [128, 512], BF16))
                for c in range(8)] for p in range(2)]
            w1b = [[p4.enter_context(
                nc.sbuf_tensor(f"w4b{p}_{c}", [128, 256], BF16))
                for c in range(32)] for p in range(2)]
            scl = p4.enter_context(nc.sbuf_tensor("scl4", [128, 3], F32))
            srow = p4.enter_context(nc.sbuf_tensor("srow4", [1, 4], F32))
            with contextlib.ExitStack() as pq:
                P4 = {
                    "row": pq.enter_context(nc.psum_tensor("f_row", [1, 1024], F32)),
                    "b0": pq.enter_context(nc.psum_tensor("f_b0", [128, 512], F32)),
                    "b1": pq.enter_context(nc.psum_tensor("f_b1", [128, 512], F32)),
                    "m0": pq.enter_context(nc.psum_tensor("f_m0", [128, 512], F32)),
                    "m1": pq.enter_context(nc.psum_tensor("f_m1", [128, 512], F32)),
                    "u": pq.enter_context(nc.psum_tensor("f_u", [128, 512], F32)),
                }
                def _lds(sync, dma):
                    dma(out=srow[0:1, 0:1], in_=s0in)
                    dma(out=srow[0:1, 1:2], in_=s1in)
                    dma(out=srow[0:1, 2:3], in_=s2in)
                ctx.dstep((), _lds)
                def _s2(v):
                    return v.tensor_scalar(srow[0:1, 0:3], srow[0:1, 0:3], 2.0,
                                           None, ALU.mult)
                ctx.vstep(("dma",), _s2)
                def _sbc(t):
                    return t.matmul(P4["u"][:, 0:3], onesf[0:1, 0:128],
                                    srow[0:1, 0:3], start=True, stop=True)
                ctx.tstep(("v",), _sbc)
                def _scp(v):
                    return v.tensor_copy(scl[:], P4["u"][:, 0:3])
                ctx.vstep(("t",), _scp)
                emit_ffn(ctx, nc, X, out_fm, (w0h_d, w1_d, w2_d), scl, P4,
                         {"lnf": lnf, "a1": a1, "a2": a2, "wff": wff, "w1b": w1b})

        emit_store_out(ctx, X, out_fm)
        return nc
    return nc


def make_host_constants():
    cores = []
    dm = np.zeros((128, 4 * 512), np.float32)
    for mi in range(4):
        delta = mi * 128
        i = np.arange(128)[:, None]
        j = np.arange(512)[None, :]
        dm[:, mi * 512:(mi + 1) * 512] = np.where(j < i + delta, NEG, 0.0)
    identf = np.eye(128, dtype=np.float32)
    identb = np.eye(128, dtype=ml_dtypes.bfloat16)
    EPSf = float(np.finfo(np.float32).eps)
    idx = np.arange(8192, dtype=np.float32)
    sc = 1.0 / (10000.0 ** (np.arange(0, DH, 2, dtype=np.float32) / DH) + EPSf)
    ang = idx[:, None] * sc[None, :]
    pos = np.stack([np.sin(ang), np.cos(ang)], axis=-1).reshape(8192, DH)
    pos = pos.astype(np.float32)

    trineg = np.zeros((128, 512), np.float32)
    for scn in range(2):
        s = np.arange(128)[:, None] + scn * 128
        l = np.arange(256)[None, :]
        trineg[:, scn * 256:(scn + 1) * 256] = np.where(s <= l, 0.0, NEG)

    for j in range(NCORES):
        c = {}
        ho = np.zeros((128, 16), np.uint32)
        p_ = np.arange(128, dtype=np.uint32)
        for ch in range(8):
            if j > 0:
                ho[:, ch] = 2048 * (j - 1) + ch * 128 + p_
            if j < 7:
                ho[:, 8 + ch] = 2048 * (j + 1) + 1024 + ch * 128 + p_
        c["halo_offs"] = ho
        hm = np.zeros((128, 2), np.float32)
        hm[:, 0] = 1.0 if j > 0 else 0.0
        hm[:, 1] = 1.0 if j < 7 else 0.0
        c["halo_mask"] = hm
        cm = np.zeros((8, 16), np.float32)
        cb = np.full((8, 2), NEG, np.float32)
        for r in range(8):
            for k in range(8):
                if r < k < j:
                    cm[k, r] = 1.0
                if j < k < r:
                    cm[k, 8 + r] = 1.0
            if r < j:
                cb[r, 0] = 0.0
            if r > j:
                cb[r, 1] = 0.0
        c["cc_mats"] = cm
        c["cc_bias"] = cb
        pp_ = np.arange(128)[:, None]
        ff_ = np.arange(128)[None, :]
        rr = pp_ // 16
        tloc = (pp_ % 16) * 128 + ff_
        g = rr * 2048 + tloc
        sA = rr * 1024 + tloc
        sB = 1024 * (7 - rr) + (tloc - 1024)
        ref = np.where(tloc < 1024, sA, 8192 + sB)
        c["ref_iota"] = (ref + 0.0).astype(np.float32) + 0 * ff_
        c["g_iota"] = (g + 0.0).astype(np.float32) + 0 * ff_
        si = np.zeros((1, 64), np.float32)
        si[0, :] = np.arange(64)
        c["slot_iota"] = si
        posA = pos[1024 * j:1024 * (j + 1)]
        posB = pos[1024 * (7 - j):1024 * (8 - j)]
        p2 = np.concatenate([posA.T, posB.T], axis=1)
        c["pos2"] = np.concatenate([p2, p2], axis=0).astype(np.float32)
        d = np.zeros((128, 256), np.float32)
        for qb in range(2):
            posq = j if qb == 0 else 7 - j
            for kp in range(8):
                live = kp < posq
                if not live:
                    d[:, (qb * 8 + kp) * 16:(qb * 8 + kp + 1) * 16] = NEG
        c["dead"] = d
        c["diagmasks"] = dm
        c["identf"] = identf
        c["identb"] = identb
        c["trineg"] = trineg
        cores.append(c)
    return cores


def build_in_maps(inputs):
    x = np.ascontiguousarray(inputs["L0_attn_a_x"])
    consts = make_host_constants()
    in_maps = []
    for j in range(NCORES):
        xa = x[0, 1024 * j:1024 * (j + 1), :]
        xb = x[1, 1024 * (7 - j):1024 * (8 - j), :]
        x_fm = np.ascontiguousarray(
            np.concatenate([xa, xb], axis=0).T)
        m = {
            "x_fm": x_fm,
            "wq": inputs["L0_attn_a_Wq"],
            "wk": inputs["L0_attn_a_Wk"],
            "wv": inputs["L0_attn_a_Wv"],
            "wo": inputs["L0_attn_a_Wo"],
            "k0u8": np.ascontiguousarray(inputs["L0_ffn_k0"]).view(np.uint8),
            "k1u8": np.ascontiguousarray(inputs["L0_ffn_k1"]).view(np.uint8),
            "k2u8": np.ascontiguousarray(inputs["L0_ffn_k2"]).view(np.uint8),
            "s0in": np.asarray(inputs["L0_ffn_s0"], np.float32).reshape(1, 1),
            "s1in": np.asarray(inputs["L0_ffn_s1"], np.float32).reshape(1, 1),
            "s2in": np.asarray(inputs["L0_ffn_s2"], np.float32).reshape(1, 1),
            "ab_norm_w": np.ascontiguousarray(
                np.asarray(inputs["L0_attn_b_norm_w"], np.float32)
                .reshape(8, 128).T),
            "ab_inproj": inputs["L0_attn_b_in_proj_w"],
            "ab_conv_w": np.ascontiguousarray(
                np.asarray(inputs["L0_attn_b_conv_w"], np.float32)
                .reshape(18, 128, 4).transpose(1, 0, 2).reshape(128, 72)),
            "ab_conv_b": np.ascontiguousarray(
                np.asarray(inputs["L0_attn_b_conv_b"], np.float32)
                .reshape(18, 128).T),
            "ab_dt_bias": np.asarray(inputs["L0_attn_b_dt_bias"], np.float32).reshape(NHEADS, 1),
            "ab_A_log": np.asarray(inputs["L0_attn_b_A_log"], np.float32).reshape(NHEADS, 1),
            "ab_D": np.asarray(inputs["L0_attn_b_D"], np.float32).reshape(NHEADS, 1),
            "ab_ssm_norm_w": np.ascontiguousarray(
                np.asarray(inputs["L0_attn_b_ssm_norm_w"], np.float32)
                .reshape(16, 128).T),
            "ab_outproj": inputs["L0_attn_b_out_proj_w"],
            "ab_scorer_w1": inputs["L0_attn_b_scorer_w1"],
            "ab_scorer_w2": np.ascontiguousarray(
                np.asarray(inputs["L0_attn_b_scorer_w2"], np.float32)
                .reshape(2, 128).T),
            "ab_summ_w": inputs["L0_attn_b_summ_w"],
            "ab_q_w": inputs["L0_attn_b_q_w"],
            "ab_k_w": inputs["L0_attn_b_k_w"],
            "ab_v_w": inputs["L0_attn_b_v_w"],
            "ab_gate_w": inputs["L0_attn_b_gate_w"],
        }
        m.update(consts[j])
        for name, shape, dt in (
            ("esel", (NHEADS, NHEADS * 128), np.float32),
            ("psel", (NHEADS, 8 * 128), np.float32),
            ("halo_offs", (128, 16), np.uint32),
            ("halo_mask", (128, 2), np.float32),
            ("cc_mats", (8, 16), np.float32),
            ("cc_bias", (8, 2), np.float32),

        ):
            if name not in m:
                m[name] = np.zeros(shape, dt)
        in_maps.append({k: v for k, v in m.items() if k in _EIN_NAMES})
    return in_maps


def kernel(**inputs):
    nc = build_program(stop_after=_STOP_AFTER[0])
    in_maps = build_in_maps(inputs)
    res = run_bass_kernel_spmd(nc, in_maps, core_ids=list(range(NCORES)))
    out = np.zeros((2, 8192, D), np.float32)
    for j in range(NCORES):
        o = res.results[j]["out_fm"]
        out[0, 1024 * j:1024 * (j + 1), :] = o[:, 0:1024].T
        out[1, 1024 * (7 - j):1024 * (8 - j), :] = o[:, 1024:2048].T
    return out


_STOP_AFTER = [None]
_EIN_NAMES = []
